# version 13
# baseline (speedup 1.0000x reference)
import numpy as np
import concourse.bass as bass
import concourse.mybir as mybir
from concourse.bass_utils import run_bass_kernel_spmd

F32 = mybir.dt.float32
BF16 = mybir.dt.bfloat16
AF = mybir.ActivationFunctionType
OP = mybir.AluOpType
AXX = mybir.AxisListType.X

NC = 8
B = 2
TL = 1024
C = 2048
DP = 4096
H = 16
HDK = 64
HDV = 128
NT = 16
NCH = 16
EPS = 1e-5
NSLAB = 4
SPOS = 256
DEBUG = False

ENGS = ["sync", "tensor", "vector", "scalar", "gpsimd"]


class Sem:
    def __init__(self, handle):
        self.h = handle
        self.n = 0

class Buf:
    def __init__(self, nslots):
        self.waits = [[] for _ in range(nslots)]
        self.k = nslots
    def acquire(self, em, eng, idx):
        idx %= self.k
        for (s, v) in self.waits[idx]:
            em.wait(eng, s, v)
        self.waits[idx] = []
        return idx
    def used_by(self, idx, sem, val):
        self.waits[idx % self.k].append((sem, val))

class Em:
    def __init__(self, name, eng):
        self.name = name
        self.e = eng
    def do(self, name, fn, inc=None):
        if self.name == name:
            inst = fn(self.e)
            if inc is not None:
                for s, k in inc:
                    inst.then_inc(s.h, k)
        if inc is not None:
            for s, k in inc:
                s.n += k
            return inc[0][0].n
        return None
    def wait(self, name, sem, val=None):
        v = sem.n if val is None else val
        if self.name == name:
            self.e.wait_ge(sem.h, v)
        return v
    def dma(self, name, out, in_, sem):
        return self.do(name, lambda e: e.dma_start(out=out, in_=in_), inc=[(sem, 16)])


def build_nc():
    nc = bass.Bass("TRN2", target_bir_lowering=False, detect_race_conditions=False)

    xin = nc.dram_tensor("x", [B, TL, C], F32, kind="ExternalInput").ap()
    wt = nc.dram_tensor("wt", [C, DP], F32, kind="ExternalInput").ap()
    b2 = nc.dram_tensor("b2", [128, 32], F32, kind="ExternalInput").ap()
    wq = nc.dram_tensor("wq", [C, 1024], F32, kind="ExternalInput").ap()
    wk = nc.dram_tensor("wk", [C, 1024], F32, kind="ExternalInput").ap()
    wv = nc.dram_tensor("wv", [C, 2048], F32, kind="ExternalInput").ap()
    wg = nc.dram_tensor("wg", [C, 2048], F32, kind="ExternalInput").ap()
    wo = nc.dram_tensor("wo", [C, 2048], F32, kind="ExternalInput").ap()
    rbd = nc.dram_tensor("rbd", [C, 32], F32, kind="ExternalInput").ap()
    cw0 = nc.dram_tensor("cw0", [128, C], F32, kind="ExternalInput").ap()
    cw1 = nc.dram_tensor("cw1", [128, C], F32, kind="ExternalInput").ap()
    cosq = nc.dram_tensor("cosq", [128, 1024], F32, kind="ExternalInput").ap()
    sins = nc.dram_tensor("sins", [128, 1024], F32, kind="ExternalInput").ap()
    idin = nc.dram_tensor("idin", [128, 128], F32, kind="ExternalInput").ap()
    maskin = nc.dram_tensor("maskin", [128, 8], F32, kind="ExternalInput").ap()
    wvt = nc.dram_tensor("wvt", [2048, C], F32, kind="ExternalInput").ap()
    epsin = nc.dram_tensor("epsin", [128, 1], F32, kind="ExternalInput").ap()
    yout = nc.dram_tensor("y", [B, TL, C], F32, kind="ExternalOutput").ap()

    gsp = nc.dram_tensor("gsp", [NCH, 128, 2 * TL], BF16).ap()
    vsp = nc.dram_tensor("vsp", [NCH, 128, 2 * TL], BF16).ap()
    wsc = nc.dram_tensor("wsc", [NCH, 128, 8192], BF16).ap()
    cc_in = nc.dram_tensor("cc_in", [2, 128, 32], F32)
    cc_out = nc.dram_tensor("cc_out", [2 * NC, 128, 32], F32)

    dbg = {}
    if DEBUG:
        for nm, shp in [("d_carry", [128, 32]), ("d_hT", [128, 8192]),
                        ("d_x1", [128, 8192]), ("d_uF", [128, 8192]),
                        ("d_q", [128, 4096]), ("d_k", [128, 4096]),
                        ("d_lg", [128, 128]), ("d_o0", [128, 2048]),
                        ("d_o1", [128, 2048]), ("d_mF", [128, 8192]),
                        ("d_out2", [128, 2048]), ("d_cf", [128, 256]),
                        ("d_gv", [128, 4096]), ("d_AS", [128, 32]),
                        ("d_BS", [128, 32]), ("d_stA", [128, 128]),
                        ("d_xlnF", [128, 32768]), ("d_xt", [128, 2048])]:
            dbg[nm] = nc.dram_tensor(nm, shp, F32, kind="ExternalOutput").ap()

    wsrc = [(wq, 0, 1024), (wk, 1024, 1024), (wv, 2048, 2048), (wg, 4096, 2048), (wo, 6144, 2048)]

    from contextlib import ExitStack
    with ExitStack() as stack:
        sb = lambda *a: stack.enter_context(nc.sbuf_tensor(*a))
        idb = sb("idb", [128, 128], BF16)
        cw0b = sb("cw0b", [128, C], BF16)
        cw1b = sb("cw1b", [128, C], BF16)
        cosb = sb("cosb", [128, 1024], BF16)
        sinb = sb("sinb", [128, 1024], BF16)
        rbdb = sb("rbdb", [128, 512], BF16)
        wvrb = sb("wvrb", [128, 512], BF16)
        carry = sb("carry", [128, 32], F32)
        epst = sb("epst", [128, 1], F32)
        pb = [stack.enter_context(nc.psum_tensor("pb%d" % i, [128, 512], F32)) for i in range(8)]
        def mksem(nm):
            return Sem(stack.enter_context(nc.semaphore(nm)))
        dCS = mksem("dCS"); dCG = mksem("dCG"); dCA = mksem("dCA")
        dX = [mksem("dX0"), mksem("dX1")]
        dWA = [mksem("dWA0"), mksem("dWA1")]
        dSP = [mksem("dSP0"), mksem("dSP1")]
        dCC = mksem("dCC")
        dGV = [mksem("dGV0"), mksem("dGV1")]
        dWS = [mksem("dWS0"), mksem("dWS1")]
        dOF = [mksem("dOF0"), mksem("dOF1")]
        dXB = [mksem("dXB0"), mksem("dXB1"), mksem("dXB2"), mksem("dXB3")]
        dWV = [mksem("dWV0"), mksem("dWV1")]
        dDB = mksem("dDB")
        dDBG = mksem("dDBG")
        cT = mksem("cT"); cV = mksem("cV"); cS = mksem("cS"); cG = mksem("cG")
        SEMS = [dCS, dCG, dCA] + dX + dWA + dSP + [dCC] + dGV + dWS + dOF + dXB + dWV + [dDB, dDBG, cT, cV, cS, cG]
        bankb = [Buf(1) for _ in range(8)]

        state = {}

        def snap():
            return [s.n for s in SEMS]

        def restore(v):
            for s, x in zip(SEMS, v):
                s.n = x

        def sched_A(em, t):
            for bk in bankb:
                bk.waits = [[]]
            em.dma("sync", t["maskt"][:, :], maskin, dCS)
            em.dma("sync", t["b2t"][:, :], b2, dCS)
            sy_const = em.dma("sync", epst[:, :], epsin, dCS)
            em.dma("gpsimd", idb[:, :], idin, dCG)
            em.dma("gpsimd", cosb[:, :], cosq, dCG)
            em.dma("gpsimd", sinb[:, :], sins, dCG)
            em.dma("gpsimd", cw0b[:, :], cw0, dCG)
            em.dma("gpsimd", cw1b[:, :], cw1, dCG)
            gp_const = em.dma("gpsimd",
                              rbdb[:, :].rearrange("p (a m) -> p a m", a=16),
                              rbd.rearrange("(a p) m -> p a m", p=128), dCG)


            def vstep(fn):
                ev = em.do("vector", fn, inc=[(cV, 1)])
                em.wait("vector", cV, ev)
                return ev

            cast_jobs = []
            for (csrc, col0, w) in wsrc:
                for ck in range(NCH):
                    cast_jobs.append((csrc, col0, w, ck))

            def emit_casts(njobs):
                for _ in range(njobs):
                    if not cast_jobs:
                        return
                    csrc, col0, w, ck = cast_jobs.pop(0)
                    em.dma("gpsimd", wsc[ck, :, col0:col0 + w],
                           csrc[128 * ck:128 * (ck + 1), :], dCA)

            xt = t["xt"]; xnb = t["xnb"]; xlnF = t["xlnF"]; sqd = t["sqd"]
            stA = t["stA"]
            xt_b = Buf(2); xnb_b = Buf(2); wvs_b = Buf(2)
            evA1 = [0] * NT

            em.do("vector", lambda e: e.memset(t["ones1"][:, :], 1.0))
            em.wait("vector", dCS, sy_const)
            em.wait("scalar", dCS, sy_const)
            em.wait("tensor", dCG, gp_const)

            for i in range(NT):
                bbi, qq = i // 8, i % 8
                s = xt_b.acquire(em, "sync", i)
                cx = em.dma("sync", xt[:, 2048 * s:2048 * (s + 1)],
                            xin[bbi, 128 * qq:128 * (qq + 1), :], dX[s])
                xts = xt[:, 2048 * s:2048 * (s + 1)]
                em.wait("vector", dX[s], cx)
                em.do("vector", lambda e, xts=xts, i=i: e.reduce_sum(stA[:, i:i + 1], xts, AXX))
                em.do("vector", lambda e, xts=xts, i=i: e.scalar_tensor_tensor(
                    sqd[:, :], xts, 1.0, xts, OP.bypass, OP.mult, accum_out=stA[:, 16 + i:17 + i]))
                em.do("vector", lambda e, i=i: e.tensor_single_scalar(stA[:, 32 + i:33 + i], stA[:, i:i + 1], 1.0 / C, OP.mult))
                em.do("vector", lambda e, i=i: e.tensor_single_scalar(stA[:, 48 + i:49 + i], stA[:, 16 + i:17 + i], 1.0 / C, OP.mult))
                vmu2 = em.do("vector", lambda e, i=i: e.tensor_mul(stA[:, 64 + i:65 + i], stA[:, 32 + i:33 + i], stA[:, 32 + i:33 + i]), inc=[(cV, 1)])
                em.wait("vector", cV, vmu2)
                v2 = em.do("vector", lambda e, i=i: e.tensor_sub(stA[:, 80 + i:81 + i], stA[:, 48 + i:49 + i], stA[:, 64 + i:65 + i]), inc=[(cV, 1)])
                em.wait("scalar", cV, v2)
                s2 = em.do("scalar", lambda e, i=i: e.activation(stA[:, 96 + i:97 + i], stA[:, 80 + i:81 + i], AF.Sqrt, bias=epst[:, 0:1]), inc=[(cS, 1)])
                em.wait("vector", cS, s2)
                vrc = em.do("vector", lambda e, i=i: e.reciprocal(stA[:, 112 + i:113 + i], stA[:, 96 + i:97 + i]), inc=[(cV, 1)])
                em.wait("vector", cV, vrc)
                sn = xnb_b.acquire(em, "vector", i)
                xnbs = xnb[:, 2048 * sn:2048 * (sn + 1)]
                v3 = em.do("vector", lambda e, xnbs=xnbs, xts=xts, i=i: e.tensor_scalar(
                    xnbs, xts, stA[:, 32 + i:33 + i], stA[:, 112 + i:113 + i], OP.subtract, OP.mult), inc=[(cV, 1)])
                xt_b.used_by(s, cV, v3)
                em.wait("tensor", cV, v3)
                for h2 in range(2):
                    bk = 4 + h2
                    bankb[bk].acquire(em, "tensor", 0)
                    pv = pb[bk][:, :].bitcast(BF16)
                    for k in range(8):
                        kk = 8 * h2 + k
                        last = em.do("tensor", lambda e, kk=kk, k=k, pv=pv, xnbs=xnbs: e.transpose(
                            pv[:, 128 * k:128 * (k + 1)], xnbs[:, 128 * kk:128 * (kk + 1)], idb[:, :]),
                            inc=[(cT, 1)] if k == 7 else None)
                    em.wait("scalar", cT, last)
                    dst = xlnF[:, :].rearrange("p (a m) -> p a m", a=16)[:, 8 * h2:8 * h2 + 8, 128 * i:128 * (i + 1)]
                    ve = em.do("scalar", lambda e, dst=dst, pv=pv: e.copy(
                        dst, pv.rearrange("p (a m) -> p a m", a=8)), inc=[(cS, 1)])
                    bankb[bk].used_by(0, cS, ve)
                xnb_b.used_by(sn, cS, ve)
                evA1[i] = ve
                j = i
                wsl = wvs_b.acquire(em, "gpsimd", j)
                cwv = em.dma("gpsimd",
                             t["wvstage"][:, 2048 * wsl:2048 * (wsl + 1)].rearrange("p (a m) -> p a m", a=16),
                             wvt[:, 128 * j:128 * (j + 1)].rearrange("(a p) m -> p a m", p=128), dWV[wsl])
                em.wait("tensor", dWV[wsl], cwv)
                bankb[7].acquire(em, "tensor", 0)
                for k in range(NCH):
                    mmw = em.do("tensor", lambda e, k=k, wsl=wsl: e.matmul(
                        pb[7][:, 0:32],
                        t["wvstage"][:, 2048 * wsl + 128 * k:2048 * wsl + 128 * (k + 1)],
                        rbdb[:, 32 * k:32 * (k + 1)],
                        start=(k == 0), stop=(k == NCH - 1)),
                        inc=[(cT, 1)] if k == NCH - 1 else None)
                wvs_b.used_by(wsl, cT, mmw)
                em.wait("scalar", cT, mmw)
                wvre = em.do("scalar", lambda e, j=j: e.copy(
                    wvrb[:, 32 * j:32 * (j + 1)], pb[7][:, 0:32]), inc=[(cS, 1)])
                bankb[7].used_by(0, cS, wvre)
                emit_casts(2)

            if DEBUG:
                em.wait("sync", cV, cV.n)
                em.wait("sync", cS, cS.n)
                em.dma("sync", dbg["d_stA"], stA[:, :], dDB)
                dxl = em.dma("sync", dbg["d_xt"], xt[:, 2048:4096], dDB)
                em.wait("gpsimd", cV, cV.n)
                dxf = em.dma("gpsimd", dbg["d_xlnF"], xlnF[:, :], dDBG)
                em.wait("vector", dDBG, dxf)
                em.wait("vector", dDB, dxl)
                em.wait("tensor", dDBG, dxf)

            wA = t["wA"]; gv = t["gv"]; hloc = t["hloc"]
            AS = t["AS"]; BS = t["BS"]
            wA_b = Buf(2)
            gv_b = Buf(2)
            se = ve2 = 0
            pend = []

            def emit_spill():
                jd, jpd, gtd, vtd, sed = pend.pop(0)
                em.wait("gpsimd", cS, sed)
                em.dma("gpsimd", gsp[jd, :, :], gtd, dSP[jpd])
                sp2d = em.dma("gpsimd", vsp[jd, :, :], vtd, dSP[jpd])
                gv_b.used_by(jpd, dSP[jpd], sp2d)
                emit_casts(3)

            for j in range(NCH):
                sl = wA_b.acquire(em, "gpsimd", j)
                for idx in range(2):
                    d = j if idx == 0 else 16 + j
                    cwl = em.dma("gpsimd",
                                 wA[:, 4096 * sl + 2048 * idx:4096 * sl + 2048 * (idx + 1)].rearrange("p (a m) -> p a m", a=16),
                                 wt[:, 128 * d:128 * (d + 1)].rearrange("(a p) m -> p a m", p=128), dWA[sl])
                if pend:
                    emit_spill()
                jp = gv_b.acquire(em, "scalar", j)
                for idx in range(2):
                    d = j if idx == 0 else 16 + j
                    em.wait("tensor", dWA[sl], cwl)
                    gvt = gv[:, 4096 * jp + 2048 * idx: 4096 * jp + 2048 * (idx + 1)]
                    for n in range(4):
                        em.wait("tensor", cS, evA1[4 * n + 3])
                        bk = n
                        bankb[bk].acquire(em, "tensor", 0)
                        for k in range(NCH):
                            last = em.do("tensor", lambda e, k=k, bk=bk, sl=sl, n=n, idx=idx: e.matmul(
                                pb[bk][:, :],
                                wA[:, 4096 * sl + 2048 * idx + 128 * k:4096 * sl + 2048 * idx + 128 * (k + 1)],
                                xlnF[:, 2048 * k + 512 * n:2048 * k + 512 * (n + 1)],
                                start=(k == 0), stop=(k == NCH - 1)),
                                inc=[(cT, 1)] if k == NCH - 1 else None)
                        em.wait("scalar", cT, last)
                        fn = AF.Sigmoid if idx == 0 else AF.Tanh
                        se = em.do("scalar", lambda e, gvt=gvt, n=n, bk=bk, fn=fn, d=d: e.activation(
                            gvt[:, 512 * n:512 * (n + 1)], pb[bk][:, :], fn,
                            bias=t["b2t"][:, d:d + 1]), inc=[(cS, 1)])
                        bankb[bk].used_by(0, cS, se)
                    wA_b.used_by(sl, cT, last)
                gt = gv[:, 4096 * jp:4096 * jp + 2048]
                vt = gv[:, 4096 * jp + 2048:4096 * jp + 4096]
                pend.append((j, jp, gt, vt, se))
                em.wait("vector", cS, se)
                em.do("vector", lambda e, gt=gt, vt=vt: e.tensor_tensor_scan(
                    hloc[:, 0:1024], gt[:, 0:1024], vt[:, 0:1024], 0.0, OP.mult, OP.add))
                vstep(lambda e, gt=gt, vt=vt: e.tensor_tensor_scan(
                    hloc[:, 1024:2048], gt[:, 1024:2048], vt[:, 1024:2048], 0.0, OP.mult, OP.add))
                em.do("vector", lambda e, j=j: e.tensor_copy(BS[:, j:j + 1], hloc[:, 1023:1024]))
                em.do("vector", lambda e, j=j: e.tensor_copy(BS[:, 16 + j:17 + j], hloc[:, 2047:2048]))
                vstep(lambda e, gt=gt: e.tensor_tensor_scan(
                    hloc[:, 0:1024], gt[:, 0:1024], t["ones1"][:, :], 1.0, OP.mult, OP.mult))
                em.do("vector", lambda e, j=j: e.tensor_copy(AS[:, j:j + 1], hloc[:, 1023:1024]))
                vstep(lambda e, gt=gt: e.tensor_tensor_scan(
                    hloc[:, 0:1024], gt[:, 1024:2048], t["ones1"][:, :], 1.0, OP.mult, OP.mult))
                ve2 = em.do("vector", lambda e, j=j: e.tensor_copy(AS[:, 16 + j:17 + j], hloc[:, 1023:1024]), inc=[(cV, 1)])
                gv_b.used_by(jp, cV, ve2)

            while pend:
                emit_spill()
            emit_casts(100)
            em.wait("sync", cV, ve2)
            em.dma("sync", cc_in.ap()[0], AS[:, :], dCC)
            ci2 = em.dma("sync", cc_in.ap()[1], BS[:, :], dCC)
            em.wait("gpsimd", dCC, ci2)
            state["a_preex"] = snap()
            cce = em.do("gpsimd", lambda e: e.collective_compute(
                "AllGather", OP.bypass, replica_groups=[list(range(NC))],
                ins=[cc_in.ap().opt()], outs=[cc_out.ap().opt()]), inc=[(cG, 1)])
            em.wait("sync", cG, cce)
            cro = em.dma("sync", t["ccAll"][:, :].rearrange("p (a m) -> p a m", a=16),
                         cc_out.ap().rearrange("a p m -> p a m"), dCC)
            em.wait("vector", dCC, cro)
            vstep(lambda e: e.memset(t["hcomb"][:, 0:32], 0.0))
            cur = 0
            tmpc = t["hcomb"][:, 64:96]
            for jj in range(NC - 1):
                A_j = t["ccAll"][:, 64 * jj:64 * jj + 32]
                B_j = t["ccAll"][:, 64 * jj + 32:64 * jj + 64]
                Hc = t["hcomb"][:, 32 * cur:32 * cur + 32]
                Hn = t["hcomb"][:, 32 * (1 - cur):32 * (1 - cur) + 32]
                vstep(lambda e, A_j=A_j, Hc=Hc: e.tensor_mul(tmpc, A_j, Hc))
                vstep(lambda e, B_j=B_j: e.tensor_add(tmpc, tmpc, B_j))
                vstep(lambda e, Hc=Hc: e.tensor_sub(tmpc, tmpc, Hc))
                vstep(lambda e, jj=jj, Hc=Hc, Hn=Hn: e.scalar_tensor_tensor(
                    Hn, tmpc, t["maskt"][:, jj:jj + 1], Hc, OP.mult, OP.add))
                cur = 1 - cur
            vfin = em.do("vector", lambda e, cur=cur: e.tensor_copy(
                carry[:, :], t["hcomb"][:, 32 * cur:32 * cur + 32]), inc=[(cV, 1)])
            if DEBUG:
                em.wait("sync", cV, vfin)
                em.dma("sync", dbg["d_carry"], carry[:, :], dDB)
                em.dma("sync", dbg["d_gv"], gv[:, 4096:8192], dDB)
                em.dma("sync", dbg["d_AS"], AS[:, :], dDB)
                em.dma("sync", dbg["d_BS"], BS[:, :], dDB)
            state["a_end"] = snap()

        def sched_B(em, t):
            for bk in bankb:
                bk.waits = [[]]
            ae = state["a_end"]
            ape = state["a_preex"]
            for w in ENGS:
                vals = ape if w == "gpsimd" else ae
                for s_, v_ in zip(SEMS, vals):
                    em.wait(w, s_, v_)

            def vstep(fn):
                ev = em.do("vector", fn, inc=[(cV, 1)])
                em.wait("vector", cV, ev)
                return ev

            hT = t["hT"]; x1a = t["x1"]; xbf = t["xbf"]
            SCR = t["SCR"]; SCR2 = t["SCR2"]
            o2a = t["o2a"]; xn1b = t["xn1b"]; vb = t["vb"]; gsb = t["gsb"]
            umb = t["umb"]; fF = t["fF"]; wS = t["wS"]
            qb = t["qb"]; kb = t["kb"]
            qr = t["SCR"][:, :].bitcast(BF16)[:, 2048:3072]
            qt = t["SCR"][:, :].bitcast(BF16)[:, 3072:4096]
            lg = t["lg"]; dcf = t["dcf"]; gvl = t["gvl"]; hFt = t["hFt"]
            st = t["st"]

            xbf_b = Buf(2)
            gvl_b = Buf(2)
            hFt_b = Buf(2)
            um_b = Buf(4)
            wS_b = Buf(2)
            outf_b = Buf(2)
            x1_b = Buf(2)
            hT_b = Buf(1)
            quad = [[0, 1, 2, 3], [4, 5, 6, 7]]
            wgi = [0]
            xli = [0]
            SL = {}

            def x1v(s):
                return x1a[:, 8192 * (s % 2):8192 * (s % 2 + 1)]

            def fF3(k, tt):
                return fF[:, :].rearrange("p (a m) -> p a m", a=16)[:, k, 128 * tt:128 * (tt + 1)]

            def run_group(wbase, col0, evac_fn, qd, ttwaits=None, srcf=None):
                if srcf is None:
                    srcf = fF3
                def emit_wdma(bt):
                    slw = wS_b.acquire(em, "gpsimd", wgi[0])
                    wgi[0] += 1
                    cwl = em.dma("gpsimd",
                                 wS[:, 2048 * slw:2048 * (slw + 1)].rearrange("p (a m) -> p a m", a=4),
                                 wsc[4 * bt:4 * (bt + 1), :, wbase + col0:wbase + col0 + 512].rearrange("a p m -> p a m"),
                                 dWS[slw])
                    return (slw, cwl)
                ch = [emit_wdma(0), emit_wdma(1)]
                for bt in range(4):
                    slw, cwl = ch[bt]
                    em.wait("tensor", dWS[slw], cwl)
                    for kk in range(4):
                        k = 4 * bt + kk
                        for tt in range(4):
                            if k == 0:
                                if ttwaits is not None:
                                    em.wait("tensor", cS, ttwaits[tt])
                                bankb[quad[qd][tt]].acquire(em, "tensor", 0)
                            last = em.do("tensor", lambda e, k=k, kk=kk, tt=tt, qd=qd, slw=slw, srcf=srcf: e.matmul(
                                pb[quad[qd][tt]][:, :], srcf(k, tt),
                                wS[:, 2048 * slw + 512 * kk:2048 * slw + 512 * (kk + 1)],
                                start=(k == 0), stop=(k == NCH - 1)),
                                inc=[(cT, 1)] if tt == 3 else None)
                    wS_b.used_by(slw, cT, last)
                    if bt + 2 < 4:
                        ch.append(emit_wdma(bt + 2))
                for tt in range(4):
                    evac_fn(tt, quad[qd][tt], last)

            def emit_B1(s):
                hT_b.acquire(em, "scalar", 0)
                hev = 0
                for j in range(NCH):
                    sl = gvl_b.acquire(em, "sync", j)
                    em.dma("sync", gvl[:, 1024 * sl:1024 * sl + 512],
                           gsp[j].rearrange("p (b t) -> p b t", b=2)[:, :, SPOS * s:SPOS * (s + 1)], dGV[sl])
                    g2 = em.dma("sync", gvl[:, 1024 * sl + 512:1024 * (sl + 1)],
                                vsp[j].rearrange("p (b t) -> p b t", b=2)[:, :, SPOS * s:SPOS * (s + 1)], dGV[sl])
                    em.wait("vector", dGV[sl], g2)
                    hsl = hFt_b.acquire(em, "vector", j)
                    hfv = hFt[:, 512 * hsl:512 * (hsl + 1)]
                    gg = gvl[:, 1024 * sl:1024 * sl + 512]
                    vv = gvl[:, 1024 * sl + 512:1024 * (sl + 1)]
                    vstep(lambda e, gg=gg, vv=vv, hfv=hfv, j=j: e.tensor_tensor_scan(
                        hfv[:, 0:256], gg[:, 0:256], vv[:, 0:256], carry[:, j:j + 1], OP.mult, OP.add))
                    em.do("vector", lambda e, hfv=hfv, j=j: e.tensor_copy(carry[:, j:j + 1], hfv[:, 255:256]))
                    vstep(lambda e, gg=gg, vv=vv, hfv=hfv, j=j: e.tensor_tensor_scan(
                        hfv[:, 256:512], gg[:, 256:512], vv[:, 256:512], carry[:, 16 + j:17 + j], OP.mult, OP.add))
                    vsc = em.do("vector", lambda e, hfv=hfv, j=j: e.tensor_copy(
                        carry[:, 16 + j:17 + j], hfv[:, 511:512]), inc=[(cV, 1)])
                    gvl_b.used_by(sl, cV, vsc)
                    em.wait("tensor", cV, vsc)
                    pbk = 6 + j % 2
                    bankb[pbk].acquire(em, "tensor", 0)
                    pvb = pb[pbk][:, :].bitcast(BF16)
                    for tt in range(4):
                        last = em.do("tensor", lambda e, tt=tt, pvb=pvb, hfv=hfv: e.transpose(
                            pvb[:, 128 * tt:128 * (tt + 1)], hfv[:, 128 * tt:128 * (tt + 1)], idb[:, :]),
                            inc=[(cT, 1)] if tt == 3 else None)
                    hFt_b.used_by(hsl, cT, last)
                    em.wait("scalar", cT, last)
                    dst = hT[:, :].rearrange("p (a m) -> p a m", a=4)[:, :, 128 * j:128 * (j + 1)]
                    hev = em.do("scalar", lambda e, dst=dst, pvb=pvb: e.copy(
                        dst, pvb[:, 0:512].rearrange("p (a m) -> p a m", a=4)), inc=[(cS, 1)])
                    bankb[pbk].used_by(0, cS, hev)
                SL[s] = {"hev": hev}

            def emit_P2(s):
                x1 = x1v(s)
                hev = SL[s]["hev"]
                x1_b.acquire(em, "vector", s)
                for tt in range(4):
                    em.wait("vector", cS, hev)
                    hblk = hT[:, 2048 * tt:2048 * (tt + 1)]
                    em.do("vector", lambda e, hblk=hblk, tt=tt: e.reduce_sum(st[:, tt:1 + tt], hblk, AXX))
                    em.do("vector", lambda e, hblk=hblk, tt=tt: e.scalar_tensor_tensor(
                        SCR[:, :], hblk, 1.0, hblk, OP.bypass, OP.mult, accum_out=st[:, 4 + tt:5 + tt]))
                vstep(lambda e: e.engine_nop())
                vstep(lambda e: e.tensor_single_scalar(st[:, 8:12], st[:, 0:4], 1.0 / C, OP.mult))
                vstep(lambda e: e.tensor_single_scalar(st[:, 12:16], st[:, 4:8], 1.0 / C, OP.mult))
                vstep(lambda e: e.tensor_mul(st[:, 16:20], st[:, 8:12], st[:, 8:12]))
                vst = em.do("vector", lambda e: e.tensor_sub(st[:, 12:16], st[:, 12:16], st[:, 16:20]), inc=[(cV, 1)])
                em.wait("scalar", cV, vst)
                sst = em.do("scalar", lambda e: e.activation(st[:, 16:20], st[:, 12:16], AF.Sqrt, bias=epst[:, 0:1]), inc=[(cS, 1)])
                em.wait("vector", cS, sst)
                vstep(lambda e: e.reciprocal(st[:, 20:24], st[:, 16:20]))
                for tt in range(4):
                    i = [2 * s, 2 * s + 1, 8 + 2 * s, 9 + 2 * s][tt]
                    bbi, qq = i // 8, i % 8
                    sx = xbf_b.acquire(em, "gpsimd", xli[0])
                    xli[0] += 1
                    cx = em.dma("gpsimd", xbf[:, 2048 * sx:2048 * (sx + 1)],
                                xin[bbi, 128 * qq:128 * (qq + 1), :], dXB[sx])
                    em.wait("vector", dXB[sx], cx)
                    hblk = hT[:, 2048 * tt:2048 * (tt + 1)]
                    x1blk = x1[:, 2048 * tt:2048 * (tt + 1)]
                    em.do("vector", lambda e, hblk=hblk, tt=tt: e.tensor_scalar(
                        SCR[:, :], hblk, st[:, 8 + tt:9 + tt], st[:, 20 + tt:21 + tt], OP.subtract, OP.mult))
                    vx1 = em.do("vector", lambda e, x1blk=x1blk, sx=sx, tt=tt: e.scalar_tensor_tensor(
                        x1blk, SCR[:, :], 1.0, xbf[:, 2048 * sx:2048 * (sx + 1)], OP.bypass, OP.add,
                        accum_out=st[:, 24 + tt:25 + tt]), inc=[(cV, 1)])
                    xbf_b.used_by(sx, cV, vx1)
                    em.do("vector", lambda e, x1blk=x1blk, tt=tt: e.scalar_tensor_tensor(
                        SCR[:, :], x1blk, 1.0, x1blk, OP.bypass, OP.mult, accum_out=st[:, 28 + tt:29 + tt]))
                hT_b.used_by(0, cV, cV.n)
                if DEBUG and s == 0:
                    em.wait("gpsimd", cV, cV.n)
                    dh = em.dma("gpsimd", dbg["d_hT"], hT[:, :], dDBG)
                    em.wait("vector", dDBG, dh)
                vstep(lambda e: e.engine_nop())
                vstep(lambda e: e.tensor_single_scalar(st[:, 32:36], st[:, 24:28], 1.0 / C, OP.mult))
                vstep(lambda e: e.tensor_single_scalar(st[:, 36:40], st[:, 28:32], 1.0 / C, OP.mult))
                vstep(lambda e: e.tensor_mul(st[:, 40:44], st[:, 32:36], st[:, 32:36]))
                vst2 = em.do("vector", lambda e: e.tensor_sub(st[:, 36:40], st[:, 36:40], st[:, 40:44]), inc=[(cV, 1)])
                em.wait("scalar", cV, vst2)
                sst2 = em.do("scalar", lambda e: e.activation(st[:, 40:44], st[:, 36:40], AF.Sqrt, bias=epst[:, 0:1]), inc=[(cS, 1)])
                em.wait("vector", cS, sst2)
                vstep(lambda e: e.reciprocal(st[:, 44:48], st[:, 40:44]))

            def emit_P3P4(s):
                x1 = x1v(s)
                outf_b.acquire(em, "vector", 0)
                outf_b.acquire(em, "vector", 1)
                uF_evs = [0] * 4
                prev_silu = None
                for tt in range(4):
                    x1blk = x1[:, 2048 * tt:2048 * (tt + 1)]
                    if prev_silu is not None:
                        em.wait("vector", cS, prev_silu)
                    if tt < 2:
                        em.do("vector", lambda e, x1blk=x1blk, tt=tt: e.tensor_scalar(
                            xn1b[:, 2048 * tt:2048 * (tt + 1)], x1blk, st[:, 32 + tt:33 + tt], st[:, 44 + tt:45 + tt], OP.subtract, OP.mult))
                        vcv = em.do("vector", lambda e, tt=tt: e.tensor_mul(
                            SCR2[:, :], xn1b[:, 2048 * tt:2048 * (tt + 1)], cw1b[:, :]), inc=[(cV, 1)])
                    else:
                        em.do("vector", lambda e, x1blk=x1blk, tt=tt: e.tensor_scalar(
                            SCR[:, :], x1blk, st[:, 32 + tt:33 + tt], st[:, 44 + tt:45 + tt], OP.subtract, OP.mult))
                        em.do("vector", lambda e: e.tensor_mul(SCR2[:, :], SCR[:, :], cw1b[:, :]))
                        em.do("vector", lambda e, tt=tt: e.tensor_mul(
                            SCR[:, :], xn1b[:, 2048 * (tt - 2):2048 * (tt - 1)], cw0b[:, :]))
                        vcv = em.do("vector", lambda e: e.tensor_add(SCR2[:, :], SCR2[:, :], SCR[:, :]), inc=[(cV, 1)])
                    usl = um_b.acquire(em, "scalar", umi[0])
                    umi[0] += 1
                    em.wait("scalar", cV, vcv)
                    sustp = em.do("scalar", lambda e, usl=usl: e.activation(
                        umb[:, 2048 * usl:2048 * (usl + 1)], SCR2[:, :], AF.Silu), inc=[(cS, 1)])
                    prev_silu = sustp
                    em.wait("tensor", cS, sustp)
                    for h2 in range(2):
                        bk = 4 + h2
                        bankb[bk].acquire(em, "tensor", 0)
                        pv = pb[bk][:, :].bitcast(BF16)
                        for k in range(8):
                            kk = 8 * h2 + k
                            last = em.do("tensor", lambda e, pv=pv, k=k, kk=kk, usl=usl: e.transpose(
                                pv[:, 128 * k:128 * (k + 1)], umb[:, 2048 * usl + 128 * kk:2048 * usl + 128 * (kk + 1)], idb[:, :]),
                                inc=[(cT, 1)] if k == 7 else None)
                        em.wait("scalar", cT, last)
                        dst = fF[:, :].rearrange("p (a m) -> p a m", a=16)[:, 8 * h2:8 * h2 + 8, 128 * tt:128 * (tt + 1)]
                        uev = em.do("scalar", lambda e, dst=dst, pv=pv: e.copy(
                            dst, pv.rearrange("p (a m) -> p a m", a=8)), inc=[(cS, 1)])
                        bankb[bk].used_by(0, cS, uev)
                    um_b.used_by(usl, cT, last)
                    uF_evs[tt] = uev
                SL[s]["uF_evs"] = uF_evs
                SL[s]["silu_last"] = prev_silu

            def emit_logits(s):
                uF_evs = SL[s]["uF_evs"]
                lgp = lg[:, 128 * (s % 2):128 * (s % 2 + 1)]
                for tt in range(4):
                    em.wait("tensor", cS, uF_evs[tt])
                    bankb[2].acquire(em, "tensor", 0)
                    for k in range(NCH):
                        last = em.do("tensor", lambda e, k=k, tt=tt: e.matmul(
                            pb[2][:, 0:32], fF3(k, tt), wvrb[:, 32 * k:32 * (k + 1)],
                            start=(k == 0), stop=(k == NCH - 1)),
                            inc=[(cT, 1)] if k == NCH - 1 else None)
                    em.wait("scalar", cT, last)
                    lev = em.do("scalar", lambda e, tt=tt, lgp=lgp: e.copy(
                        lgp[:, 32 * tt:32 * (tt + 1)], pb[2][:, 0:32]), inc=[(cS, 1)])
                    bankb[2].used_by(0, cS, lev)
                SL[s]["lg_ev"] = lev

            def emit_qkvg(s):
                prev = SL.get(s - 1, {})
                def ev_q(tt, bkk, last, gi2):
                    dste = qb[:, 1024 * tt + 512 * gi2:1024 * tt + 512 * (gi2 + 1)]
                    em.wait("scalar", cT, last)
                    if "qk_read" in prev:
                        em.wait("scalar", cV, prev["qk_read"])
                    ee = em.do("scalar", lambda e, dste=dste, bkk=bkk: e.activation(
                        dste, pb[bkk][:, :], AF.Silu), inc=[(cS, 1)])
                    bankb[bkk].used_by(0, cS, ee)
                def ev_k(tt, bkk, last, gi2):
                    dste = kb[:, 1024 * tt + 512 * gi2:1024 * tt + 512 * (gi2 + 1)]
                    em.wait("scalar", cT, last)
                    if "qk_read" in prev:
                        em.wait("scalar", cV, prev["qk_read"])
                    ee = em.do("scalar", lambda e, dste=dste, bkk=bkk: e.copy(
                        dste, pb[bkk][:, :]), inc=[(cS, 1)])
                    bankb[bkk].used_by(0, cS, ee)
                gidx = 0
                for (wap, ngr, evf) in [(0, 2, ev_q), (1024, 2, ev_k)]:
                    for gi2 in range(ngr):
                        run_group(wap, 512 * gi2,
                                  lambda tt, bkk, last, gi2=gi2, evf=evf: evf(tt, bkk, last, gi2),
                                  gidx % 2)
                        gidx += 1
                SL[s]["qk_S"] = cS.n

            def emit_qkvg_vg(s):
                prev = SL.get(s - 1, {})
                def ev_v(tt, bkk, last, gi2):
                    dste = vb[:, 2048 * tt + 512 * gi2:2048 * tt + 512 * (gi2 + 1)]
                    em.wait("scalar", cT, last)
                    if "v_read" in prev:
                        em.wait("scalar", cV, prev["v_read"])
                    ee = em.do("scalar", lambda e, dste=dste, bkk=bkk: e.copy(
                        dste, pb[bkk][:, :]), inc=[(cS, 1)])
                    bankb[bkk].used_by(0, cS, ee)
                def ev_g(tt, bkk, last, gi2):
                    dste = gsb[:, 2048 * tt + 512 * gi2:2048 * tt + 512 * (gi2 + 1)]
                    em.wait("scalar", cT, last)
                    if "g_read" in prev:
                        em.wait("scalar", cV, prev["g_read"])
                    ee = em.do("scalar", lambda e, dste=dste, bkk=bkk: e.activation(
                        dste, pb[bkk][:, :], AF.Silu), inc=[(cS, 1)])
                    bankb[bkk].used_by(0, cS, ee)
                gidx = 4
                for (wap, ngr, evf) in [(2048, 4, ev_v), (4096, 4, ev_g)]:
                    for gi2 in range(ngr):
                        run_group(wap, 512 * gi2,
                                  lambda tt, bkk, last, gi2=gi2, evf=evf: evf(tt, bkk, last, gi2),
                                  gidx % 2)
                        gidx += 1
                SL[s]["vg_S"] = cS.n
                if DEBUG and s == 0:
                    em.wait("gpsimd", cS, SL[s]["qk_S"])
                    em.dma("gpsimd", dbg["d_uF"], fF[:, :], dDBG)
                    em.dma("gpsimd", dbg["d_q"], qb[:, :], dDBG)
                    duk = em.dma("gpsimd", dbg["d_k"], kb[:, :], dDBG)
                    dlg = em.dma("gpsimd", dbg["d_lg"], lg[:, 0:128], dDBG)
                    em.wait("tensor", dDBG, dlg)

            umi = [0]

            def emit_mid_v(s):
                mevs = [0] * 4
                uslmap = {}
                vmes = {}
                o2abf = o2a[:, :].bitcast(BF16)
                lgp = lg[:, 128 * (s % 2):128 * (s % 2 + 1)]
                r32 = lambda ap: ap.rearrange("p (h two d) -> p h two d", two=2, d=32)
                r3 = lambda ap: ap.rearrange("p (h d) -> p h d", h=16)
                em.wait("vector", cS, SL[s]["qk_S"])
                for pi in range(2):
                    tb0, tb1 = pi, 2 + pi
                    dc = 256 * pi
                    qsrc = qb[:, 1024 * tb1:1024 * (tb1 + 1)]
                    em.do("vector", lambda e, qsrc=qsrc: e.tensor_mul(qr, qsrc, cosb[:, :]))
                    em.do("vector", lambda e, qsrc=qsrc: e.tensor_mul(
                        r32(qt)[:, :, 0, :], r32(qsrc)[:, :, 1, :], r32(sinb[:, :])[:, :, 0, :]))
                    vstep(lambda e, qsrc=qsrc: e.tensor_mul(
                        r32(qt)[:, :, 1, :], r32(qsrc)[:, :, 0, :], r32(sinb[:, :])[:, :, 1, :]))
                    em.do("vector", lambda e: e.tensor_add(qr, qr, qt))
                    DOT = SCR
                    em.do("vector", lambda e, tb0=tb0: e.tensor_mul(
                        DOT[:, 0:1024], qb[:, 1024 * tb0:1024 * (tb0 + 1)], kb[:, 1024 * tb0:1024 * (tb0 + 1)]))
                    em.do("vector", lambda e, dc=dc: e.tensor_reduce(dcf[:, dc:dc + 16], r3(DOT[:, 0:1024]), AXX, OP.add))
                    em.do("vector", lambda e, tb0=tb0: e.tensor_mul(
                        DOT[:, 0:1024], qr, kb[:, 1024 * tb0:1024 * (tb0 + 1)]))
                    em.do("vector", lambda e, dc=dc: e.tensor_reduce(dcf[:, dc + 16:dc + 32], r3(DOT[:, 0:1024]), AXX, OP.add))
                    em.do("vector", lambda e, tb1=tb1: e.tensor_mul(
                        DOT[:, 0:1024], qb[:, 1024 * tb1:1024 * (tb1 + 1)], kb[:, 1024 * tb1:1024 * (tb1 + 1)]))
                    em.do("vector", lambda e, dc=dc: e.tensor_reduce(dcf[:, dc + 32:dc + 48], r3(DOT[:, 0:1024]), AXX, OP.add))
                SL[s]["qk_read"] = em.do("vector", lambda e: e.engine_nop(), inc=[(cV, 1)])
                for pi in range(2):
                    tb0, tb1 = pi, 2 + pi
                    dc = 256 * pi
                    L0 = lgp[:, 32 * tb0:32 * (tb0 + 1)]
                    L1 = lgp[:, 32 * tb1:32 * (tb1 + 1)]
                    vstep(lambda e, L0=L0, dc=dc: e.tensor_sub(dcf[:, dc + 48:dc + 64], L0[:, 0:16], L0[:, 16:32]))
                    vstep(lambda e, L1=L1, dc=dc: e.tensor_sub(dcf[:, dc + 64:dc + 80], L1[:, 0:16], L1[:, 16:32]))
                    vstep(lambda e, dc=dc: e.tensor_single_scalar(dcf[:, dc + 80:dc + 96], dcf[:, dc + 48:dc + 64], -1.0, OP.mult))
                    vstep(lambda e, dc=dc: e.tensor_max(dcf[:, dc + 80:dc + 96], dcf[:, dc + 80:dc + 96], dcf[:, dc + 48:dc + 64]))
                    vstep(lambda e, dc=dc: e.tensor_single_scalar(dcf[:, dc + 96:dc + 112], dcf[:, dc + 64:dc + 80], -1.0, OP.mult))
                    vmx = em.do("vector", lambda e, dc=dc: e.tensor_max(dcf[:, dc + 96:dc + 112], dcf[:, dc + 96:dc + 112], dcf[:, dc + 64:dc + 80]), inc=[(cV, 1)])
                    em.wait("scalar", cV, vmx)
                    ssg = em.do("scalar", lambda e, dc=dc: e.activation(dcf[:, dc + 112:dc + 144], dcf[:, dc + 80:dc + 112], AF.Tanh, scale=0.5), inc=[(cS, 1)])
                    em.wait("vector", cS, ssg)
                    vstep(lambda e, dc=dc: e.tensor_scalar(dcf[:, dc + 112:dc + 144], dcf[:, dc + 112:dc + 144], 0.5, 0.5, OP.mult, OP.add))
                    vstep(lambda e, dc=dc: e.tensor_single_scalar(dcf[:, dc + 144:dc + 160], dcf[:, dc + 48:dc + 64], 0.0, OP.is_ge))
                    vstep(lambda e, dc=dc: e.tensor_single_scalar(dcf[:, dc + 160:dc + 176], dcf[:, dc + 64:dc + 80], 0.0, OP.is_ge))
                    vstep(lambda e, dc=dc: e.tensor_mul(dcf[:, dc + 176:dc + 192], dcf[:, dc + 144:dc + 160], dcf[:, dc + 160:dc + 176]))
                    vstep(lambda e, dc=dc: e.tensor_add(dcf[:, dc + 144:dc + 160], dcf[:, dc + 144:dc + 160], dcf[:, dc + 160:dc + 176]))
                    vstep(lambda e, dc=dc: e.tensor_single_scalar(dcf[:, dc + 176:dc + 192], dcf[:, dc + 176:dc + 192], 2.0, OP.mult))
                    vstep(lambda e, dc=dc: e.tensor_sub(dcf[:, dc + 176:dc + 192], dcf[:, dc + 176:dc + 192], dcf[:, dc + 144:dc + 160]))
                    vstep(lambda e, dc=dc: e.tensor_single_scalar(dcf[:, dc + 176:dc + 192], dcf[:, dc + 176:dc + 192], 1.0, OP.add))
                    vstep(lambda e, dc=dc: e.tensor_mul(dcf[:, dc + 192:dc + 208], dcf[:, dc + 112:dc + 128], dcf[:, dc:dc + 16]))
                    vstep(lambda e, dc=dc: e.tensor_single_scalar(dcf[:, dc + 192:dc + 208], dcf[:, dc + 192:dc + 208], 0.125, OP.mult))
                    vstep(lambda e, dc=dc: e.tensor_mul(dcf[:, dc + 208:dc + 224], dcf[:, dc + 128:dc + 144], dcf[:, dc + 16:dc + 32]))
                    vstep(lambda e, dc=dc: e.tensor_mul(dcf[:, dc + 208:dc + 224], dcf[:, dc + 208:dc + 224], dcf[:, dc + 176:dc + 192]))
                    vstep(lambda e, dc=dc: e.tensor_single_scalar(dcf[:, dc + 208:dc + 224], dcf[:, dc + 208:dc + 224], 0.0625, OP.mult))
                    vstep(lambda e, dc=dc: e.tensor_scalar(dcf[:, dc + 240:dc + 256], dcf[:, dc + 176:dc + 192], -0.5, 1.0, OP.mult, OP.add))
                    vstep(lambda e, dc=dc: e.tensor_mul(dcf[:, dc + 224:dc + 240], dcf[:, dc + 128:dc + 144], dcf[:, dc + 32:dc + 48]))
                    vstep(lambda e, dc=dc: e.tensor_mul(dcf[:, dc + 224:dc + 240], dcf[:, dc + 224:dc + 240], dcf[:, dc + 240:dc + 256]))
                    vstep(lambda e, dc=dc: e.tensor_single_scalar(dcf[:, dc + 224:dc + 240], dcf[:, dc + 224:dc + 240], 0.125, OP.mult))
                em.wait("vector", cS, SL[s]["vg_S"])
                if s + 1 in SL and "silu_last" in SL[s + 1]:
                    em.wait("vector", cS, SL[s + 1]["silu_last"])
                outf_b.acquire(em, "vector", 0)
                r3v = r3
                bc = lambda ap: ap.broadcast_to((128, 16, 128))
                sq = SCR2[:, :]
                for pi in range(2):
                    tb0, tb1 = pi, 2 + pi
                    dc = 256 * pi
                    o0 = o2abf[:, 4096 * pi:4096 * pi + 2048]
                    o1 = o2abf[:, 4096 * pi + 2048:4096 * (pi + 1)]
                    vb0 = vb[:, 2048 * tb0:2048 * (tb0 + 1)]
                    vb1 = vb[:, 2048 * tb1:2048 * (tb1 + 1)]
                    em.do("vector", lambda e, o0=o0, vb0=vb0, dc=dc: e.tensor_tensor(
                        r3v(o0), r3v(vb0), bc(dcf[:, dc + 192:dc + 208]), OP.mult))
                    em.do("vector", lambda e, sq=sq, vb0=vb0, dc=dc: e.tensor_tensor(
                        r3v(sq), r3v(vb0), bc(dcf[:, dc + 208:dc + 224]), OP.mult))
                    em.do("vector", lambda e, o1=o1, vb1=vb1, dc=dc: e.tensor_tensor(
                        r3v(o1), r3v(vb1), bc(dcf[:, dc + 224:dc + 240]), OP.mult))
                    em.do("vector", lambda e, o1=o1, sq=sq: e.tensor_add(o1, o1, sq))
                SL[s]["v_read"] = em.do("vector", lambda e: e.engine_nop(), inc=[(cV, 1)])
                for pi in range(2):
                    tb0, tb1 = pi, 2 + pi
                    o0 = o2abf[:, 4096 * pi:4096 * pi + 2048]
                    o1 = o2abf[:, 4096 * pi + 2048:4096 * (pi + 1)]
                    vstep(lambda e, o0=o0: e.tensor_reduce(st[:, 48:64], r3(o0), AXX, OP.add))
                    vstep(lambda e, o0=o0, sq=sq: e.scalar_tensor_tensor(sq, o0, 1.0, o0, OP.bypass, OP.mult))
                    vstep(lambda e, sq=sq: e.tensor_reduce(st[:, 64:80], r3(sq), AXX, OP.add))
                    vstep(lambda e, o1=o1: e.tensor_reduce(st[:, 80:96], r3(o1), AXX, OP.add))
                    vstep(lambda e, o1=o1, sq=sq: e.scalar_tensor_tensor(sq, o1, 1.0, o1, OP.bypass, OP.mult))
                    vstep(lambda e, sq=sq: e.tensor_reduce(st[:, 96:112], r3(sq), AXX, OP.add))
                    vstep(lambda e: e.tensor_single_scalar(st[:, 48:64], st[:, 48:64], 1.0 / HDV, OP.mult))
                    vstep(lambda e: e.tensor_single_scalar(st[:, 64:80], st[:, 64:80], 1.0 / HDV, OP.mult))
                    vstep(lambda e: e.tensor_single_scalar(st[:, 80:96], st[:, 80:96], 1.0 / HDV, OP.mult))
                    vstep(lambda e: e.tensor_single_scalar(st[:, 96:112], st[:, 96:112], 1.0 / HDV, OP.mult))
                    vstep(lambda e: e.tensor_mul(st[:, 112:128], st[:, 48:64], st[:, 48:64]))
                    vstep(lambda e: e.tensor_sub(st[:, 64:80], st[:, 64:80], st[:, 112:128]))
                    vstep(lambda e: e.tensor_mul(st[:, 112:128], st[:, 80:96], st[:, 80:96]))
                    vhl = em.do("vector", lambda e: e.tensor_sub(st[:, 96:112], st[:, 96:112], st[:, 112:128]), inc=[(cV, 1)])
                    em.wait("scalar", cV, vhl)
                    em.do("scalar", lambda e: e.activation(st[:, 128:144], st[:, 64:80], AF.Sqrt, bias=epst[:, 0:1]), inc=[(cS, 1)])
                    shl2 = em.do("scalar", lambda e: e.activation(st[:, 144:160], st[:, 96:112], AF.Sqrt, bias=epst[:, 0:1]), inc=[(cS, 1)])
                    em.wait("vector", cS, shl2)
                    vstep(lambda e: e.reciprocal(st[:, 112:128], st[:, 128:144]))
                    vstep(lambda e: e.reciprocal(st[:, 128:144], st[:, 144:160]))
                    for (oo, tb, mu0, rs0) in [(o0, tb0, 48, 112), (o1, tb1, 80, 128)]:
                        usl = um_b.acquire(em, "vector", umi[0])
                        umi[0] += 1
                        uslmap[tb] = usl
                        for h in range(H):
                            hs0 = slice(128 * h, 128 * (h + 1))
                            em.do("vector", lambda e, oo=oo, h=h, hs0=hs0, mu0=mu0, rs0=rs0: e.tensor_scalar(
                                SCR[:, hs0], oo[:, hs0], st[:, mu0 + h:mu0 + h + 1], st[:, rs0 + h:rs0 + h + 1], OP.subtract, OP.mult))
                        vme = em.do("vector", lambda e, usl=usl, tb=tb: e.tensor_mul(
                            umb[:, 2048 * usl:2048 * (usl + 1)], SCR[:, :], gsb[:, 2048 * tb:2048 * (tb + 1)]), inc=[(cV, 1)])
                        vmes[tb] = vme
                SL[s]["g_read"] = vme
                SL[s]["uslmap"] = uslmap
                SL[s]["vmes"] = vmes

            def emit_mid_t(s):
                mevs = [0] * 4
                for tb in [0, 2, 1, 3]:
                    usl = SL[s]["uslmap"][tb]
                    em.wait("tensor", cV, SL[s]["vmes"][tb])
                    for h2 in range(2):
                        bk = 4 + h2
                        bankb[bk].acquire(em, "tensor", 0)
                        pv = pb[bk][:, :].bitcast(BF16)
                        for k in range(8):
                            kk = 8 * h2 + k
                            last = em.do("tensor", lambda e, pv=pv, k=k, kk=kk, usl=usl: e.transpose(
                                pv[:, 128 * k:128 * (k + 1)], umb[:, 2048 * usl + 128 * kk:2048 * usl + 128 * (kk + 1)], idb[:, :]),
                                inc=[(cT, 1)] if k == 7 else None)
                        em.wait("scalar", cT, last)
                        dst = hT[:, :].rearrange("p (a m) -> p a m", a=16)[:, 8 * h2:8 * h2 + 8, 128 * tb:128 * (tb + 1)]
                        mev = em.do("scalar", lambda e, dst=dst, pv=pv: e.copy(
                            dst, pv.rearrange("p (a m) -> p a m", a=8)), inc=[(cS, 1)])
                        bankb[bk].used_by(0, cS, mev)
                    um_b.used_by(usl, cT, last)
                    mevs[tb] = mev
                SL[s]["mevs"] = mevs

            def emit_wout(s):
                oevs = [0] * 4
                def ev_o(tt, bkk, last, gi2):
                    o2blk = o2a[:, :].bitcast(BF16)[:, 2048 * tt + 512 * gi2:2048 * tt + 512 * (gi2 + 1)]
                    em.wait("scalar", cT, last)
                    ee = em.do("scalar", lambda e, o2blk=o2blk, bkk=bkk: e.activation(
                        o2blk, pb[bkk][:, :], AF.Copy, accum_out=st[:, 160 + 4 * tt + gi2:161 + 4 * tt + gi2]), inc=[(cS, 1)])
                    bankb[bkk].used_by(0, cS, ee)
                    if gi2 == 3:
                        oevs[tt] = ee
                hTm = lambda k, tt: hT[:, :].rearrange("p (a m) -> p a m", a=16)[:, k, 128 * tt:128 * (tt + 1)]
                for gi2 in range(4):
                    run_group(6144, 512 * gi2,
                              lambda tt, bkk, last, gi2=gi2: ev_o(tt, bkk, last, gi2),
                              gi2 % 2, ttwaits=SL[s]["mevs"], srcf=hTm)
                hT_b.used_by(0, cT, cT.n)
                SL[s]["oevs"] = oevs

            def emit_fin(s):
                x1 = x1v(s)
                o2bf = o2a[:, :].bitcast(BF16)
                outf_slots = [SCR2[:, :], xn1b[:, :].bitcast(F32)[:, 0:2048]]
                vfo = 0
                for tt in range(4):
                    i = [2 * s, 2 * s + 1, 8 + 2 * s, 9 + 2 * s][tt]
                    bbi, qq = i // 8, i % 8
                    o2t = o2bf[:, 2048 * tt:2048 * (tt + 1)]
                    em.wait("vector", cS, SL[s]["oevs"][tt])
                    em.do("vector", lambda e, o2t=o2t, tt=tt: e.reduce_sum(st[:, 176 + tt:177 + tt], o2t, AXX))
                    vstep(lambda e, o2t=o2t, tt=tt: e.scalar_tensor_tensor(
                        SCR[:, :], o2t, 1.0, o2t, OP.bypass, OP.mult,
                        accum_out=st[:, 180 + tt:181 + tt]))
                    vstep(lambda e, tt=tt: e.tensor_single_scalar(st[:, 184 + tt:185 + tt], st[:, 176 + tt:177 + tt], 1.0 / C, OP.mult))
                    vstep(lambda e, tt=tt: e.tensor_single_scalar(st[:, 188 + tt:189 + tt], st[:, 180 + tt:181 + tt], 1.0 / C, OP.mult))
                    vstep(lambda e, tt=tt: e.tensor_mul(st[:, 200 + tt:201 + tt], st[:, 184 + tt:185 + tt], st[:, 184 + tt:185 + tt]))
                    vv2 = em.do("vector", lambda e, tt=tt: e.tensor_sub(st[:, 188 + tt:189 + tt], st[:, 188 + tt:189 + tt], st[:, 200 + tt:201 + tt]), inc=[(cV, 1)])
                    em.wait("scalar", cV, vv2)
                    sq2 = em.do("scalar", lambda e, tt=tt: e.activation(st[:, 192 + tt:193 + tt], st[:, 188 + tt:189 + tt], AF.Sqrt, bias=epst[:, 0:1]), inc=[(cS, 1)])
                    em.wait("vector", cS, sq2)
                    vstep(lambda e, tt=tt: e.reciprocal(st[:, 196 + tt:197 + tt], st[:, 192 + tt:193 + tt]))
                    fs = outf_b.acquire(em, "vector", tt)
                    fblk = outf_slots[fs]
                    em.do("vector", lambda e, o2t=o2t, tt=tt: e.tensor_scalar(
                        SCR[:, :], o2t, st[:, 184 + tt:185 + tt], st[:, 196 + tt:197 + tt], OP.subtract, OP.mult))
                    vfo = em.do("vector", lambda e, fblk=fblk, tt=tt, x1=x1: e.tensor_add(
                        fblk, SCR[:, :], x1[:, 2048 * tt:2048 * (tt + 1)]), inc=[(cV, 1)])
                    em.wait("sync", cV, vfo)
                    cst = em.dma("sync", yout[bbi, 128 * qq:128 * (qq + 1), :], fblk, dOF[fs])
                    outf_b.used_by(fs, dOF[fs], cst)
                x1_b.used_by(s % 2, cV, vfo)
                if DEBUG and s == 0:
                    em.wait("gpsimd", cV, cV.n)
                    em.wait("gpsimd", cS, cS.n)
                    em.dma("gpsimd", dbg["d_x1"], x1[:, :], dDBG)
                    dlast = em.dma("gpsimd", dbg["d_out2"], o2bf[:, 0:2048], dDBG)
                    em.wait("vector", dDBG, dlast)
                    em.wait("scalar", dDBG, dlast)
                    em.wait("tensor", dDBG, dlast)

            emit_B1(0)
            emit_P2(0)
            emit_P3P4(0)
            emit_logits(0)
            emit_qkvg(0)
            emit_qkvg_vg(0)
            emit_B1(1)
            for s in range(NSLAB):
                if s + 1 < NSLAB:
                    emit_P2(s + 1)
                    emit_P3P4(s + 1)
                    emit_logits(s + 1)
                emit_mid_v(s)
                if s + 1 < NSLAB:
                    emit_qkvg(s + 1)
                emit_mid_t(s)
                emit_wout(s)
                if s + 2 < NSLAB:
                    emit_B1(s + 2)
                emit_fin(s)
                if s + 1 < NSLAB:
                    emit_qkvg_vg(s + 1)
            em.wait("sync", dOF[0], dOF[0].n)
            em.wait("sync", dOF[1], dOF[1].n)
            if DEBUG:
                em.wait("gpsimd", dDBG, dDBG.n)

        with ExitStack() as stA_alloc:
            sa = lambda *a: stA_alloc.enter_context(nc.sbuf_tensor(*a))
            tA = {"b2t": sa("b2t", [128, 32], F32),
                  "xt": sa("xtA", [128, 2 * 2048], F32),
                  "xnb": sa("xnbA", [128, 2 * 2048], BF16),
                  "xlnF": sa("xlnFA", [128, 16 * 2048], BF16),
                  "sqd": sa("sqdA", [128, 2048], F32),
                  "stA": sa("stAA", [128, 128], F32),
                  "wA": sa("wAA", [128, 4 * 2048], BF16),
                  "gv": sa("gvA", [128, 2 * 4096], F32),
                  "hloc": sa("hlocA", [128, 2048], F32),
                  "ones1": sa("ones1A", [128, 1024], F32),
                  "wvstage": sa("wvstageA", [128, 2 * 2048], BF16),
                  "AS": sa("ASA", [128, 32], F32),
                  "BS": sa("BSA", [128, 32], F32),
                  "maskt": sa("masktA", [128, 8], F32),
                  "ccAll": sa("ccAllA", [128, 512], F32),
                  "hcomb": sa("hcombA", [128, 96], F32)}
            base = snap()
            with nc.Block() as blk:
                for eng in ENGS:
                    def mkA(eng):
                        def f(e):
                            restore(base)
                            sched_A(Em(eng, e), tA)
                        return f
                    getattr(blk, eng)(mkA(eng))

        with ExitStack() as stB_alloc:
            sbb = lambda *a: stB_alloc.enter_context(nc.sbuf_tensor(*a))
            tB = {"hT": sbb("hTB", [128, 4 * 2048], BF16),
                  "x1": sbb("x1B", [128, 8 * 2048], BF16),
                  "xbf": sbb("xbfB", [128, 2 * 2048], BF16),
                  "SCR": sbb("SCRB", [128, 2048], F32),
                  "SCR2": sbb("SCR2B", [128, 2048], F32),
                  "o2a": sbb("o2aB", [128, 2 * 2048], F32),
                  "xn1b": sbb("xn1bB", [128, 2 * 2048], BF16),
                  "vb": sbb("vbB", [128, 4 * 2048], BF16),
                  "gsb": sbb("gsbB", [128, 4 * 2048], BF16),
                  "umb": sbb("umbB", [128, 4 * 2048], BF16),
                  "fF": sbb("fFB", [128, 16 * 512], BF16),
                  "wS": sbb("wSB", [128, 8 * 512], BF16),
                  "qb": sbb("qbB", [128, 4 * 1024], BF16),
                  "kb": sbb("kbB", [128, 4 * 1024], BF16),
                  "lg": sbb("lgB", [128, 256], BF16),
                  "dcf": sbb("dcfB", [128, 512], F32),
                  "gvl": sbb("gvlB", [128, 2048], BF16),
                  "hFt": sbb("hFtB", [128, 1024], BF16),
                  "st": sbb("stBB", [128, 204], F32)}
            baseB = snap()
            with nc.Block() as blk:
                for eng in ENGS:
                    def mkB(eng):
                        def f(e):
                            restore(baseB)
                            sched_B(Em(eng, e), tB)
                        return f
                    getattr(blk, eng)(mkB(eng))

    return nc



def kernel(L0_attn_x, L0_attn_W, L0_attn_b, L0_ffn_conv_w, L0_ffn_Wq, L0_ffn_Wk,
           L0_ffn_Wv, L0_ffn_Wg, L0_ffn_router_w, L0_ffn_Wout):
    x = np.ascontiguousarray(np.asarray(L0_attn_x, dtype=np.float32))
    W = np.asarray(L0_attn_W, dtype=np.float32)
    bb = np.asarray(L0_attn_b, dtype=np.float32)
    conv_w = np.asarray(L0_ffn_conv_w, dtype=np.float32)
    Wq = np.ascontiguousarray(np.asarray(L0_ffn_Wq, dtype=np.float32))
    Wk = np.ascontiguousarray(np.asarray(L0_ffn_Wk, dtype=np.float32))
    Wv = np.ascontiguousarray(np.asarray(L0_ffn_Wv, dtype=np.float32))
    Wg = np.ascontiguousarray(np.asarray(L0_ffn_Wg, dtype=np.float32))
    rw = np.asarray(L0_ffn_router_w, dtype=np.float32)
    Wo = np.ascontiguousarray(np.asarray(L0_ffn_Wout, dtype=np.float32))

    wt = np.ascontiguousarray(W.T)
    b2 = np.ascontiguousarray(bb.reshape(32, 128).T)
    rbdm = np.zeros((C, 32), np.float32)
    for h in range(H):
        rbdm[HDV * h:HDV * (h + 1), h] = rw[h, :, 0]
        rbdm[HDV * h:HDV * (h + 1), 16 + h] = rw[h, :, 1]
    cw0b = np.ascontiguousarray(np.broadcast_to(conv_w[:, 0], (128, C)))
    cw1b = np.ascontiguousarray(np.broadcast_to(conv_w[:, 1], (128, C)))
    inv_freq = (1.0 / (10000.0 ** (np.arange(0, HDK, 2, dtype=np.float32) / HDK))).astype(np.float32)
    cos_row = np.cos(inv_freq).astype(np.float32)
    sin_row = np.sin(inv_freq).astype(np.float32)
    cosqm = np.zeros((128, 1024), np.float32)
    sinsm = np.zeros((128, 1024), np.float32)
    for h in range(H):
        cosqm[:, 64 * h:64 * h + 32] = cos_row
        cosqm[:, 64 * h + 32:64 * h + 64] = cos_row
        sinsm[:, 64 * h:64 * h + 32] = -sin_row
        sinsm[:, 64 * h + 32:64 * h + 64] = sin_row
    ident = np.eye(128, dtype=np.float32)

    nc = build_nc()
    in_maps = []
    for c in range(NC):
        xc = np.ascontiguousarray(x[:, TL * c:TL * (c + 1), :])
        mk = np.zeros((128, 8), np.float32)
        mk[:, :c] = 1.0
        in_maps.append({
            "x": xc, "wt": wt, "b2": b2, "wq": Wq, "wk": Wk, "wv": Wv, "wg": Wg,
            "wo": Wo, "rbd": rbdm, "cw0": cw0b, "cw1": cw1b, "cosq": cosqm,
            "sins": sinsm, "idin": ident, "maskin": mk, "epsin": np.full((128, 1), EPS, np.float32),
            "wvt": np.ascontiguousarray(Wv.T),
        })
    res = run_bass_kernel_spmd(nc, in_maps, core_ids=list(range(NC)))
    out = np.concatenate([res.results[c]["y"] for c in range(NC)], axis=1)
    if DEBUG:
        kernel.debug = res.results
    return out


# version 14
# speedup vs baseline: 1.0198x; 1.0198x over previous
import numpy as np
import concourse.bass as bass
import concourse.mybir as mybir
from concourse.bass_utils import run_bass_kernel_spmd

F32 = mybir.dt.float32
BF16 = mybir.dt.bfloat16
AF = mybir.ActivationFunctionType
OP = mybir.AluOpType
AXX = mybir.AxisListType.X

NC = 8
B = 2
TL = 1024
C = 2048
DP = 4096
H = 16
HDK = 64
HDV = 128
NT = 16
NCH = 16
EPS = 1e-5
NSLAB = 4
SPOS = 256
DEBUG = False

ENGS = ["sync", "tensor", "vector", "scalar", "gpsimd"]


class Sem:
    def __init__(self, handle):
        self.h = handle
        self.n = 0

class Buf:
    def __init__(self, nslots):
        self.waits = [[] for _ in range(nslots)]
        self.k = nslots
    def acquire(self, em, eng, idx):
        idx %= self.k
        for (s, v) in self.waits[idx]:
            em.wait(eng, s, v)
        self.waits[idx] = []
        return idx
    def used_by(self, idx, sem, val):
        self.waits[idx % self.k].append((sem, val))

class Em:
    def __init__(self, name, eng):
        self.name = name
        self.e = eng
    def do(self, name, fn, inc=None):
        if self.name == name:
            inst = fn(self.e)
            if inc is not None:
                for s, k in inc:
                    inst.then_inc(s.h, k)
        if inc is not None:
            for s, k in inc:
                s.n += k
            return inc[0][0].n
        return None
    def wait(self, name, sem, val=None):
        v = sem.n if val is None else val
        if self.name == name:
            self.e.wait_ge(sem.h, v)
        return v
    def dma(self, name, out, in_, sem):
        return self.do(name, lambda e: e.dma_start(out=out, in_=in_), inc=[(sem, 16)])


def build_nc():
    nc = bass.Bass("TRN2", target_bir_lowering=False, detect_race_conditions=False)

    xin = nc.dram_tensor("x", [B, TL, C], F32, kind="ExternalInput").ap()
    wt = nc.dram_tensor("wt", [C, DP], F32, kind="ExternalInput").ap()
    b2 = nc.dram_tensor("b2", [128, 32], F32, kind="ExternalInput").ap()
    wq = nc.dram_tensor("wq", [C, 1024], F32, kind="ExternalInput").ap()
    wk = nc.dram_tensor("wk", [C, 1024], F32, kind="ExternalInput").ap()
    wv = nc.dram_tensor("wv", [C, 2048], F32, kind="ExternalInput").ap()
    wg = nc.dram_tensor("wg", [C, 2048], F32, kind="ExternalInput").ap()
    wo = nc.dram_tensor("wo", [C, 2048], F32, kind="ExternalInput").ap()
    rbd = nc.dram_tensor("rbd", [C, 32], F32, kind="ExternalInput").ap()
    cw0 = nc.dram_tensor("cw0", [128, C], F32, kind="ExternalInput").ap()
    cw1 = nc.dram_tensor("cw1", [128, C], F32, kind="ExternalInput").ap()
    cosq = nc.dram_tensor("cosq", [128, 1024], F32, kind="ExternalInput").ap()
    sins = nc.dram_tensor("sins", [128, 1024], F32, kind="ExternalInput").ap()
    idin = nc.dram_tensor("idin", [128, 128], F32, kind="ExternalInput").ap()
    maskin = nc.dram_tensor("maskin", [128, 8], F32, kind="ExternalInput").ap()
    wvt = nc.dram_tensor("wvt", [2048, C], F32, kind="ExternalInput").ap()
    epsin = nc.dram_tensor("epsin", [128, 1], F32, kind="ExternalInput").ap()
    yout = nc.dram_tensor("y", [B, TL, C], F32, kind="ExternalOutput").ap()

    gsp = nc.dram_tensor("gsp", [NCH, 128, 2 * TL], BF16).ap()
    vsp = nc.dram_tensor("vsp", [NCH, 128, 2 * TL], BF16).ap()
    wsc = nc.dram_tensor("wsc", [NCH, 128, 8192], BF16).ap()
    cc_in = nc.dram_tensor("cc_in", [2, 128, 32], F32)
    cc_out = nc.dram_tensor("cc_out", [2 * NC, 128, 32], F32)

    dbg = {}
    if DEBUG:
        for nm, shp in [("d_carry", [128, 32]), ("d_hT", [128, 8192]),
                        ("d_x1", [128, 8192]), ("d_uF", [128, 8192]),
                        ("d_q", [128, 4096]), ("d_k", [128, 4096]),
                        ("d_lg", [128, 128]), ("d_o0", [128, 2048]),
                        ("d_o1", [128, 2048]), ("d_mF", [128, 8192]),
                        ("d_out2", [128, 2048]), ("d_cf", [128, 256]),
                        ("d_gv", [128, 4096]), ("d_AS", [128, 32]),
                        ("d_BS", [128, 32]), ("d_stA", [128, 128]),
                        ("d_xlnF", [128, 32768]), ("d_xt", [128, 2048])]:
            dbg[nm] = nc.dram_tensor(nm, shp, F32, kind="ExternalOutput").ap()

    wsrc = [(wq, 0, 1024), (wk, 1024, 1024), (wv, 2048, 2048), (wg, 4096, 2048), (wo, 6144, 2048)]

    from contextlib import ExitStack
    with ExitStack() as stack:
        sb = lambda *a: stack.enter_context(nc.sbuf_tensor(*a))
        idb = sb("idb", [128, 128], BF16)
        cw0b = sb("cw0b", [128, C], BF16)
        cw1b = sb("cw1b", [128, C], BF16)
        cosb = sb("cosb", [128, 1024], BF16)
        sinb = sb("sinb", [128, 1024], BF16)
        rbdb = sb("rbdb", [128, 512], BF16)
        wvrb = sb("wvrb", [128, 512], BF16)
        carry = sb("carry", [128, 32], F32)
        epst = sb("epst", [128, 1], F32)
        pb = [stack.enter_context(nc.psum_tensor("pb%d" % i, [128, 512], F32)) for i in range(8)]
        def mksem(nm):
            return Sem(stack.enter_context(nc.semaphore(nm)))
        dCS = mksem("dCS"); dCG = mksem("dCG"); dCA = mksem("dCA")
        dX = [mksem("dX0"), mksem("dX1")]
        dWA = [mksem("dWA0"), mksem("dWA1")]
        dSP = [mksem("dSP0"), mksem("dSP1")]
        dCC = mksem("dCC")
        dGV = [mksem("dGV0"), mksem("dGV1")]
        dWS = [mksem("dWS0"), mksem("dWS1")]
        dOF = [mksem("dOF0"), mksem("dOF1")]
        dXB = [mksem("dXB0"), mksem("dXB1"), mksem("dXB2"), mksem("dXB3")]
        dWV = [mksem("dWV0"), mksem("dWV1")]
        dDB = mksem("dDB")
        dDBG = mksem("dDBG")
        cT = mksem("cT"); cV = mksem("cV"); cS = mksem("cS"); cG = mksem("cG")
        SEMS = [dCS, dCG, dCA] + dX + dWA + dSP + [dCC] + dGV + dWS + dOF + dXB + dWV + [dDB, dDBG, cT, cV, cS, cG]
        bankb = [Buf(1) for _ in range(8)]

        state = {}

        def snap():
            return [s.n for s in SEMS]

        def restore(v):
            for s, x in zip(SEMS, v):
                s.n = x

        def sched_A(em, t):
            for bk in bankb:
                bk.waits = [[]]
            em.dma("sync", t["maskt"][:, :], maskin, dCS)
            em.dma("sync", t["b2t"][:, :], b2, dCS)
            sy_const = em.dma("sync", epst[:, :], epsin, dCS)
            em.dma("gpsimd", idb[:, :], idin, dCG)
            em.dma("gpsimd", cosb[:, :], cosq, dCG)
            em.dma("gpsimd", sinb[:, :], sins, dCG)
            em.dma("gpsimd", cw0b[:, :], cw0, dCG)
            em.dma("gpsimd", cw1b[:, :], cw1, dCG)
            gp_const = em.dma("gpsimd",
                              rbdb[:, :].rearrange("p (a m) -> p a m", a=16),
                              rbd.rearrange("(a p) m -> p a m", p=128), dCG)


            def vstep(fn):
                ev = em.do("vector", fn, inc=[(cV, 1)])
                em.wait("vector", cV, ev)
                return ev

            cast_jobs = []
            for (csrc, col0, w) in wsrc:
                for ck in range(NCH):
                    cast_jobs.append((csrc, col0, w, ck))

            def emit_casts(njobs):
                for _ in range(njobs):
                    if not cast_jobs:
                        return
                    csrc, col0, w, ck = cast_jobs.pop(0)
                    em.dma("gpsimd", wsc[ck, :, col0:col0 + w],
                           csrc[128 * ck:128 * (ck + 1), :], dCA)

            xt = t["xt"]; xnb = t["xnb"]; xlnF = t["xlnF"]; sqd = t["sqd"]
            stA = t["stA"]
            xt_b = Buf(2); xnb_b = Buf(2); wvs_b = Buf(2)
            evA1 = [0] * NT

            em.do("vector", lambda e: e.memset(t["ones1"][:, :], 1.0))
            em.wait("vector", dCS, sy_const)
            em.wait("scalar", dCS, sy_const)
            em.wait("tensor", dCG, gp_const)

            for i in range(NT):
                bbi, qq = i // 8, i % 8
                s = xt_b.acquire(em, "sync", i)
                cx = em.dma("sync", xt[:, 2048 * s:2048 * (s + 1)],
                            xin[bbi, 128 * qq:128 * (qq + 1), :], dX[s])
                xts = xt[:, 2048 * s:2048 * (s + 1)]
                em.wait("vector", dX[s], cx)
                em.do("vector", lambda e, xts=xts, i=i: e.reduce_sum(stA[:, i:i + 1], xts, AXX))
                em.do("vector", lambda e, xts=xts, i=i: e.scalar_tensor_tensor(
                    sqd[:, :], xts, 1.0, xts, OP.bypass, OP.mult, accum_out=stA[:, 16 + i:17 + i]))
                em.do("vector", lambda e, i=i: e.tensor_single_scalar(stA[:, 32 + i:33 + i], stA[:, i:i + 1], 1.0 / C, OP.mult))
                em.do("vector", lambda e, i=i: e.tensor_single_scalar(stA[:, 48 + i:49 + i], stA[:, 16 + i:17 + i], 1.0 / C, OP.mult))
                vmu2 = em.do("vector", lambda e, i=i: e.tensor_mul(stA[:, 64 + i:65 + i], stA[:, 32 + i:33 + i], stA[:, 32 + i:33 + i]), inc=[(cV, 1)])
                em.wait("vector", cV, vmu2)
                v2 = em.do("vector", lambda e, i=i: e.tensor_sub(stA[:, 80 + i:81 + i], stA[:, 48 + i:49 + i], stA[:, 64 + i:65 + i]), inc=[(cV, 1)])
                em.wait("scalar", cV, v2)
                s2 = em.do("scalar", lambda e, i=i: e.activation(stA[:, 96 + i:97 + i], stA[:, 80 + i:81 + i], AF.Sqrt, bias=epst[:, 0:1]), inc=[(cS, 1)])
                em.wait("vector", cS, s2)
                vrc = em.do("vector", lambda e, i=i: e.reciprocal(stA[:, 112 + i:113 + i], stA[:, 96 + i:97 + i]), inc=[(cV, 1)])
                em.wait("vector", cV, vrc)
                sn = xnb_b.acquire(em, "vector", i)
                xnbs = xnb[:, 2048 * sn:2048 * (sn + 1)]
                v3 = em.do("vector", lambda e, xnbs=xnbs, xts=xts, i=i: e.tensor_scalar(
                    xnbs, xts, stA[:, 32 + i:33 + i], stA[:, 112 + i:113 + i], OP.subtract, OP.mult), inc=[(cV, 1)])
                xt_b.used_by(s, cV, v3)
                em.wait("tensor", cV, v3)
                for h2 in range(2):
                    bk = 4 + h2
                    bankb[bk].acquire(em, "tensor", 0)
                    pv = pb[bk][:, :].bitcast(BF16)
                    for k in range(8):
                        kk = 8 * h2 + k
                        last = em.do("tensor", lambda e, kk=kk, k=k, pv=pv, xnbs=xnbs: e.transpose(
                            pv[:, 128 * k:128 * (k + 1)], xnbs[:, 128 * kk:128 * (kk + 1)], idb[:, :]),
                            inc=[(cT, 1)] if k == 7 else None)
                    em.wait("scalar", cT, last)
                    dst = xlnF[:, :].rearrange("p (a m) -> p a m", a=16)[:, 8 * h2:8 * h2 + 8, 128 * i:128 * (i + 1)]
                    ve = em.do("scalar", lambda e, dst=dst, pv=pv: e.copy(
                        dst, pv.rearrange("p (a m) -> p a m", a=8)), inc=[(cS, 1)])
                    bankb[bk].used_by(0, cS, ve)
                xnb_b.used_by(sn, cS, ve)
                evA1[i] = ve
                j = i
                wsl = wvs_b.acquire(em, "gpsimd", j)
                cwv = em.dma("gpsimd",
                             t["wvstage"][:, 2048 * wsl:2048 * (wsl + 1)].rearrange("p (a m) -> p a m", a=16),
                             wvt[:, 128 * j:128 * (j + 1)].rearrange("(a p) m -> p a m", p=128), dWV[wsl])
                em.wait("tensor", dWV[wsl], cwv)
                bankb[7].acquire(em, "tensor", 0)
                for k in range(NCH):
                    mmw = em.do("tensor", lambda e, k=k, wsl=wsl: e.matmul(
                        pb[7][:, 0:32],
                        t["wvstage"][:, 2048 * wsl + 128 * k:2048 * wsl + 128 * (k + 1)],
                        rbdb[:, 32 * k:32 * (k + 1)],
                        start=(k == 0), stop=(k == NCH - 1)),
                        inc=[(cT, 1)] if k == NCH - 1 else None)
                wvs_b.used_by(wsl, cT, mmw)
                em.wait("scalar", cT, mmw)
                wvre = em.do("scalar", lambda e, j=j: e.copy(
                    wvrb[:, 32 * j:32 * (j + 1)], pb[7][:, 0:32]), inc=[(cS, 1)])
                bankb[7].used_by(0, cS, wvre)
                emit_casts(2)

            if DEBUG:
                em.wait("sync", cV, cV.n)
                em.wait("sync", cS, cS.n)
                em.dma("sync", dbg["d_stA"], stA[:, :], dDB)
                dxl = em.dma("sync", dbg["d_xt"], xt[:, 2048:4096], dDB)
                em.wait("gpsimd", cV, cV.n)
                dxf = em.dma("gpsimd", dbg["d_xlnF"], xlnF[:, :], dDBG)
                em.wait("vector", dDBG, dxf)
                em.wait("vector", dDB, dxl)
                em.wait("tensor", dDBG, dxf)

            wA = t["wA"]; gv = t["gv"]; hloc = t["hloc"]
            AS = t["AS"]; BS = t["BS"]
            wA_b = Buf(2)
            gv_b = Buf(2)
            se = ve2 = 0
            pend = []

            def emit_spill():
                jd, jpd, gtd, vtd, sed = pend.pop(0)
                em.wait("gpsimd", cS, sed)
                em.dma("gpsimd", gsp[jd, :, :], gtd, dSP[jpd])
                sp2d = em.dma("gpsimd", vsp[jd, :, :], vtd, dSP[jpd])
                gv_b.used_by(jpd, dSP[jpd], sp2d)
                emit_casts(3)

            for j in range(NCH):
                sl = wA_b.acquire(em, "gpsimd", j)
                for idx in range(2):
                    d = j if idx == 0 else 16 + j
                    cwl = em.dma("gpsimd",
                                 wA[:, 4096 * sl + 2048 * idx:4096 * sl + 2048 * (idx + 1)].rearrange("p (a m) -> p a m", a=16),
                                 wt[:, 128 * d:128 * (d + 1)].rearrange("(a p) m -> p a m", p=128), dWA[sl])
                if pend:
                    emit_spill()
                jp = gv_b.acquire(em, "scalar", j)
                for idx in range(2):
                    d = j if idx == 0 else 16 + j
                    em.wait("tensor", dWA[sl], cwl)
                    gvt = gv[:, 4096 * jp + 2048 * idx: 4096 * jp + 2048 * (idx + 1)]
                    for n in range(4):
                        em.wait("tensor", cS, evA1[4 * n + 3])
                        bk = n
                        bankb[bk].acquire(em, "tensor", 0)
                        for k in range(NCH):
                            last = em.do("tensor", lambda e, k=k, bk=bk, sl=sl, n=n, idx=idx: e.matmul(
                                pb[bk][:, :],
                                wA[:, 4096 * sl + 2048 * idx + 128 * k:4096 * sl + 2048 * idx + 128 * (k + 1)],
                                xlnF[:, 2048 * k + 512 * n:2048 * k + 512 * (n + 1)],
                                start=(k == 0), stop=(k == NCH - 1)),
                                inc=[(cT, 1)] if k == NCH - 1 else None)
                        em.wait("scalar", cT, last)
                        fn = AF.Sigmoid if idx == 0 else AF.Tanh
                        se = em.do("scalar", lambda e, gvt=gvt, n=n, bk=bk, fn=fn, d=d: e.activation(
                            gvt[:, 512 * n:512 * (n + 1)], pb[bk][:, :], fn,
                            bias=t["b2t"][:, d:d + 1]), inc=[(cS, 1)])
                        bankb[bk].used_by(0, cS, se)
                    wA_b.used_by(sl, cT, last)
                gt = gv[:, 4096 * jp:4096 * jp + 2048]
                vt = gv[:, 4096 * jp + 2048:4096 * jp + 4096]
                pend.append((j, jp, gt, vt, se))
                em.wait("vector", cS, se)
                em.do("vector", lambda e, gt=gt, vt=vt: e.tensor_tensor_scan(
                    hloc[:, 0:1024], gt[:, 0:1024], vt[:, 0:1024], 0.0, OP.mult, OP.add))
                vstep(lambda e, gt=gt, vt=vt: e.tensor_tensor_scan(
                    hloc[:, 1024:2048], gt[:, 1024:2048], vt[:, 1024:2048], 0.0, OP.mult, OP.add))
                em.do("vector", lambda e, j=j: e.tensor_copy(BS[:, j:j + 1], hloc[:, 1023:1024]))
                em.do("vector", lambda e, j=j: e.tensor_copy(BS[:, 16 + j:17 + j], hloc[:, 2047:2048]))
                vstep(lambda e, gt=gt: e.tensor_tensor_scan(
                    hloc[:, 0:1024], gt[:, 0:1024], t["ones1"][:, :], 1.0, OP.mult, OP.mult))
                em.do("vector", lambda e, j=j: e.tensor_copy(AS[:, j:j + 1], hloc[:, 1023:1024]))
                vstep(lambda e, gt=gt: e.tensor_tensor_scan(
                    hloc[:, 0:1024], gt[:, 1024:2048], t["ones1"][:, :], 1.0, OP.mult, OP.mult))
                ve2 = em.do("vector", lambda e, j=j: e.tensor_copy(AS[:, 16 + j:17 + j], hloc[:, 1023:1024]), inc=[(cV, 1)])
                gv_b.used_by(jp, cV, ve2)

            while pend:
                emit_spill()
            emit_casts(100)
            em.wait("sync", cV, ve2)
            em.dma("sync", cc_in.ap()[0], AS[:, :], dCC)
            ci2 = em.dma("sync", cc_in.ap()[1], BS[:, :], dCC)
            em.wait("gpsimd", dCC, ci2)
            state["a_preex"] = snap()
            cce = em.do("gpsimd", lambda e: e.collective_compute(
                "AllGather", OP.bypass, replica_groups=[list(range(NC))],
                ins=[cc_in.ap().opt()], outs=[cc_out.ap().opt()]), inc=[(cG, 1)])
            em.wait("sync", cG, cce)
            cro = em.dma("sync", t["ccAll"][:, :].rearrange("p (a m) -> p a m", a=16),
                         cc_out.ap().rearrange("a p m -> p a m"), dCC)
            em.wait("vector", dCC, cro)
            vstep(lambda e: e.memset(t["hcomb"][:, 0:32], 0.0))
            cur = 0
            tmpc = t["hcomb"][:, 64:96]
            for jj in range(NC - 1):
                A_j = t["ccAll"][:, 64 * jj:64 * jj + 32]
                B_j = t["ccAll"][:, 64 * jj + 32:64 * jj + 64]
                Hc = t["hcomb"][:, 32 * cur:32 * cur + 32]
                Hn = t["hcomb"][:, 32 * (1 - cur):32 * (1 - cur) + 32]
                vstep(lambda e, A_j=A_j, Hc=Hc: e.tensor_mul(tmpc, A_j, Hc))
                vstep(lambda e, B_j=B_j: e.tensor_add(tmpc, tmpc, B_j))
                vstep(lambda e, Hc=Hc: e.tensor_sub(tmpc, tmpc, Hc))
                vstep(lambda e, jj=jj, Hc=Hc, Hn=Hn: e.scalar_tensor_tensor(
                    Hn, tmpc, t["maskt"][:, jj:jj + 1], Hc, OP.mult, OP.add))
                cur = 1 - cur
            vfin = em.do("vector", lambda e, cur=cur: e.tensor_copy(
                carry[:, :], t["hcomb"][:, 32 * cur:32 * cur + 32]), inc=[(cV, 1)])
            if DEBUG:
                em.wait("sync", cV, vfin)
                em.dma("sync", dbg["d_carry"], carry[:, :], dDB)
                em.dma("sync", dbg["d_gv"], gv[:, 4096:8192], dDB)
                em.dma("sync", dbg["d_AS"], AS[:, :], dDB)
                em.dma("sync", dbg["d_BS"], BS[:, :], dDB)
            state["a_end"] = snap()

        def sched_B(em, t):
            for bk in bankb:
                bk.waits = [[]]
            ae = state["a_end"]
            ape = state["a_preex"]
            for w in ENGS:
                vals = ape if w == "gpsimd" else ae
                for s_, v_ in zip(SEMS, vals):
                    em.wait(w, s_, v_)

            def vstep(fn):
                ev = em.do("vector", fn, inc=[(cV, 1)])
                em.wait("vector", cV, ev)
                return ev

            hT = t["hT"]; x1a = t["x1"]; xbf = t["xbf"]
            SCR = t["SCR"]; SCR2 = t["SCR2"]
            o2a = t["o2a"]; xn1b = t["xn1b"]; vb = t["vb"]; gsb = t["gsb"]
            umb = t["umb"]; fF = t["fF"]; wS = t["wS"]
            qb = t["qb"]; kb = t["kb"]
            qr = t["SCR"][:, :].bitcast(BF16)[:, 2048:3072]
            qt = t["SCR"][:, :].bitcast(BF16)[:, 3072:4096]
            lg = t["lg"]; dcf = t["dcf"]; gvl = t["gvl"]; hFt = t["hFt"]
            st = t["st"]

            xbf_b = Buf(2)
            gvl_b = Buf(2)
            hFt_b = Buf(2)
            um_b = Buf(4)
            wS_b = Buf(2)
            outf_b = Buf(2)
            x1_b = Buf(2)
            hT_b = Buf(1)
            quad = [[0, 1, 2, 3], [4, 5, 6, 7]]
            wgi = [0]
            xli = [0]
            SL = {}

            def x1v(s):
                return x1a[:, 8192 * (s % 2):8192 * (s % 2 + 1)]

            def fF3(k, tt):
                return fF[:, :].rearrange("p (a m) -> p a m", a=16)[:, k, 128 * tt:128 * (tt + 1)]

            def run_group(wbase, col0, evac_fn, qd, ttwaits=None, srcf=None):
                if srcf is None:
                    srcf = fF3
                def emit_wdma(bt):
                    slw = wS_b.acquire(em, "gpsimd", wgi[0])
                    wgi[0] += 1
                    cwl = em.dma("gpsimd",
                                 wS[:, 2048 * slw:2048 * (slw + 1)].rearrange("p (a m) -> p a m", a=4),
                                 wsc[4 * bt:4 * (bt + 1), :, wbase + col0:wbase + col0 + 512].rearrange("a p m -> p a m"),
                                 dWS[slw])
                    return (slw, cwl)
                ch = [emit_wdma(0), emit_wdma(1)]
                for bt in range(4):
                    slw, cwl = ch[bt]
                    em.wait("tensor", dWS[slw], cwl)
                    for kk in range(4):
                        k = 4 * bt + kk
                        for tt in range(4):
                            if k == 0:
                                if ttwaits is not None:
                                    em.wait("tensor", cS, ttwaits[tt])
                                bankb[quad[qd][tt]].acquire(em, "tensor", 0)
                            last = em.do("tensor", lambda e, k=k, kk=kk, tt=tt, qd=qd, slw=slw, srcf=srcf: e.matmul(
                                pb[quad[qd][tt]][:, :], srcf(k, tt),
                                wS[:, 2048 * slw + 512 * kk:2048 * slw + 512 * (kk + 1)],
                                start=(k == 0), stop=(k == NCH - 1)),
                                inc=[(cT, 1)] if tt == 3 else None)
                    wS_b.used_by(slw, cT, last)
                    if bt + 2 < 4:
                        ch.append(emit_wdma(bt + 2))
                for tt in range(4):
                    evac_fn(tt, quad[qd][tt], last)

            def emit_B1(s):
                hT_b.acquire(em, "scalar", 0)
                hev = 0
                for j in range(NCH):
                    sl = gvl_b.acquire(em, "sync", j)
                    em.dma("sync", gvl[:, 1024 * sl:1024 * sl + 512],
                           gsp[j].rearrange("p (b t) -> p b t", b=2)[:, :, SPOS * s:SPOS * (s + 1)], dGV[sl])
                    g2 = em.dma("sync", gvl[:, 1024 * sl + 512:1024 * (sl + 1)],
                                vsp[j].rearrange("p (b t) -> p b t", b=2)[:, :, SPOS * s:SPOS * (s + 1)], dGV[sl])
                    em.wait("vector", dGV[sl], g2)
                    hsl = hFt_b.acquire(em, "vector", j)
                    hfv = hFt[:, 512 * hsl:512 * (hsl + 1)]
                    gg = gvl[:, 1024 * sl:1024 * sl + 512]
                    vv = gvl[:, 1024 * sl + 512:1024 * (sl + 1)]
                    vstep(lambda e, gg=gg, vv=vv, hfv=hfv, j=j: e.tensor_tensor_scan(
                        hfv[:, 0:256], gg[:, 0:256], vv[:, 0:256], carry[:, j:j + 1], OP.mult, OP.add))
                    em.do("vector", lambda e, hfv=hfv, j=j: e.tensor_copy(carry[:, j:j + 1], hfv[:, 255:256]))
                    vstep(lambda e, gg=gg, vv=vv, hfv=hfv, j=j: e.tensor_tensor_scan(
                        hfv[:, 256:512], gg[:, 256:512], vv[:, 256:512], carry[:, 16 + j:17 + j], OP.mult, OP.add))
                    vsc = em.do("vector", lambda e, hfv=hfv, j=j: e.tensor_copy(
                        carry[:, 16 + j:17 + j], hfv[:, 511:512]), inc=[(cV, 1)])
                    gvl_b.used_by(sl, cV, vsc)
                    em.wait("tensor", cV, vsc)
                    pbk = 6 + j % 2
                    bankb[pbk].acquire(em, "tensor", 0)
                    pvb = pb[pbk][:, :].bitcast(BF16)
                    for tt in range(4):
                        last = em.do("tensor", lambda e, tt=tt, pvb=pvb, hfv=hfv: e.transpose(
                            pvb[:, 128 * tt:128 * (tt + 1)], hfv[:, 128 * tt:128 * (tt + 1)], idb[:, :]),
                            inc=[(cT, 1)] if tt == 3 else None)
                    hFt_b.used_by(hsl, cT, last)
                    em.wait("scalar", cT, last)
                    dst = hT[:, :].rearrange("p (a m) -> p a m", a=4)[:, :, 128 * j:128 * (j + 1)]
                    hev = em.do("scalar", lambda e, dst=dst, pvb=pvb: e.copy(
                        dst, pvb[:, 0:512].rearrange("p (a m) -> p a m", a=4)), inc=[(cS, 1)])
                    bankb[pbk].used_by(0, cS, hev)
                SL[s] = {"hev": hev}

            def emit_P2(s):
                x1 = x1v(s)
                hev = SL[s]["hev"]
                x1_b.acquire(em, "vector", s)
                for tt in range(4):
                    em.wait("vector", cS, hev)
                    hblk = hT[:, 2048 * tt:2048 * (tt + 1)]
                    em.do("vector", lambda e, hblk=hblk, tt=tt: e.reduce_sum(st[:, tt:1 + tt], hblk, AXX))
                    em.do("vector", lambda e, hblk=hblk, tt=tt: e.scalar_tensor_tensor(
                        SCR[:, :], hblk, 1.0, hblk, OP.bypass, OP.mult, accum_out=st[:, 4 + tt:5 + tt]))
                vstep(lambda e: e.engine_nop())
                vstep(lambda e: e.tensor_single_scalar(st[:, 8:12], st[:, 0:4], 1.0 / C, OP.mult))
                vstep(lambda e: e.tensor_single_scalar(st[:, 12:16], st[:, 4:8], 1.0 / C, OP.mult))
                vstep(lambda e: e.tensor_mul(st[:, 16:20], st[:, 8:12], st[:, 8:12]))
                vst = em.do("vector", lambda e: e.tensor_sub(st[:, 12:16], st[:, 12:16], st[:, 16:20]), inc=[(cV, 1)])
                em.wait("scalar", cV, vst)
                sst = em.do("scalar", lambda e: e.activation(st[:, 16:20], st[:, 12:16], AF.Sqrt, bias=epst[:, 0:1]), inc=[(cS, 1)])
                em.wait("vector", cS, sst)
                vstep(lambda e: e.reciprocal(st[:, 20:24], st[:, 16:20]))
                for tt in range(4):
                    i = [2 * s, 2 * s + 1, 8 + 2 * s, 9 + 2 * s][tt]
                    bbi, qq = i // 8, i % 8
                    sx = xbf_b.acquire(em, "gpsimd", xli[0])
                    xli[0] += 1
                    cx = em.dma("gpsimd", xbf[:, 2048 * sx:2048 * (sx + 1)],
                                xin[bbi, 128 * qq:128 * (qq + 1), :], dXB[sx])
                    em.wait("vector", dXB[sx], cx)
                    hblk = hT[:, 2048 * tt:2048 * (tt + 1)]
                    x1blk = x1[:, 2048 * tt:2048 * (tt + 1)]
                    em.do("vector", lambda e, hblk=hblk, tt=tt: e.tensor_scalar(
                        SCR[:, :], hblk, st[:, 8 + tt:9 + tt], st[:, 20 + tt:21 + tt], OP.subtract, OP.mult))
                    vx1 = em.do("vector", lambda e, x1blk=x1blk, sx=sx, tt=tt: e.scalar_tensor_tensor(
                        x1blk, SCR[:, :], 1.0, xbf[:, 2048 * sx:2048 * (sx + 1)], OP.bypass, OP.add,
                        accum_out=st[:, 24 + tt:25 + tt]), inc=[(cV, 1)])
                    xbf_b.used_by(sx, cV, vx1)
                    em.do("vector", lambda e, x1blk=x1blk, tt=tt: e.scalar_tensor_tensor(
                        SCR[:, :], x1blk, 1.0, x1blk, OP.bypass, OP.mult, accum_out=st[:, 28 + tt:29 + tt]))
                hT_b.used_by(0, cV, cV.n)
                if DEBUG and s == 0:
                    em.wait("gpsimd", cV, cV.n)
                    dh = em.dma("gpsimd", dbg["d_hT"], hT[:, :], dDBG)
                    em.wait("vector", dDBG, dh)
                vstep(lambda e: e.engine_nop())
                vstep(lambda e: e.tensor_single_scalar(st[:, 32:36], st[:, 24:28], 1.0 / C, OP.mult))
                vstep(lambda e: e.tensor_single_scalar(st[:, 36:40], st[:, 28:32], 1.0 / C, OP.mult))
                vstep(lambda e: e.tensor_mul(st[:, 40:44], st[:, 32:36], st[:, 32:36]))
                vst2 = em.do("vector", lambda e: e.tensor_sub(st[:, 36:40], st[:, 36:40], st[:, 40:44]), inc=[(cV, 1)])
                em.wait("scalar", cV, vst2)
                sst2 = em.do("scalar", lambda e: e.activation(st[:, 40:44], st[:, 36:40], AF.Sqrt, bias=epst[:, 0:1]), inc=[(cS, 1)])
                em.wait("vector", cS, sst2)
                vstep(lambda e: e.reciprocal(st[:, 44:48], st[:, 40:44]))

            def emit_P3P4(s):
                x1 = x1v(s)
                outf_b.acquire(em, "vector", 0)
                outf_b.acquire(em, "vector", 1)
                uF_evs = [0] * 4
                prev_silu = None
                for tt in range(4):
                    x1blk = x1[:, 2048 * tt:2048 * (tt + 1)]
                    if prev_silu is not None:
                        em.wait("vector", cS, prev_silu)
                    if tt < 2:
                        em.do("vector", lambda e, x1blk=x1blk, tt=tt: e.tensor_scalar(
                            xn1b[:, 2048 * tt:2048 * (tt + 1)], x1blk, st[:, 32 + tt:33 + tt], st[:, 44 + tt:45 + tt], OP.subtract, OP.mult))
                        vcv = em.do("vector", lambda e, tt=tt: e.tensor_mul(
                            SCR2[:, :], xn1b[:, 2048 * tt:2048 * (tt + 1)], cw1b[:, :]), inc=[(cV, 1)])
                    else:
                        em.do("vector", lambda e, x1blk=x1blk, tt=tt: e.tensor_scalar(
                            SCR[:, :], x1blk, st[:, 32 + tt:33 + tt], st[:, 44 + tt:45 + tt], OP.subtract, OP.mult))
                        em.do("vector", lambda e: e.tensor_mul(SCR2[:, :], SCR[:, :], cw1b[:, :]))
                        em.do("vector", lambda e, tt=tt: e.tensor_mul(
                            SCR[:, :], xn1b[:, 2048 * (tt - 2):2048 * (tt - 1)], cw0b[:, :]))
                        vcv = em.do("vector", lambda e: e.tensor_add(SCR2[:, :], SCR2[:, :], SCR[:, :]), inc=[(cV, 1)])
                    usl = um_b.acquire(em, "scalar", umi[0])
                    umi[0] += 1
                    em.wait("scalar", cV, vcv)
                    sustp = em.do("scalar", lambda e, usl=usl: e.activation(
                        umb[:, 2048 * usl:2048 * (usl + 1)], SCR2[:, :], AF.Silu), inc=[(cS, 1)])
                    prev_silu = sustp
                    em.wait("tensor", cS, sustp)
                    for h2 in range(2):
                        bk = 4 + h2
                        bankb[bk].acquire(em, "tensor", 0)
                        pv = pb[bk][:, :].bitcast(BF16)
                        for k in range(8):
                            kk = 8 * h2 + k
                            last = em.do("tensor", lambda e, pv=pv, k=k, kk=kk, usl=usl: e.transpose(
                                pv[:, 128 * k:128 * (k + 1)], umb[:, 2048 * usl + 128 * kk:2048 * usl + 128 * (kk + 1)], idb[:, :]),
                                inc=[(cT, 1)] if k == 7 else None)
                        em.wait("scalar", cT, last)
                        dst = fF[:, :].rearrange("p (a m) -> p a m", a=16)[:, 8 * h2:8 * h2 + 8, 128 * tt:128 * (tt + 1)]
                        uev = em.do("scalar", lambda e, dst=dst, pv=pv: e.copy(
                            dst, pv.rearrange("p (a m) -> p a m", a=8)), inc=[(cS, 1)])
                        bankb[bk].used_by(0, cS, uev)
                    um_b.used_by(usl, cT, last)
                    uF_evs[tt] = uev
                SL[s]["uF_evs"] = uF_evs
                SL[s]["silu_last"] = prev_silu

            def emit_logits(s):
                uF_evs = SL[s]["uF_evs"]
                lgp = lg[:, 128 * (s % 2):128 * (s % 2 + 1)]
                for tt in range(4):
                    em.wait("tensor", cS, uF_evs[tt])
                    bankb[2].acquire(em, "tensor", 0)
                    for k in range(NCH):
                        last = em.do("tensor", lambda e, k=k, tt=tt: e.matmul(
                            pb[2][:, 0:32], fF3(k, tt), wvrb[:, 32 * k:32 * (k + 1)],
                            start=(k == 0), stop=(k == NCH - 1)),
                            inc=[(cT, 1)] if k == NCH - 1 else None)
                    em.wait("scalar", cT, last)
                    lev = em.do("scalar", lambda e, tt=tt, lgp=lgp: e.copy(
                        lgp[:, 32 * tt:32 * (tt + 1)], pb[2][:, 0:32]), inc=[(cS, 1)])
                    bankb[2].used_by(0, cS, lev)
                SL[s]["lg_ev"] = lev

            def emit_qkvg(s):
                prev = SL.get(s - 1, {})
                def ev_q(tt, bkk, last, gi2):
                    dste = qb[:, 1024 * tt + 512 * gi2:1024 * tt + 512 * (gi2 + 1)]
                    em.wait("scalar", cT, last)
                    if "qk_read" in prev:
                        em.wait("scalar", cV, prev["qk_read"])
                    ee = em.do("scalar", lambda e, dste=dste, bkk=bkk: e.activation(
                        dste, pb[bkk][:, :], AF.Silu), inc=[(cS, 1)])
                    bankb[bkk].used_by(0, cS, ee)
                def ev_k(tt, bkk, last, gi2):
                    dste = kb[:, 1024 * tt + 512 * gi2:1024 * tt + 512 * (gi2 + 1)]
                    em.wait("scalar", cT, last)
                    if "qk_read" in prev:
                        em.wait("scalar", cV, prev["qk_read"])
                    ee = em.do("scalar", lambda e, dste=dste, bkk=bkk: e.copy(
                        dste, pb[bkk][:, :]), inc=[(cS, 1)])
                    bankb[bkk].used_by(0, cS, ee)
                gidx = 0
                for (wap, ngr, evf) in [(0, 2, ev_q), (1024, 2, ev_k)]:
                    for gi2 in range(ngr):
                        run_group(wap, 512 * gi2,
                                  lambda tt, bkk, last, gi2=gi2, evf=evf: evf(tt, bkk, last, gi2),
                                  gidx % 2)
                        gidx += 1
                SL[s]["qk_S"] = cS.n

            def emit_qkvg_vg(s):
                prev = SL.get(s - 1, {})
                def ev_v(tt, bkk, last, gi2):
                    dste = vb[:, 2048 * tt + 512 * gi2:2048 * tt + 512 * (gi2 + 1)]
                    em.wait("scalar", cT, last)
                    if "v_read" in prev:
                        em.wait("scalar", cV, prev["v_read"])
                    ee = em.do("scalar", lambda e, dste=dste, bkk=bkk: e.copy(
                        dste, pb[bkk][:, :]), inc=[(cS, 1)])
                    bankb[bkk].used_by(0, cS, ee)
                def ev_g(tt, bkk, last, gi2):
                    dste = gsb[:, 2048 * tt + 512 * gi2:2048 * tt + 512 * (gi2 + 1)]
                    em.wait("scalar", cT, last)
                    if "g_read" in prev:
                        em.wait("scalar", cV, prev["g_read"])
                    ee = em.do("scalar", lambda e, dste=dste, bkk=bkk: e.activation(
                        dste, pb[bkk][:, :], AF.Silu), inc=[(cS, 1)])
                    bankb[bkk].used_by(0, cS, ee)
                gidx = 4
                for (wap, ngr, evf) in [(2048, 4, ev_v), (4096, 4, ev_g)]:
                    for gi2 in range(ngr):
                        run_group(wap, 512 * gi2,
                                  lambda tt, bkk, last, gi2=gi2, evf=evf: evf(tt, bkk, last, gi2),
                                  gidx % 2)
                        gidx += 1
                SL[s]["vg_S"] = cS.n
                if DEBUG and s == 0:
                    em.wait("gpsimd", cS, SL[s]["qk_S"])
                    em.dma("gpsimd", dbg["d_uF"], fF[:, :], dDBG)
                    em.dma("gpsimd", dbg["d_q"], qb[:, :], dDBG)
                    duk = em.dma("gpsimd", dbg["d_k"], kb[:, :], dDBG)
                    dlg = em.dma("gpsimd", dbg["d_lg"], lg[:, 0:128], dDBG)
                    em.wait("tensor", dDBG, dlg)

            umi = [0]

            def emit_mid_v(s):
                mevs = [0] * 4
                uslmap = {}
                vmes = {}
                o2abf = o2a[:, :].bitcast(BF16)
                lgp = lg[:, 128 * (s % 2):128 * (s % 2 + 1)]
                r32 = lambda ap: ap.rearrange("p (h two d) -> p h two d", two=2, d=32)
                r3 = lambda ap: ap.rearrange("p (h d) -> p h d", h=16)
                em.wait("vector", cS, SL[s]["qk_S"])
                for pi in range(2):
                    tb0, tb1 = pi, 2 + pi
                    dc = 256 * pi
                    qsrc = qb[:, 1024 * tb1:1024 * (tb1 + 1)]
                    em.do("vector", lambda e, qsrc=qsrc: e.tensor_mul(qr, qsrc, cosb[:, :]))
                    em.do("vector", lambda e, qsrc=qsrc: e.tensor_mul(
                        r32(qt)[:, :, 0, :], r32(qsrc)[:, :, 1, :], r32(sinb[:, :])[:, :, 0, :]))
                    vstep(lambda e, qsrc=qsrc: e.tensor_mul(
                        r32(qt)[:, :, 1, :], r32(qsrc)[:, :, 0, :], r32(sinb[:, :])[:, :, 1, :]))
                    em.do("vector", lambda e: e.tensor_add(qr, qr, qt))
                    DOT = SCR
                    em.do("vector", lambda e, tb0=tb0: e.tensor_mul(
                        DOT[:, 0:1024], qb[:, 1024 * tb0:1024 * (tb0 + 1)], kb[:, 1024 * tb0:1024 * (tb0 + 1)]))
                    em.do("vector", lambda e, dc=dc: e.tensor_reduce(dcf[:, dc:dc + 16], r3(DOT[:, 0:1024]), AXX, OP.add))
                    em.do("vector", lambda e, tb0=tb0: e.tensor_mul(
                        DOT[:, 0:1024], qr, kb[:, 1024 * tb0:1024 * (tb0 + 1)]))
                    em.do("vector", lambda e, dc=dc: e.tensor_reduce(dcf[:, dc + 16:dc + 32], r3(DOT[:, 0:1024]), AXX, OP.add))
                    em.do("vector", lambda e, tb1=tb1: e.tensor_mul(
                        DOT[:, 0:1024], qb[:, 1024 * tb1:1024 * (tb1 + 1)], kb[:, 1024 * tb1:1024 * (tb1 + 1)]))
                    em.do("vector", lambda e, dc=dc: e.tensor_reduce(dcf[:, dc + 32:dc + 48], r3(DOT[:, 0:1024]), AXX, OP.add))
                SL[s]["qk_read"] = em.do("vector", lambda e: e.engine_nop(), inc=[(cV, 1)])
                for pi in range(2):
                    tb0, tb1 = pi, 2 + pi
                    dc = 256 * pi
                    L0 = lgp[:, 32 * tb0:32 * (tb0 + 1)]
                    L1 = lgp[:, 32 * tb1:32 * (tb1 + 1)]
                    vstep(lambda e, L0=L0, dc=dc: e.tensor_sub(dcf[:, dc + 48:dc + 64], L0[:, 0:16], L0[:, 16:32]))
                    vstep(lambda e, L1=L1, dc=dc: e.tensor_sub(dcf[:, dc + 64:dc + 80], L1[:, 0:16], L1[:, 16:32]))
                    vstep(lambda e, dc=dc: e.tensor_single_scalar(dcf[:, dc + 80:dc + 96], dcf[:, dc + 48:dc + 64], -1.0, OP.mult))
                    vstep(lambda e, dc=dc: e.tensor_max(dcf[:, dc + 80:dc + 96], dcf[:, dc + 80:dc + 96], dcf[:, dc + 48:dc + 64]))
                    vstep(lambda e, dc=dc: e.tensor_single_scalar(dcf[:, dc + 96:dc + 112], dcf[:, dc + 64:dc + 80], -1.0, OP.mult))
                    vmx = em.do("vector", lambda e, dc=dc: e.tensor_max(dcf[:, dc + 96:dc + 112], dcf[:, dc + 96:dc + 112], dcf[:, dc + 64:dc + 80]), inc=[(cV, 1)])
                    em.wait("scalar", cV, vmx)
                    ssg = em.do("scalar", lambda e, dc=dc: e.activation(dcf[:, dc + 112:dc + 144], dcf[:, dc + 80:dc + 112], AF.Tanh, scale=0.5), inc=[(cS, 1)])
                    em.wait("vector", cS, ssg)
                    vstep(lambda e, dc=dc: e.tensor_scalar(dcf[:, dc + 112:dc + 144], dcf[:, dc + 112:dc + 144], 0.5, 0.5, OP.mult, OP.add))
                    vstep(lambda e, dc=dc: e.tensor_single_scalar(dcf[:, dc + 144:dc + 160], dcf[:, dc + 48:dc + 64], 0.0, OP.is_ge))
                    vstep(lambda e, dc=dc: e.tensor_single_scalar(dcf[:, dc + 160:dc + 176], dcf[:, dc + 64:dc + 80], 0.0, OP.is_ge))
                    vstep(lambda e, dc=dc: e.tensor_mul(dcf[:, dc + 176:dc + 192], dcf[:, dc + 144:dc + 160], dcf[:, dc + 160:dc + 176]))
                    vstep(lambda e, dc=dc: e.tensor_add(dcf[:, dc + 144:dc + 160], dcf[:, dc + 144:dc + 160], dcf[:, dc + 160:dc + 176]))
                    vstep(lambda e, dc=dc: e.tensor_single_scalar(dcf[:, dc + 176:dc + 192], dcf[:, dc + 176:dc + 192], 2.0, OP.mult))
                    vstep(lambda e, dc=dc: e.tensor_sub(dcf[:, dc + 176:dc + 192], dcf[:, dc + 176:dc + 192], dcf[:, dc + 144:dc + 160]))
                    vstep(lambda e, dc=dc: e.tensor_single_scalar(dcf[:, dc + 176:dc + 192], dcf[:, dc + 176:dc + 192], 1.0, OP.add))
                    vstep(lambda e, dc=dc: e.tensor_mul(dcf[:, dc + 192:dc + 208], dcf[:, dc + 112:dc + 128], dcf[:, dc:dc + 16]))
                    vstep(lambda e, dc=dc: e.tensor_single_scalar(dcf[:, dc + 192:dc + 208], dcf[:, dc + 192:dc + 208], 0.125, OP.mult))
                    vstep(lambda e, dc=dc: e.tensor_mul(dcf[:, dc + 208:dc + 224], dcf[:, dc + 128:dc + 144], dcf[:, dc + 16:dc + 32]))
                    vstep(lambda e, dc=dc: e.tensor_mul(dcf[:, dc + 208:dc + 224], dcf[:, dc + 208:dc + 224], dcf[:, dc + 176:dc + 192]))
                    vstep(lambda e, dc=dc: e.tensor_single_scalar(dcf[:, dc + 208:dc + 224], dcf[:, dc + 208:dc + 224], 0.0625, OP.mult))
                    vstep(lambda e, dc=dc: e.tensor_scalar(dcf[:, dc + 240:dc + 256], dcf[:, dc + 176:dc + 192], -0.5, 1.0, OP.mult, OP.add))
                    vstep(lambda e, dc=dc: e.tensor_mul(dcf[:, dc + 224:dc + 240], dcf[:, dc + 128:dc + 144], dcf[:, dc + 32:dc + 48]))
                    vstep(lambda e, dc=dc: e.tensor_mul(dcf[:, dc + 224:dc + 240], dcf[:, dc + 224:dc + 240], dcf[:, dc + 240:dc + 256]))
                    vstep(lambda e, dc=dc: e.tensor_single_scalar(dcf[:, dc + 224:dc + 240], dcf[:, dc + 224:dc + 240], 0.125, OP.mult))
                em.wait("vector", cS, SL[s]["vg_S"])
                if s + 1 in SL and "silu_last" in SL[s + 1]:
                    em.wait("vector", cS, SL[s + 1]["silu_last"])
                outf_b.acquire(em, "vector", 0)
                r3v = r3
                bc = lambda ap: ap.broadcast_to((128, 16, 128))
                sq = SCR2[:, :]
                for pi in range(2):
                    tb0, tb1 = pi, 2 + pi
                    dc = 256 * pi
                    o0 = o2abf[:, 4096 * pi:4096 * pi + 2048]
                    o1 = o2abf[:, 4096 * pi + 2048:4096 * (pi + 1)]
                    vb0 = vb[:, 2048 * tb0:2048 * (tb0 + 1)]
                    vb1 = vb[:, 2048 * tb1:2048 * (tb1 + 1)]
                    em.do("vector", lambda e, o0=o0, vb0=vb0, dc=dc: e.tensor_tensor(
                        r3v(o0), r3v(vb0), bc(dcf[:, dc + 192:dc + 208]), OP.mult))
                    em.do("vector", lambda e, sq=sq, vb0=vb0, dc=dc: e.tensor_tensor(
                        r3v(sq), r3v(vb0), bc(dcf[:, dc + 208:dc + 224]), OP.mult))
                    em.do("vector", lambda e, o1=o1, vb1=vb1, dc=dc: e.tensor_tensor(
                        r3v(o1), r3v(vb1), bc(dcf[:, dc + 224:dc + 240]), OP.mult))
                    em.do("vector", lambda e, o1=o1, sq=sq: e.tensor_add(o1, o1, sq))
                SL[s]["v_read"] = em.do("vector", lambda e: e.engine_nop(), inc=[(cV, 1)])
                for pi in range(2):
                    tb0, tb1 = pi, 2 + pi
                    o0 = o2abf[:, 4096 * pi:4096 * pi + 2048]
                    o1 = o2abf[:, 4096 * pi + 2048:4096 * (pi + 1)]
                    vstep(lambda e, o0=o0: e.tensor_reduce(st[:, 48:64], r3(o0), AXX, OP.add))
                    vstep(lambda e, o0=o0, sq=sq: e.scalar_tensor_tensor(sq, o0, 1.0, o0, OP.bypass, OP.mult))
                    vstep(lambda e, sq=sq: e.tensor_reduce(st[:, 64:80], r3(sq), AXX, OP.add))
                    vstep(lambda e, o1=o1: e.tensor_reduce(st[:, 80:96], r3(o1), AXX, OP.add))
                    vstep(lambda e, o1=o1, sq=sq: e.scalar_tensor_tensor(sq, o1, 1.0, o1, OP.bypass, OP.mult))
                    vstep(lambda e, sq=sq: e.tensor_reduce(st[:, 96:112], r3(sq), AXX, OP.add))
                    vstep(lambda e: e.tensor_single_scalar(st[:, 48:64], st[:, 48:64], 1.0 / HDV, OP.mult))
                    vstep(lambda e: e.tensor_single_scalar(st[:, 64:80], st[:, 64:80], 1.0 / HDV, OP.mult))
                    vstep(lambda e: e.tensor_single_scalar(st[:, 80:96], st[:, 80:96], 1.0 / HDV, OP.mult))
                    vstep(lambda e: e.tensor_single_scalar(st[:, 96:112], st[:, 96:112], 1.0 / HDV, OP.mult))
                    vstep(lambda e: e.tensor_mul(st[:, 112:128], st[:, 48:64], st[:, 48:64]))
                    vstep(lambda e: e.tensor_sub(st[:, 64:80], st[:, 64:80], st[:, 112:128]))
                    vstep(lambda e: e.tensor_mul(st[:, 112:128], st[:, 80:96], st[:, 80:96]))
                    vhl = em.do("vector", lambda e: e.tensor_sub(st[:, 96:112], st[:, 96:112], st[:, 112:128]), inc=[(cV, 1)])
                    em.wait("scalar", cV, vhl)
                    em.do("scalar", lambda e: e.activation(st[:, 128:144], st[:, 64:80], AF.Sqrt, bias=epst[:, 0:1]), inc=[(cS, 1)])
                    shl2 = em.do("scalar", lambda e: e.activation(st[:, 144:160], st[:, 96:112], AF.Sqrt, bias=epst[:, 0:1]), inc=[(cS, 1)])
                    em.wait("vector", cS, shl2)
                    vstep(lambda e: e.reciprocal(st[:, 112:128], st[:, 128:144]))
                    vstep(lambda e: e.reciprocal(st[:, 128:144], st[:, 144:160]))
                    for (oo, tb, mu0, rs0) in [(o0, tb0, 48, 112), (o1, tb1, 80, 128)]:
                        usl = um_b.acquire(em, "vector", umi[0])
                        umi[0] += 1
                        uslmap[tb] = usl
                        for h in range(H):
                            hs0 = slice(128 * h, 128 * (h + 1))
                            em.do("vector", lambda e, oo=oo, h=h, hs0=hs0, mu0=mu0, rs0=rs0: e.tensor_scalar(
                                SCR[:, hs0], oo[:, hs0], st[:, mu0 + h:mu0 + h + 1], st[:, rs0 + h:rs0 + h + 1], OP.subtract, OP.mult))
                        vme = em.do("vector", lambda e, usl=usl, tb=tb: e.tensor_mul(
                            umb[:, 2048 * usl:2048 * (usl + 1)], SCR[:, :], gsb[:, 2048 * tb:2048 * (tb + 1)]), inc=[(cV, 1)])
                        vmes[tb] = vme
                SL[s]["g_read"] = vme
                SL[s]["uslmap"] = uslmap
                SL[s]["vmes"] = vmes

            def emit_mid_t(s):
                mevs = [0] * 4
                for tb in [0, 2, 1, 3]:
                    usl = SL[s]["uslmap"][tb]
                    em.wait("tensor", cV, SL[s]["vmes"][tb])
                    for h2 in range(2):
                        bk = 4 + h2
                        bankb[bk].acquire(em, "tensor", 0)
                        pv = pb[bk][:, :].bitcast(BF16)
                        for k in range(8):
                            kk = 8 * h2 + k
                            last = em.do("tensor", lambda e, pv=pv, k=k, kk=kk, usl=usl: e.transpose(
                                pv[:, 128 * k:128 * (k + 1)], umb[:, 2048 * usl + 128 * kk:2048 * usl + 128 * (kk + 1)], idb[:, :]),
                                inc=[(cT, 1)] if k == 7 else None)
                        em.wait("scalar", cT, last)
                        dst = hT[:, :].rearrange("p (a m) -> p a m", a=16)[:, 8 * h2:8 * h2 + 8, 128 * tb:128 * (tb + 1)]
                        mev = em.do("scalar", lambda e, dst=dst, pv=pv: e.copy(
                            dst, pv.rearrange("p (a m) -> p a m", a=8)), inc=[(cS, 1)])
                        bankb[bk].used_by(0, cS, mev)
                    um_b.used_by(usl, cT, last)
                    mevs[tb] = mev
                SL[s]["mevs"] = mevs

            def emit_wout(s):
                oevs = [0] * 4
                def ev_o(tt, bkk, last, gi2):
                    o2blk = o2a[:, :].bitcast(BF16)[:, 2048 * tt + 512 * gi2:2048 * tt + 512 * (gi2 + 1)]
                    em.wait("scalar", cT, last)
                    ee = em.do("scalar", lambda e, o2blk=o2blk, bkk=bkk: e.activation(
                        o2blk, pb[bkk][:, :], AF.Copy, accum_out=st[:, 160 + 4 * tt + gi2:161 + 4 * tt + gi2]), inc=[(cS, 1)])
                    bankb[bkk].used_by(0, cS, ee)
                    if gi2 == 3:
                        oevs[tt] = ee
                hTm = lambda k, tt: hT[:, :].rearrange("p (a m) -> p a m", a=16)[:, k, 128 * tt:128 * (tt + 1)]
                for gi2 in range(4):
                    run_group(6144, 512 * gi2,
                              lambda tt, bkk, last, gi2=gi2: ev_o(tt, bkk, last, gi2),
                              gi2 % 2, ttwaits=SL[s]["mevs"], srcf=hTm)
                hT_b.used_by(0, cT, cT.n)
                SL[s]["oevs"] = oevs

            def emit_fin(s):
                x1 = x1v(s)
                o2bf = o2a[:, :].bitcast(BF16)
                outf_slots = [SCR2[:, :], xn1b[:, :].bitcast(F32)[:, 0:2048]]
                vfo = 0
                for tt in range(4):
                    i = [2 * s, 2 * s + 1, 8 + 2 * s, 9 + 2 * s][tt]
                    bbi, qq = i // 8, i % 8
                    o2t = o2bf[:, 2048 * tt:2048 * (tt + 1)]
                    em.wait("vector", cS, SL[s]["oevs"][tt])
                    em.do("vector", lambda e, o2t=o2t, tt=tt: e.reduce_sum(st[:, 176 + tt:177 + tt], o2t, AXX))
                    vstep(lambda e, o2t=o2t, tt=tt: e.scalar_tensor_tensor(
                        SCR[:, :], o2t, 1.0, o2t, OP.bypass, OP.mult,
                        accum_out=st[:, 180 + tt:181 + tt]))
                    vstep(lambda e, tt=tt: e.tensor_single_scalar(st[:, 184 + tt:185 + tt], st[:, 176 + tt:177 + tt], 1.0 / C, OP.mult))
                    vstep(lambda e, tt=tt: e.tensor_single_scalar(st[:, 188 + tt:189 + tt], st[:, 180 + tt:181 + tt], 1.0 / C, OP.mult))
                    vstep(lambda e, tt=tt: e.tensor_mul(st[:, 200 + tt:201 + tt], st[:, 184 + tt:185 + tt], st[:, 184 + tt:185 + tt]))
                    vv2 = em.do("vector", lambda e, tt=tt: e.tensor_sub(st[:, 188 + tt:189 + tt], st[:, 188 + tt:189 + tt], st[:, 200 + tt:201 + tt]), inc=[(cV, 1)])
                    em.wait("scalar", cV, vv2)
                    sq2 = em.do("scalar", lambda e, tt=tt: e.activation(st[:, 192 + tt:193 + tt], st[:, 188 + tt:189 + tt], AF.Sqrt, bias=epst[:, 0:1]), inc=[(cS, 1)])
                    em.wait("vector", cS, sq2)
                    vstep(lambda e, tt=tt: e.reciprocal(st[:, 196 + tt:197 + tt], st[:, 192 + tt:193 + tt]))
                    fs = outf_b.acquire(em, "vector", tt)
                    fblk = outf_slots[fs]
                    em.do("vector", lambda e, o2t=o2t, tt=tt: e.tensor_scalar(
                        SCR[:, :], o2t, st[:, 184 + tt:185 + tt], st[:, 196 + tt:197 + tt], OP.subtract, OP.mult))
                    vfo = em.do("vector", lambda e, fblk=fblk, tt=tt, x1=x1: e.tensor_add(
                        fblk, SCR[:, :], x1[:, 2048 * tt:2048 * (tt + 1)]), inc=[(cV, 1)])
                    em.wait("sync", cV, vfo)
                    cst = em.dma("sync", yout[bbi, 128 * qq:128 * (qq + 1), :], fblk, dOF[fs])
                    outf_b.used_by(fs, dOF[fs], cst)
                x1_b.used_by(s % 2, cV, vfo)
                if DEBUG and s == 0:
                    em.wait("gpsimd", cV, cV.n)
                    em.wait("gpsimd", cS, cS.n)
                    em.dma("gpsimd", dbg["d_x1"], x1[:, :], dDBG)
                    dlast = em.dma("gpsimd", dbg["d_out2"], o2bf[:, 0:2048], dDBG)
                    em.wait("vector", dDBG, dlast)
                    em.wait("scalar", dDBG, dlast)
                    em.wait("tensor", dDBG, dlast)

            emit_B1(0)
            emit_P2(0)
            emit_P3P4(0)
            emit_logits(0)
            emit_qkvg(0)
            emit_qkvg_vg(0)
            emit_B1(1)
            for s in range(NSLAB):
                if s + 1 < NSLAB:
                    emit_P2(s + 1)
                    emit_P3P4(s + 1)
                    emit_logits(s + 1)
                emit_mid_v(s)
                if s + 1 < NSLAB:
                    emit_qkvg(s + 1)
                    emit_qkvg_vg(s + 1)
                emit_mid_t(s)
                emit_wout(s)
                if s + 2 < NSLAB:
                    emit_B1(s + 2)
                emit_fin(s)
            em.wait("sync", dOF[0], dOF[0].n)
            em.wait("sync", dOF[1], dOF[1].n)
            if DEBUG:
                em.wait("gpsimd", dDBG, dDBG.n)

        with ExitStack() as stA_alloc:
            sa = lambda *a: stA_alloc.enter_context(nc.sbuf_tensor(*a))
            tA = {"b2t": sa("b2t", [128, 32], F32),
                  "xt": sa("xtA", [128, 2 * 2048], F32),
                  "xnb": sa("xnbA", [128, 2 * 2048], BF16),
                  "xlnF": sa("xlnFA", [128, 16 * 2048], BF16),
                  "sqd": sa("sqdA", [128, 2048], F32),
                  "stA": sa("stAA", [128, 128], F32),
                  "wA": sa("wAA", [128, 4 * 2048], BF16),
                  "gv": sa("gvA", [128, 2 * 4096], F32),
                  "hloc": sa("hlocA", [128, 2048], F32),
                  "ones1": sa("ones1A", [128, 1024], F32),
                  "wvstage": sa("wvstageA", [128, 2 * 2048], BF16),
                  "AS": sa("ASA", [128, 32], F32),
                  "BS": sa("BSA", [128, 32], F32),
                  "maskt": sa("masktA", [128, 8], F32),
                  "ccAll": sa("ccAllA", [128, 512], F32),
                  "hcomb": sa("hcombA", [128, 96], F32)}
            base = snap()
            with nc.Block() as blk:
                for eng in ENGS:
                    def mkA(eng):
                        def f(e):
                            restore(base)
                            sched_A(Em(eng, e), tA)
                        return f
                    getattr(blk, eng)(mkA(eng))

        with ExitStack() as stB_alloc:
            sbb = lambda *a: stB_alloc.enter_context(nc.sbuf_tensor(*a))
            tB = {"hT": sbb("hTB", [128, 4 * 2048], BF16),
                  "x1": sbb("x1B", [128, 8 * 2048], BF16),
                  "xbf": sbb("xbfB", [128, 2 * 2048], BF16),
                  "SCR": sbb("SCRB", [128, 2048], F32),
                  "SCR2": sbb("SCR2B", [128, 2048], F32),
                  "o2a": sbb("o2aB", [128, 2 * 2048], F32),
                  "xn1b": sbb("xn1bB", [128, 2 * 2048], BF16),
                  "vb": sbb("vbB", [128, 4 * 2048], BF16),
                  "gsb": sbb("gsbB", [128, 4 * 2048], BF16),
                  "umb": sbb("umbB", [128, 4 * 2048], BF16),
                  "fF": sbb("fFB", [128, 16 * 512], BF16),
                  "wS": sbb("wSB", [128, 8 * 512], BF16),
                  "qb": sbb("qbB", [128, 4 * 1024], BF16),
                  "kb": sbb("kbB", [128, 4 * 1024], BF16),
                  "lg": sbb("lgB", [128, 256], BF16),
                  "dcf": sbb("dcfB", [128, 512], F32),
                  "gvl": sbb("gvlB", [128, 2048], BF16),
                  "hFt": sbb("hFtB", [128, 1024], BF16),
                  "st": sbb("stBB", [128, 204], F32)}
            baseB = snap()
            with nc.Block() as blk:
                for eng in ENGS:
                    def mkB(eng):
                        def f(e):
                            restore(baseB)
                            sched_B(Em(eng, e), tB)
                        return f
                    getattr(blk, eng)(mkB(eng))

    return nc



def kernel(L0_attn_x, L0_attn_W, L0_attn_b, L0_ffn_conv_w, L0_ffn_Wq, L0_ffn_Wk,
           L0_ffn_Wv, L0_ffn_Wg, L0_ffn_router_w, L0_ffn_Wout):
    x = np.ascontiguousarray(np.asarray(L0_attn_x, dtype=np.float32))
    W = np.asarray(L0_attn_W, dtype=np.float32)
    bb = np.asarray(L0_attn_b, dtype=np.float32)
    conv_w = np.asarray(L0_ffn_conv_w, dtype=np.float32)
    Wq = np.ascontiguousarray(np.asarray(L0_ffn_Wq, dtype=np.float32))
    Wk = np.ascontiguousarray(np.asarray(L0_ffn_Wk, dtype=np.float32))
    Wv = np.ascontiguousarray(np.asarray(L0_ffn_Wv, dtype=np.float32))
    Wg = np.ascontiguousarray(np.asarray(L0_ffn_Wg, dtype=np.float32))
    rw = np.asarray(L0_ffn_router_w, dtype=np.float32)
    Wo = np.ascontiguousarray(np.asarray(L0_ffn_Wout, dtype=np.float32))

    wt = np.ascontiguousarray(W.T)
    b2 = np.ascontiguousarray(bb.reshape(32, 128).T)
    rbdm = np.zeros((C, 32), np.float32)
    for h in range(H):
        rbdm[HDV * h:HDV * (h + 1), h] = rw[h, :, 0]
        rbdm[HDV * h:HDV * (h + 1), 16 + h] = rw[h, :, 1]
    cw0b = np.ascontiguousarray(np.broadcast_to(conv_w[:, 0], (128, C)))
    cw1b = np.ascontiguousarray(np.broadcast_to(conv_w[:, 1], (128, C)))
    inv_freq = (1.0 / (10000.0 ** (np.arange(0, HDK, 2, dtype=np.float32) / HDK))).astype(np.float32)
    cos_row = np.cos(inv_freq).astype(np.float32)
    sin_row = np.sin(inv_freq).astype(np.float32)
    cosqm = np.zeros((128, 1024), np.float32)
    sinsm = np.zeros((128, 1024), np.float32)
    for h in range(H):
        cosqm[:, 64 * h:64 * h + 32] = cos_row
        cosqm[:, 64 * h + 32:64 * h + 64] = cos_row
        sinsm[:, 64 * h:64 * h + 32] = -sin_row
        sinsm[:, 64 * h + 32:64 * h + 64] = sin_row
    ident = np.eye(128, dtype=np.float32)

    nc = build_nc()
    in_maps = []
    for c in range(NC):
        xc = np.ascontiguousarray(x[:, TL * c:TL * (c + 1), :])
        mk = np.zeros((128, 8), np.float32)
        mk[:, :c] = 1.0
        in_maps.append({
            "x": xc, "wt": wt, "b2": b2, "wq": Wq, "wk": Wk, "wv": Wv, "wg": Wg,
            "wo": Wo, "rbd": rbdm, "cw0": cw0b, "cw1": cw1b, "cosq": cosqm,
            "sins": sinsm, "idin": ident, "maskin": mk, "epsin": np.full((128, 1), EPS, np.float32),
            "wvt": np.ascontiguousarray(Wv.T),
        })
    res = run_bass_kernel_spmd(nc, in_maps, core_ids=list(range(NC)))
    out = np.concatenate([res.results[c]["y"] for c in range(NC)], axis=1)
    if DEBUG:
        kernel.debug = res.results
    return out


# version 15
# speedup vs baseline: 1.0607x; 1.0401x over previous
import numpy as np
import concourse.bass as bass
import concourse.mybir as mybir
from concourse.bass_utils import run_bass_kernel_spmd

F32 = mybir.dt.float32
BF16 = mybir.dt.bfloat16
AF = mybir.ActivationFunctionType
OP = mybir.AluOpType
AXX = mybir.AxisListType.X

NC = 8
B = 2
TL = 1024
C = 2048
DP = 4096
H = 16
HDK = 64
HDV = 128
NT = 16
NCH = 16
EPS = 1e-5
NSLAB = 4
SPOS = 256
DEBUG = False

ENGS = ["sync", "tensor", "vector", "scalar", "gpsimd"]


class Sem:
    def __init__(self, handle):
        self.h = handle
        self.n = 0

class Buf:
    def __init__(self, nslots):
        self.waits = [[] for _ in range(nslots)]
        self.k = nslots
    def acquire(self, em, eng, idx):
        idx %= self.k
        for (s, v) in self.waits[idx]:
            em.wait(eng, s, v)
        self.waits[idx] = []
        return idx
    def used_by(self, idx, sem, val):
        self.waits[idx % self.k].append((sem, val))

class Em:
    def __init__(self, name, eng):
        self.name = name
        self.e = eng
    def do(self, name, fn, inc=None):
        if self.name == name:
            inst = fn(self.e)
            if inc is not None:
                for s, k in inc:
                    inst.then_inc(s.h, k)
        if inc is not None:
            for s, k in inc:
                s.n += k
            return inc[0][0].n
        return None
    def wait(self, name, sem, val=None):
        v = sem.n if val is None else val
        if self.name == name:
            self.e.wait_ge(sem.h, v)
        return v
    def dma(self, name, out, in_, sem):
        return self.do(name, lambda e: e.dma_start(out=out, in_=in_), inc=[(sem, 16)])


def build_nc():
    nc = bass.Bass("TRN2", target_bir_lowering=False, detect_race_conditions=False)

    xin = nc.dram_tensor("x", [B, TL, C], F32, kind="ExternalInput").ap()
    wt = nc.dram_tensor("wt", [C, DP], F32, kind="ExternalInput").ap()
    b2 = nc.dram_tensor("b2", [128, 32], F32, kind="ExternalInput").ap()
    wq = nc.dram_tensor("wq", [C, 1024], F32, kind="ExternalInput").ap()
    wk = nc.dram_tensor("wk", [C, 1024], F32, kind="ExternalInput").ap()
    wv = nc.dram_tensor("wv", [C, 2048], F32, kind="ExternalInput").ap()
    wg = nc.dram_tensor("wg", [C, 2048], F32, kind="ExternalInput").ap()
    wo = nc.dram_tensor("wo", [C, 2048], F32, kind="ExternalInput").ap()
    rbd = nc.dram_tensor("rbd", [C, 32], F32, kind="ExternalInput").ap()
    cw0 = nc.dram_tensor("cw0", [128, C], F32, kind="ExternalInput").ap()
    cw1 = nc.dram_tensor("cw1", [128, C], F32, kind="ExternalInput").ap()
    cosq = nc.dram_tensor("cosq", [128, 1024], F32, kind="ExternalInput").ap()
    sins = nc.dram_tensor("sins", [128, 1024], F32, kind="ExternalInput").ap()
    idin = nc.dram_tensor("idin", [128, 128], F32, kind="ExternalInput").ap()
    maskin = nc.dram_tensor("maskin", [128, 8], F32, kind="ExternalInput").ap()
    wvt = nc.dram_tensor("wvt", [2048, C], F32, kind="ExternalInput").ap()
    epsin = nc.dram_tensor("epsin", [128, 1], F32, kind="ExternalInput").ap()
    yout = nc.dram_tensor("y", [B, TL, C], F32, kind="ExternalOutput").ap()

    gsp = nc.dram_tensor("gsp", [NCH, 128, 2 * TL], BF16).ap()
    vsp = nc.dram_tensor("vsp", [NCH, 128, 2 * TL], BF16).ap()
    wsc = nc.dram_tensor("wsc", [NCH, 128, 8192], BF16).ap()
    cc_in = nc.dram_tensor("cc_in", [2, 128, 32], F32)
    cc_out = nc.dram_tensor("cc_out", [2 * NC, 128, 32], F32)

    dbg = {}
    if DEBUG:
        for nm, shp in [("d_carry", [128, 32]), ("d_hT", [128, 8192]),
                        ("d_x1", [128, 8192]), ("d_uF", [128, 8192]),
                        ("d_q", [128, 4096]), ("d_k", [128, 4096]),
                        ("d_lg", [128, 128]), ("d_o0", [128, 2048]),
                        ("d_o1", [128, 2048]), ("d_mF", [128, 8192]),
                        ("d_out2", [128, 2048]), ("d_cf", [128, 256]),
                        ("d_gv", [128, 4096]), ("d_AS", [128, 32]),
                        ("d_BS", [128, 32]), ("d_stA", [128, 128]),
                        ("d_xlnF", [128, 32768]), ("d_xt", [128, 2048])]:
            dbg[nm] = nc.dram_tensor(nm, shp, F32, kind="ExternalOutput").ap()

    wsrc = [(wq, 0, 1024), (wk, 1024, 1024), (wv, 2048, 2048), (wg, 4096, 2048), (wo, 6144, 2048)]

    from contextlib import ExitStack
    with ExitStack() as stack:
        sb = lambda *a: stack.enter_context(nc.sbuf_tensor(*a))
        idb = sb("idb", [128, 128], BF16)
        cw0b = sb("cw0b", [128, C], BF16)
        cw1b = sb("cw1b", [128, C], BF16)
        cosb = sb("cosb", [128, 1024], BF16)
        sinb = sb("sinb", [128, 1024], BF16)
        rbdb = sb("rbdb", [128, 512], BF16)
        wvrb = sb("wvrb", [128, 512], BF16)
        carry = sb("carry", [128, 32], F32)
        epst = sb("epst", [128, 1], F32)
        pb = [stack.enter_context(nc.psum_tensor("pb%d" % i, [128, 512], F32)) for i in range(8)]
        def mksem(nm):
            return Sem(stack.enter_context(nc.semaphore(nm)))
        dCS = mksem("dCS"); dCG = mksem("dCG"); dCA = mksem("dCA")
        dX = [mksem("dX0"), mksem("dX1")]
        dWA = [mksem("dWA0"), mksem("dWA1")]
        dSP = [mksem("dSP0"), mksem("dSP1")]
        dCC = mksem("dCC")
        dGV = [mksem("dGV0"), mksem("dGV1")]
        dWS = [mksem("dWS0"), mksem("dWS1")]
        dOF = [mksem("dOF0"), mksem("dOF1")]
        dXB = [mksem("dXB0"), mksem("dXB1"), mksem("dXB2"), mksem("dXB3")]
        dWV = [mksem("dWV0"), mksem("dWV1")]
        dDB = mksem("dDB")
        dDBG = mksem("dDBG")
        cT = mksem("cT"); cV = mksem("cV"); cS = mksem("cS"); cG = mksem("cG")
        SEMS = [dCS, dCG, dCA] + dX + dWA + dSP + [dCC] + dGV + dWS + dOF + dXB + dWV + [dDB, dDBG, cT, cV, cS, cG]
        bankb = [Buf(1) for _ in range(8)]

        state = {}

        def snap():
            return [s.n for s in SEMS]

        def restore(v):
            for s, x in zip(SEMS, v):
                s.n = x

        def sched_A(em, t):
            for bk in bankb:
                bk.waits = [[]]
            em.dma("sync", t["maskt"][:, :], maskin, dCS)
            em.dma("sync", t["b2t"][:, :], b2, dCS)
            sy_const = em.dma("sync", epst[:, :], epsin, dCS)
            em.dma("gpsimd", idb[:, :], idin, dCG)
            em.dma("gpsimd", cosb[:, :], cosq, dCG)
            em.dma("gpsimd", sinb[:, :], sins, dCG)
            em.dma("gpsimd", cw0b[:, :], cw0, dCG)
            em.dma("gpsimd", cw1b[:, :], cw1, dCG)
            gp_const = em.dma("gpsimd",
                              rbdb[:, :].rearrange("p (a m) -> p a m", a=16),
                              rbd.rearrange("(a p) m -> p a m", p=128), dCG)


            def vstep(fn):
                ev = em.do("vector", fn, inc=[(cV, 1)])
                em.wait("vector", cV, ev)
                return ev

            cast_jobs = []
            for (csrc, col0, w) in wsrc:
                for ck in range(NCH):
                    cast_jobs.append((csrc, col0, w, ck))

            def emit_casts(njobs):
                for _ in range(njobs):
                    if not cast_jobs:
                        return
                    csrc, col0, w, ck = cast_jobs.pop(0)
                    em.dma("gpsimd", wsc[ck, :, col0:col0 + w],
                           csrc[128 * ck:128 * (ck + 1), :], dCA)

            xt = t["xt"]; xnb = t["xnb"]; xlnF = t["xlnF"]; sqd = t["sqd"]
            stA = t["stA"]
            xt_b = Buf(2); xnb_b = Buf(2); wvs_b = Buf(2)
            evA1 = [0] * NT

            em.do("vector", lambda e: e.memset(t["ones1"][:, :], 1.0))
            em.wait("vector", dCS, sy_const)
            em.wait("scalar", dCS, sy_const)
            em.wait("tensor", dCG, gp_const)

            for i in range(NT):
                bbi, qq = i // 8, i % 8
                s = xt_b.acquire(em, "sync", i)
                cx = em.dma("sync", xt[:, 2048 * s:2048 * (s + 1)],
                            xin[bbi, 128 * qq:128 * (qq + 1), :], dX[s])
                xts = xt[:, 2048 * s:2048 * (s + 1)]
                em.wait("vector", dX[s], cx)
                em.do("vector", lambda e, xts=xts, i=i: e.reduce_sum(stA[:, i:i + 1], xts, AXX))
                em.do("vector", lambda e, xts=xts, i=i: e.scalar_tensor_tensor(
                    sqd[:, :], xts, 1.0, xts, OP.bypass, OP.mult, accum_out=stA[:, 16 + i:17 + i]))
                em.do("vector", lambda e, i=i: e.tensor_single_scalar(stA[:, 32 + i:33 + i], stA[:, i:i + 1], 1.0 / C, OP.mult))
                em.do("vector", lambda e, i=i: e.tensor_single_scalar(stA[:, 48 + i:49 + i], stA[:, 16 + i:17 + i], 1.0 / C, OP.mult))
                vmu2 = em.do("vector", lambda e, i=i: e.tensor_mul(stA[:, 64 + i:65 + i], stA[:, 32 + i:33 + i], stA[:, 32 + i:33 + i]), inc=[(cV, 1)])
                em.wait("vector", cV, vmu2)
                v2 = em.do("vector", lambda e, i=i: e.tensor_sub(stA[:, 80 + i:81 + i], stA[:, 48 + i:49 + i], stA[:, 64 + i:65 + i]), inc=[(cV, 1)])
                em.wait("scalar", cV, v2)
                s2 = em.do("scalar", lambda e, i=i: e.activation(stA[:, 96 + i:97 + i], stA[:, 80 + i:81 + i], AF.Sqrt, bias=epst[:, 0:1]), inc=[(cS, 1)])
                em.wait("vector", cS, s2)
                vrc = em.do("vector", lambda e, i=i: e.reciprocal(stA[:, 112 + i:113 + i], stA[:, 96 + i:97 + i]), inc=[(cV, 1)])
                em.wait("vector", cV, vrc)
                sn = xnb_b.acquire(em, "vector", i)
                xnbs = xnb[:, 2048 * sn:2048 * (sn + 1)]
                v3 = em.do("vector", lambda e, xnbs=xnbs, xts=xts, i=i: e.tensor_scalar(
                    xnbs, xts, stA[:, 32 + i:33 + i], stA[:, 112 + i:113 + i], OP.subtract, OP.mult), inc=[(cV, 1)])
                xt_b.used_by(s, cV, v3)
                em.wait("tensor", cV, v3)
                for h2 in range(2):
                    bk = 4 + h2
                    bankb[bk].acquire(em, "tensor", 0)
                    pv = pb[bk][:, :].bitcast(BF16)
                    for k in range(8):
                        kk = 8 * h2 + k
                        last = em.do("tensor", lambda e, kk=kk, k=k, pv=pv, xnbs=xnbs: e.transpose(
                            pv[:, 128 * k:128 * (k + 1)], xnbs[:, 128 * kk:128 * (kk + 1)], idb[:, :]),
                            inc=[(cT, 1)] if k == 7 else None)
                    em.wait("scalar", cT, last)
                    dst = xlnF[:, :].rearrange("p (a m) -> p a m", a=16)[:, 8 * h2:8 * h2 + 8, 128 * i:128 * (i + 1)]
                    ve = em.do("scalar", lambda e, dst=dst, pv=pv: e.copy(
                        dst, pv.rearrange("p (a m) -> p a m", a=8)), inc=[(cS, 1)])
                    bankb[bk].used_by(0, cS, ve)
                xnb_b.used_by(sn, cS, ve)
                evA1[i] = ve
                j = i
                wsl = wvs_b.acquire(em, "gpsimd", j)
                cwv = em.dma("gpsimd",
                             t["wvstage"][:, 2048 * wsl:2048 * (wsl + 1)].rearrange("p (a m) -> p a m", a=16),
                             wvt[:, 128 * j:128 * (j + 1)].rearrange("(a p) m -> p a m", p=128), dWV[wsl])
                em.wait("tensor", dWV[wsl], cwv)
                bankb[7].acquire(em, "tensor", 0)
                for k in range(NCH):
                    mmw = em.do("tensor", lambda e, k=k, wsl=wsl: e.matmul(
                        pb[7][:, 0:32],
                        t["wvstage"][:, 2048 * wsl + 128 * k:2048 * wsl + 128 * (k + 1)],
                        rbdb[:, 32 * k:32 * (k + 1)],
                        start=(k == 0), stop=(k == NCH - 1)),
                        inc=[(cT, 1)] if k == NCH - 1 else None)
                wvs_b.used_by(wsl, cT, mmw)
                em.wait("scalar", cT, mmw)
                wvre = em.do("scalar", lambda e, j=j: e.copy(
                    wvrb[:, 32 * j:32 * (j + 1)], pb[7][:, 0:32]), inc=[(cS, 1)])
                bankb[7].used_by(0, cS, wvre)
                emit_casts(2)

            if DEBUG:
                em.wait("sync", cV, cV.n)
                em.wait("sync", cS, cS.n)
                em.dma("sync", dbg["d_stA"], stA[:, :], dDB)
                dxl = em.dma("sync", dbg["d_xt"], xt[:, 2048:4096], dDB)
                em.wait("gpsimd", cV, cV.n)
                dxf = em.dma("gpsimd", dbg["d_xlnF"], xlnF[:, :], dDBG)
                em.wait("vector", dDBG, dxf)
                em.wait("vector", dDB, dxl)
                em.wait("tensor", dDBG, dxf)

            wA = t["wA"]; gv = t["gv"]; hloc = t["hloc"]
            AS = t["AS"]; BS = t["BS"]
            wA_b = Buf(2)
            gv_b = Buf(2)
            se = ve2 = 0
            pend = []

            def emit_spill():
                jd, jpd, gtd, vtd, sed = pend.pop(0)
                em.wait("gpsimd", cS, sed)
                em.dma("gpsimd", gsp[jd, :, :], gtd, dSP[jpd])
                sp2d = em.dma("gpsimd", vsp[jd, :, :], vtd, dSP[jpd])
                gv_b.used_by(jpd, dSP[jpd], sp2d)
                emit_casts(3)

            for j in range(NCH):
                sl = wA_b.acquire(em, "gpsimd", j)
                for idx in range(2):
                    d = j if idx == 0 else 16 + j
                    cwl = em.dma("gpsimd",
                                 wA[:, 4096 * sl + 2048 * idx:4096 * sl + 2048 * (idx + 1)].rearrange("p (a m) -> p a m", a=16),
                                 wt[:, 128 * d:128 * (d + 1)].rearrange("(a p) m -> p a m", p=128), dWA[sl])
                if pend:
                    emit_spill()
                jp = gv_b.acquire(em, "scalar", j)
                for idx in range(2):
                    d = j if idx == 0 else 16 + j
                    em.wait("tensor", dWA[sl], cwl)
                    gvt = gv[:, 4096 * jp + 2048 * idx: 4096 * jp + 2048 * (idx + 1)]
                    for n in range(4):
                        em.wait("tensor", cS, evA1[4 * n + 3])
                        bk = n
                        bankb[bk].acquire(em, "tensor", 0)
                        for k in range(NCH):
                            last = em.do("tensor", lambda e, k=k, bk=bk, sl=sl, n=n, idx=idx: e.matmul(
                                pb[bk][:, :],
                                wA[:, 4096 * sl + 2048 * idx + 128 * k:4096 * sl + 2048 * idx + 128 * (k + 1)],
                                xlnF[:, 2048 * k + 512 * n:2048 * k + 512 * (n + 1)],
                                start=(k == 0), stop=(k == NCH - 1)),
                                inc=[(cT, 1)] if k == NCH - 1 else None)
                        em.wait("scalar", cT, last)
                        fn = AF.Sigmoid if idx == 0 else AF.Tanh
                        se = em.do("scalar", lambda e, gvt=gvt, n=n, bk=bk, fn=fn, d=d: e.activation(
                            gvt[:, 512 * n:512 * (n + 1)], pb[bk][:, :], fn,
                            bias=t["b2t"][:, d:d + 1]), inc=[(cS, 1)])
                        bankb[bk].used_by(0, cS, se)
                    wA_b.used_by(sl, cT, last)
                gt = gv[:, 4096 * jp:4096 * jp + 2048]
                vt = gv[:, 4096 * jp + 2048:4096 * jp + 4096]
                pend.append((j, jp, gt, vt, se))
                em.wait("vector", cS, se)
                em.do("vector", lambda e, gt=gt, vt=vt: e.tensor_tensor_scan(
                    hloc[:, 0:1024], gt[:, 0:1024], vt[:, 0:1024], 0.0, OP.mult, OP.add))
                vstep(lambda e, gt=gt, vt=vt: e.tensor_tensor_scan(
                    hloc[:, 1024:2048], gt[:, 1024:2048], vt[:, 1024:2048], 0.0, OP.mult, OP.add))
                em.do("vector", lambda e, j=j: e.tensor_copy(BS[:, j:j + 1], hloc[:, 1023:1024]))
                em.do("vector", lambda e, j=j: e.tensor_copy(BS[:, 16 + j:17 + j], hloc[:, 2047:2048]))
                vstep(lambda e, gt=gt: e.tensor_tensor_scan(
                    hloc[:, 0:1024], gt[:, 0:1024], t["ones1"][:, :], 1.0, OP.mult, OP.mult))
                em.do("vector", lambda e, j=j: e.tensor_copy(AS[:, j:j + 1], hloc[:, 1023:1024]))
                vstep(lambda e, gt=gt: e.tensor_tensor_scan(
                    hloc[:, 0:1024], gt[:, 1024:2048], t["ones1"][:, :], 1.0, OP.mult, OP.mult))
                ve2 = em.do("vector", lambda e, j=j: e.tensor_copy(AS[:, 16 + j:17 + j], hloc[:, 1023:1024]), inc=[(cV, 1)])
                gv_b.used_by(jp, cV, ve2)

            while pend:
                emit_spill()
            emit_casts(100)
            em.wait("sync", cV, ve2)
            em.dma("sync", cc_in.ap()[0], AS[:, :], dCC)
            ci2 = em.dma("sync", cc_in.ap()[1], BS[:, :], dCC)
            em.wait("gpsimd", dCC, ci2)
            state["a_preex"] = snap()
            cce = em.do("gpsimd", lambda e: e.collective_compute(
                "AllGather", OP.bypass, replica_groups=[list(range(NC))],
                ins=[cc_in.ap().opt()], outs=[cc_out.ap().opt()]), inc=[(cG, 1)])
            em.wait("sync", cG, cce)
            cro = em.dma("sync", t["ccAll"][:, :].rearrange("p (a m) -> p a m", a=16),
                         cc_out.ap().rearrange("a p m -> p a m"), dCC)
            em.wait("vector", dCC, cro)
            vstep(lambda e: e.memset(t["hcomb"][:, 0:32], 0.0))
            cur = 0
            tmpc = t["hcomb"][:, 64:96]
            for jj in range(NC - 1):
                A_j = t["ccAll"][:, 64 * jj:64 * jj + 32]
                B_j = t["ccAll"][:, 64 * jj + 32:64 * jj + 64]
                Hc = t["hcomb"][:, 32 * cur:32 * cur + 32]
                Hn = t["hcomb"][:, 32 * (1 - cur):32 * (1 - cur) + 32]
                vstep(lambda e, A_j=A_j, Hc=Hc: e.tensor_mul(tmpc, A_j, Hc))
                vstep(lambda e, B_j=B_j: e.tensor_add(tmpc, tmpc, B_j))
                vstep(lambda e, Hc=Hc: e.tensor_sub(tmpc, tmpc, Hc))
                vstep(lambda e, jj=jj, Hc=Hc, Hn=Hn: e.scalar_tensor_tensor(
                    Hn, tmpc, t["maskt"][:, jj:jj + 1], Hc, OP.mult, OP.add))
                cur = 1 - cur
            vfin = em.do("vector", lambda e, cur=cur: e.tensor_copy(
                carry[:, :], t["hcomb"][:, 32 * cur:32 * cur + 32]), inc=[(cV, 1)])
            if DEBUG:
                em.wait("sync", cV, vfin)
                em.dma("sync", dbg["d_carry"], carry[:, :], dDB)
                em.dma("sync", dbg["d_gv"], gv[:, 4096:8192], dDB)
                em.dma("sync", dbg["d_AS"], AS[:, :], dDB)
                em.dma("sync", dbg["d_BS"], BS[:, :], dDB)
            state["a_end"] = snap()

        def sched_B(em, t):
            for bk in bankb:
                bk.waits = [[]]
            ae = state["a_end"]
            ape = state["a_preex"]
            for w in ENGS:
                vals = ape if w == "gpsimd" else ae
                for s_, v_ in zip(SEMS, vals):
                    em.wait(w, s_, v_)

            def vstep(fn):
                ev = em.do("vector", fn, inc=[(cV, 1)])
                em.wait("vector", cV, ev)
                return ev

            hT = t["hT"]; x1a = t["x1"]; xbf = t["xbf"]
            SCR = t["SCR"]; SCR2 = t["SCR2"]
            o2a = t["o2a"]; xn1b = t["xn1b"]; vb = t["vb"]; gsb = t["gsb"]
            umb = t["umb"]; fF = t["fF"]; wS = t["wS"]
            qb = t["qb"]; kb = t["kb"]
            qr = t["SCR"][:, :].bitcast(BF16)[:, 2048:3072]
            qt = t["SCR"][:, :].bitcast(BF16)[:, 3072:4096]
            lg = t["lg"]; dcf = t["dcf"]; gvl = t["gvl"]; hFt = t["hFt"]
            st = t["st"]

            xbf_b = Buf(1)
            gvl_b = Buf(2)
            hFt_b = Buf(2)
            um_b = Buf(4)
            wS_b = Buf(2)
            outf_b = Buf(2)
            x1_b = Buf(2)
            hT_b = Buf(1)
            quad = [[0, 1, 2, 3], [4, 5, 6, 7]]
            wgi = [0]
            xli = [0]
            SL = {}

            def x1v(s):
                return x1a[:, 8192 * (s % 2):8192 * (s % 2 + 1)]

            def fF3(k, tt):
                return fF[:, :].rearrange("p (a m) -> p a m", a=16)[:, k, 128 * tt:128 * (tt + 1)]

            def run_group(wbase, col0, evac_fn, qd, ttwaits=None, srcf=None):
                if srcf is None:
                    srcf = fF3
                def emit_wdma(bt):
                    slw = wS_b.acquire(em, "gpsimd", wgi[0])
                    wgi[0] += 1
                    cwl = em.dma("gpsimd",
                                 wS[:, 2048 * slw:2048 * (slw + 1)].rearrange("p (a m) -> p a m", a=4),
                                 wsc[4 * bt:4 * (bt + 1), :, wbase + col0:wbase + col0 + 512].rearrange("a p m -> p a m"),
                                 dWS[slw])
                    return (slw, cwl)
                ch = [emit_wdma(0), emit_wdma(1)]
                for bt in range(4):
                    slw, cwl = ch[bt]
                    em.wait("tensor", dWS[slw], cwl)
                    for kk in range(4):
                        k = 4 * bt + kk
                        for tt in range(4):
                            if k == 0:
                                if ttwaits is not None:
                                    em.wait("tensor", cS, ttwaits[tt])
                                bankb[quad[qd][tt]].acquire(em, "tensor", 0)
                            last = em.do("tensor", lambda e, k=k, kk=kk, tt=tt, qd=qd, slw=slw, srcf=srcf: e.matmul(
                                pb[quad[qd][tt]][:, :], srcf(k, tt),
                                wS[:, 2048 * slw + 512 * kk:2048 * slw + 512 * (kk + 1)],
                                start=(k == 0), stop=(k == NCH - 1)),
                                inc=[(cT, 1)] if tt == 3 else None)
                    wS_b.used_by(slw, cT, last)
                    if bt + 2 < 4:
                        ch.append(emit_wdma(bt + 2))
                for tt in range(4):
                    evac_fn(tt, quad[qd][tt], last)

            def emit_B1(s):
                hT_b.acquire(em, "scalar", 0)
                hev = 0
                for j in range(NCH):
                    sl = gvl_b.acquire(em, "sync", j)
                    em.dma("sync", gvl[:, 1024 * sl:1024 * sl + 512],
                           gsp[j].rearrange("p (b t) -> p b t", b=2)[:, :, SPOS * s:SPOS * (s + 1)], dGV[sl])
                    g2 = em.dma("sync", gvl[:, 1024 * sl + 512:1024 * (sl + 1)],
                                vsp[j].rearrange("p (b t) -> p b t", b=2)[:, :, SPOS * s:SPOS * (s + 1)], dGV[sl])
                    em.wait("vector", dGV[sl], g2)
                    hsl = hFt_b.acquire(em, "vector", j)
                    hfv = hFt[:, 512 * hsl:512 * (hsl + 1)]
                    gg = gvl[:, 1024 * sl:1024 * sl + 512]
                    vv = gvl[:, 1024 * sl + 512:1024 * (sl + 1)]
                    vstep(lambda e, gg=gg, vv=vv, hfv=hfv, j=j: e.tensor_tensor_scan(
                        hfv[:, 0:256], gg[:, 0:256], vv[:, 0:256], carry[:, j:j + 1], OP.mult, OP.add))
                    em.do("vector", lambda e, hfv=hfv, j=j: e.tensor_copy(carry[:, j:j + 1], hfv[:, 255:256]))
                    vstep(lambda e, gg=gg, vv=vv, hfv=hfv, j=j: e.tensor_tensor_scan(
                        hfv[:, 256:512], gg[:, 256:512], vv[:, 256:512], carry[:, 16 + j:17 + j], OP.mult, OP.add))
                    vsc = em.do("vector", lambda e, hfv=hfv, j=j: e.tensor_copy(
                        carry[:, 16 + j:17 + j], hfv[:, 511:512]), inc=[(cV, 1)])
                    gvl_b.used_by(sl, cV, vsc)
                    em.wait("tensor", cV, vsc)
                    pbk = 6 + j % 2
                    bankb[pbk].acquire(em, "tensor", 0)
                    pvb = pb[pbk][:, :].bitcast(BF16)
                    for tt in range(4):
                        last = em.do("tensor", lambda e, tt=tt, pvb=pvb, hfv=hfv: e.transpose(
                            pvb[:, 128 * tt:128 * (tt + 1)], hfv[:, 128 * tt:128 * (tt + 1)], idb[:, :]),
                            inc=[(cT, 1)] if tt == 3 else None)
                    hFt_b.used_by(hsl, cT, last)
                    em.wait("scalar", cT, last)
                    dst = hT[:, :].rearrange("p (a m) -> p a m", a=4)[:, :, 128 * j:128 * (j + 1)]
                    hev = em.do("scalar", lambda e, dst=dst, pvb=pvb: e.copy(
                        dst, pvb[:, 0:512].rearrange("p (a m) -> p a m", a=4)), inc=[(cS, 1)])
                    bankb[pbk].used_by(0, cS, hev)
                SL[s] = {"hev": hev}

            def emit_P2(s):
                x1 = x1v(s)
                hev = SL[s]["hev"]
                x1_b.acquire(em, "vector", s)
                for tt in range(4):
                    em.wait("vector", cS, hev)
                    hblk = hT[:, 2048 * tt:2048 * (tt + 1)]
                    em.do("vector", lambda e, hblk=hblk, tt=tt: e.reduce_sum(st[:, tt:1 + tt], hblk, AXX))
                    em.do("vector", lambda e, hblk=hblk, tt=tt: e.scalar_tensor_tensor(
                        SCR[:, :], hblk, 1.0, hblk, OP.bypass, OP.mult, accum_out=st[:, 4 + tt:5 + tt]))
                vstep(lambda e: e.engine_nop())
                vstep(lambda e: e.tensor_single_scalar(st[:, 8:12], st[:, 0:4], 1.0 / C, OP.mult))
                vstep(lambda e: e.tensor_single_scalar(st[:, 12:16], st[:, 4:8], 1.0 / C, OP.mult))
                vstep(lambda e: e.tensor_mul(st[:, 16:20], st[:, 8:12], st[:, 8:12]))
                vst = em.do("vector", lambda e: e.tensor_sub(st[:, 12:16], st[:, 12:16], st[:, 16:20]), inc=[(cV, 1)])
                em.wait("scalar", cV, vst)
                sst = em.do("scalar", lambda e: e.activation(st[:, 16:20], st[:, 12:16], AF.Sqrt, bias=epst[:, 0:1]), inc=[(cS, 1)])
                em.wait("vector", cS, sst)
                vstep(lambda e: e.reciprocal(st[:, 20:24], st[:, 16:20]))
                for tt in range(4):
                    i = [2 * s, 2 * s + 1, 8 + 2 * s, 9 + 2 * s][tt]
                    bbi, qq = i // 8, i % 8
                    sx = xbf_b.acquire(em, "sync", xli[0])
                    xli[0] += 1
                    cx = em.dma("sync", xbf[:, 2048 * sx:2048 * (sx + 1)],
                                xin[bbi, 128 * qq:128 * (qq + 1), :], dXB[sx])
                    em.wait("vector", dXB[sx], cx)
                    hblk = hT[:, 2048 * tt:2048 * (tt + 1)]
                    x1blk = x1[:, 2048 * tt:2048 * (tt + 1)]
                    em.do("vector", lambda e, hblk=hblk, tt=tt: e.tensor_scalar(
                        SCR[:, :], hblk, st[:, 8 + tt:9 + tt], st[:, 20 + tt:21 + tt], OP.subtract, OP.mult))
                    vx1 = em.do("vector", lambda e, x1blk=x1blk, sx=sx, tt=tt: e.scalar_tensor_tensor(
                        x1blk, SCR[:, :], 1.0, xbf[:, 2048 * sx:2048 * (sx + 1)], OP.bypass, OP.add,
                        accum_out=st[:, 24 + tt:25 + tt]), inc=[(cV, 1)])
                    xbf_b.used_by(sx, cV, vx1)
                    em.do("vector", lambda e, x1blk=x1blk, tt=tt: e.scalar_tensor_tensor(
                        SCR[:, :], x1blk, 1.0, x1blk, OP.bypass, OP.mult, accum_out=st[:, 28 + tt:29 + tt]))
                hT_b.used_by(0, cV, cV.n)
                if DEBUG and s == 0:
                    em.wait("gpsimd", cV, cV.n)
                    dh = em.dma("gpsimd", dbg["d_hT"], hT[:, :], dDBG)
                    em.wait("vector", dDBG, dh)
                vstep(lambda e: e.engine_nop())
                vstep(lambda e: e.tensor_single_scalar(st[:, 32:36], st[:, 24:28], 1.0 / C, OP.mult))
                vstep(lambda e: e.tensor_single_scalar(st[:, 36:40], st[:, 28:32], 1.0 / C, OP.mult))
                vstep(lambda e: e.tensor_mul(st[:, 40:44], st[:, 32:36], st[:, 32:36]))
                vst2 = em.do("vector", lambda e: e.tensor_sub(st[:, 36:40], st[:, 36:40], st[:, 40:44]), inc=[(cV, 1)])
                em.wait("scalar", cV, vst2)
                sst2 = em.do("scalar", lambda e: e.activation(st[:, 40:44], st[:, 36:40], AF.Sqrt, bias=epst[:, 0:1]), inc=[(cS, 1)])
                em.wait("vector", cS, sst2)
                vstep(lambda e: e.reciprocal(st[:, 44:48], st[:, 40:44]))

            def emit_P3P4(s):
                x1 = x1v(s)
                outf_b.acquire(em, "vector", 0)
                outf_b.acquire(em, "vector", 1)
                uF_evs = [0] * 4
                prev_silu = None
                for tt in range(4):
                    x1blk = x1[:, 2048 * tt:2048 * (tt + 1)]
                    if prev_silu is not None:
                        em.wait("vector", cS, prev_silu)
                    if tt < 2:
                        em.do("vector", lambda e, x1blk=x1blk, tt=tt: e.tensor_scalar(
                            xn1b[:, 2048 * tt:2048 * (tt + 1)], x1blk, st[:, 32 + tt:33 + tt], st[:, 44 + tt:45 + tt], OP.subtract, OP.mult))
                        vcv = em.do("vector", lambda e, tt=tt: e.tensor_mul(
                            SCR2[:, :], xn1b[:, 2048 * tt:2048 * (tt + 1)], cw1b[:, :]), inc=[(cV, 1)])
                    else:
                        em.do("vector", lambda e, x1blk=x1blk, tt=tt: e.tensor_scalar(
                            SCR[:, :], x1blk, st[:, 32 + tt:33 + tt], st[:, 44 + tt:45 + tt], OP.subtract, OP.mult))
                        em.do("vector", lambda e: e.tensor_mul(SCR2[:, :], SCR[:, :], cw1b[:, :]))
                        em.do("vector", lambda e, tt=tt: e.tensor_mul(
                            SCR[:, :], xn1b[:, 2048 * (tt - 2):2048 * (tt - 1)], cw0b[:, :]))
                        vcv = em.do("vector", lambda e: e.tensor_add(SCR2[:, :], SCR2[:, :], SCR[:, :]), inc=[(cV, 1)])
                    usl = um_b.acquire(em, "scalar", umi[0])
                    umi[0] += 1
                    em.wait("scalar", cV, vcv)
                    sustp = em.do("scalar", lambda e, usl=usl: e.activation(
                        umb[:, 2048 * usl:2048 * (usl + 1)], SCR2[:, :], AF.Silu), inc=[(cS, 1)])
                    prev_silu = sustp
                    em.wait("tensor", cS, sustp)
                    for h2 in range(2):
                        bk = 4 + h2
                        bankb[bk].acquire(em, "tensor", 0)
                        pv = pb[bk][:, :].bitcast(BF16)
                        for k in range(8):
                            kk = 8 * h2 + k
                            last = em.do("tensor", lambda e, pv=pv, k=k, kk=kk, usl=usl: e.transpose(
                                pv[:, 128 * k:128 * (k + 1)], umb[:, 2048 * usl + 128 * kk:2048 * usl + 128 * (kk + 1)], idb[:, :]),
                                inc=[(cT, 1)] if k == 7 else None)
                        em.wait("scalar", cT, last)
                        dst = fF[:, :].rearrange("p (a m) -> p a m", a=16)[:, 8 * h2:8 * h2 + 8, 128 * tt:128 * (tt + 1)]
                        uev = em.do("scalar", lambda e, dst=dst, pv=pv: e.copy(
                            dst, pv.rearrange("p (a m) -> p a m", a=8)), inc=[(cS, 1)])
                        bankb[bk].used_by(0, cS, uev)
                    um_b.used_by(usl, cT, last)
                    uF_evs[tt] = uev
                SL[s]["uF_evs"] = uF_evs
                SL[s]["silu_last"] = prev_silu

            def emit_logits(s):
                uF_evs = SL[s]["uF_evs"]
                lgp = lg[:, 128 * (s % 2):128 * (s % 2 + 1)]
                for tt in range(4):
                    em.wait("tensor", cS, uF_evs[tt])
                    bankb[2].acquire(em, "tensor", 0)
                    for k in range(NCH):
                        last = em.do("tensor", lambda e, k=k, tt=tt: e.matmul(
                            pb[2][:, 0:32], fF3(k, tt), wvrb[:, 32 * k:32 * (k + 1)],
                            start=(k == 0), stop=(k == NCH - 1)),
                            inc=[(cT, 1)] if k == NCH - 1 else None)
                    em.wait("scalar", cT, last)
                    lev = em.do("scalar", lambda e, tt=tt, lgp=lgp: e.copy(
                        lgp[:, 32 * tt:32 * (tt + 1)], pb[2][:, 0:32]), inc=[(cS, 1)])
                    bankb[2].used_by(0, cS, lev)
                SL[s]["lg_ev"] = lev

            def emit_qkvg(s):
                prev = SL.get(s - 1, {})
                def ev_q(tt, bkk, last, gi2):
                    dste = qb[:, 1024 * tt + 512 * gi2:1024 * tt + 512 * (gi2 + 1)]
                    em.wait("scalar", cT, last)
                    if "qk_read" in prev:
                        em.wait("scalar", cV, prev["qk_read"])
                    ee = em.do("scalar", lambda e, dste=dste, bkk=bkk: e.activation(
                        dste, pb[bkk][:, :], AF.Silu), inc=[(cS, 1)])
                    bankb[bkk].used_by(0, cS, ee)
                def ev_k(tt, bkk, last, gi2):
                    dste = kb[:, 1024 * tt + 512 * gi2:1024 * tt + 512 * (gi2 + 1)]
                    em.wait("scalar", cT, last)
                    if "qk_read" in prev:
                        em.wait("scalar", cV, prev["qk_read"])
                    ee = em.do("scalar", lambda e, dste=dste, bkk=bkk: e.copy(
                        dste, pb[bkk][:, :]), inc=[(cS, 1)])
                    bankb[bkk].used_by(0, cS, ee)
                gidx = 0
                for (wap, ngr, evf) in [(0, 2, ev_q), (1024, 2, ev_k)]:
                    for gi2 in range(ngr):
                        run_group(wap, 512 * gi2,
                                  lambda tt, bkk, last, gi2=gi2, evf=evf: evf(tt, bkk, last, gi2),
                                  gidx % 2)
                        gidx += 1
                SL[s]["qk_S"] = cS.n

            def emit_qkvg_vg(s):
                prev = SL.get(s - 1, {})
                def ev_v(tt, bkk, last, gi2):
                    dste = vb[:, 2048 * tt + 512 * gi2:2048 * tt + 512 * (gi2 + 1)]
                    em.wait("scalar", cT, last)
                    if "v_read" in prev:
                        em.wait("scalar", cV, prev["v_read"])
                    ee = em.do("scalar", lambda e, dste=dste, bkk=bkk: e.copy(
                        dste, pb[bkk][:, :]), inc=[(cS, 1)])
                    bankb[bkk].used_by(0, cS, ee)
                def ev_g(tt, bkk, last, gi2):
                    dste = gsb[:, 2048 * tt + 512 * gi2:2048 * tt + 512 * (gi2 + 1)]
                    em.wait("scalar", cT, last)
                    if "g_read" in prev:
                        em.wait("scalar", cV, prev["g_read"])
                    ee = em.do("scalar", lambda e, dste=dste, bkk=bkk: e.activation(
                        dste, pb[bkk][:, :], AF.Silu), inc=[(cS, 1)])
                    bankb[bkk].used_by(0, cS, ee)
                gidx = 4
                for (wap, ngr, evf) in [(2048, 4, ev_v), (4096, 4, ev_g)]:
                    for gi2 in range(ngr):
                        run_group(wap, 512 * gi2,
                                  lambda tt, bkk, last, gi2=gi2, evf=evf: evf(tt, bkk, last, gi2),
                                  gidx % 2)
                        gidx += 1
                SL[s]["vg_S"] = cS.n
                if DEBUG and s == 0:
                    em.wait("gpsimd", cS, SL[s]["qk_S"])
                    em.dma("gpsimd", dbg["d_uF"], fF[:, :], dDBG)
                    em.dma("gpsimd", dbg["d_q"], qb[:, :], dDBG)
                    duk = em.dma("gpsimd", dbg["d_k"], kb[:, :], dDBG)
                    dlg = em.dma("gpsimd", dbg["d_lg"], lg[:, 0:128], dDBG)
                    em.wait("tensor", dDBG, dlg)

            umi = [0]

            def emit_mid_v(s):
                mevs = [0] * 4
                uslmap = {}
                vmes = {}
                o2abf = o2a[:, :].bitcast(BF16)
                lgp = lg[:, 128 * (s % 2):128 * (s % 2 + 1)]
                r32 = lambda ap: ap.rearrange("p (h two d) -> p h two d", two=2, d=32)
                r3 = lambda ap: ap.rearrange("p (h d) -> p h d", h=16)
                em.wait("vector", cS, SL[s]["qk_S"])
                for pi in range(2):
                    tb0, tb1 = pi, 2 + pi
                    dc = 256 * pi
                    qsrc = qb[:, 1024 * tb1:1024 * (tb1 + 1)]
                    em.do("vector", lambda e, qsrc=qsrc: e.tensor_mul(qr, qsrc, cosb[:, :]))
                    em.do("vector", lambda e, qsrc=qsrc: e.tensor_mul(
                        r32(qt)[:, :, 0, :], r32(qsrc)[:, :, 1, :], r32(sinb[:, :])[:, :, 0, :]))
                    vstep(lambda e, qsrc=qsrc: e.tensor_mul(
                        r32(qt)[:, :, 1, :], r32(qsrc)[:, :, 0, :], r32(sinb[:, :])[:, :, 1, :]))
                    em.do("vector", lambda e: e.tensor_add(qr, qr, qt))
                    DOT = SCR
                    em.do("vector", lambda e, tb0=tb0: e.tensor_mul(
                        DOT[:, 0:1024], qb[:, 1024 * tb0:1024 * (tb0 + 1)], kb[:, 1024 * tb0:1024 * (tb0 + 1)]))
                    em.do("vector", lambda e, dc=dc: e.tensor_reduce(dcf[:, dc:dc + 16], r3(DOT[:, 0:1024]), AXX, OP.add))
                    em.do("vector", lambda e, tb0=tb0: e.tensor_mul(
                        DOT[:, 0:1024], qr, kb[:, 1024 * tb0:1024 * (tb0 + 1)]))
                    em.do("vector", lambda e, dc=dc: e.tensor_reduce(dcf[:, dc + 16:dc + 32], r3(DOT[:, 0:1024]), AXX, OP.add))
                    em.do("vector", lambda e, tb1=tb1: e.tensor_mul(
                        DOT[:, 0:1024], qb[:, 1024 * tb1:1024 * (tb1 + 1)], kb[:, 1024 * tb1:1024 * (tb1 + 1)]))
                    em.do("vector", lambda e, dc=dc: e.tensor_reduce(dcf[:, dc + 32:dc + 48], r3(DOT[:, 0:1024]), AXX, OP.add))
                SL[s]["qk_read"] = em.do("vector", lambda e: e.engine_nop(), inc=[(cV, 1)])
                for pi in range(2):
                    tb0, tb1 = pi, 2 + pi
                    dc = 256 * pi
                    L0 = lgp[:, 32 * tb0:32 * (tb0 + 1)]
                    L1 = lgp[:, 32 * tb1:32 * (tb1 + 1)]
                    vstep(lambda e, L0=L0, dc=dc: e.tensor_sub(dcf[:, dc + 48:dc + 64], L0[:, 0:16], L0[:, 16:32]))
                    vstep(lambda e, L1=L1, dc=dc: e.tensor_sub(dcf[:, dc + 64:dc + 80], L1[:, 0:16], L1[:, 16:32]))
                    vstep(lambda e, dc=dc: e.tensor_single_scalar(dcf[:, dc + 80:dc + 96], dcf[:, dc + 48:dc + 64], -1.0, OP.mult))
                    vstep(lambda e, dc=dc: e.tensor_max(dcf[:, dc + 80:dc + 96], dcf[:, dc + 80:dc + 96], dcf[:, dc + 48:dc + 64]))
                    vstep(lambda e, dc=dc: e.tensor_single_scalar(dcf[:, dc + 96:dc + 112], dcf[:, dc + 64:dc + 80], -1.0, OP.mult))
                    vmx = em.do("vector", lambda e, dc=dc: e.tensor_max(dcf[:, dc + 96:dc + 112], dcf[:, dc + 96:dc + 112], dcf[:, dc + 64:dc + 80]), inc=[(cV, 1)])
                    em.wait("scalar", cV, vmx)
                    ssg = em.do("scalar", lambda e, dc=dc: e.activation(dcf[:, dc + 112:dc + 144], dcf[:, dc + 80:dc + 112], AF.Tanh, scale=0.5), inc=[(cS, 1)])
                    em.wait("vector", cS, ssg)
                    vstep(lambda e, dc=dc: e.tensor_scalar(dcf[:, dc + 112:dc + 144], dcf[:, dc + 112:dc + 144], 0.5, 0.5, OP.mult, OP.add))
                    vstep(lambda e, dc=dc: e.tensor_single_scalar(dcf[:, dc + 144:dc + 160], dcf[:, dc + 48:dc + 64], 0.0, OP.is_ge))
                    vstep(lambda e, dc=dc: e.tensor_single_scalar(dcf[:, dc + 160:dc + 176], dcf[:, dc + 64:dc + 80], 0.0, OP.is_ge))
                    vstep(lambda e, dc=dc: e.tensor_mul(dcf[:, dc + 176:dc + 192], dcf[:, dc + 144:dc + 160], dcf[:, dc + 160:dc + 176]))
                    vstep(lambda e, dc=dc: e.tensor_add(dcf[:, dc + 144:dc + 160], dcf[:, dc + 144:dc + 160], dcf[:, dc + 160:dc + 176]))
                    vstep(lambda e, dc=dc: e.tensor_single_scalar(dcf[:, dc + 176:dc + 192], dcf[:, dc + 176:dc + 192], 2.0, OP.mult))
                    vstep(lambda e, dc=dc: e.tensor_sub(dcf[:, dc + 176:dc + 192], dcf[:, dc + 176:dc + 192], dcf[:, dc + 144:dc + 160]))
                    vstep(lambda e, dc=dc: e.tensor_single_scalar(dcf[:, dc + 176:dc + 192], dcf[:, dc + 176:dc + 192], 1.0, OP.add))
                    vstep(lambda e, dc=dc: e.tensor_mul(dcf[:, dc + 192:dc + 208], dcf[:, dc + 112:dc + 128], dcf[:, dc:dc + 16]))
                    vstep(lambda e, dc=dc: e.tensor_single_scalar(dcf[:, dc + 192:dc + 208], dcf[:, dc + 192:dc + 208], 0.125, OP.mult))
                    vstep(lambda e, dc=dc: e.tensor_mul(dcf[:, dc + 208:dc + 224], dcf[:, dc + 128:dc + 144], dcf[:, dc + 16:dc + 32]))
                    vstep(lambda e, dc=dc: e.tensor_mul(dcf[:, dc + 208:dc + 224], dcf[:, dc + 208:dc + 224], dcf[:, dc + 176:dc + 192]))
                    vstep(lambda e, dc=dc: e.tensor_single_scalar(dcf[:, dc + 208:dc + 224], dcf[:, dc + 208:dc + 224], 0.0625, OP.mult))
                    vstep(lambda e, dc=dc: e.tensor_scalar(dcf[:, dc + 240:dc + 256], dcf[:, dc + 176:dc + 192], -0.5, 1.0, OP.mult, OP.add))
                    vstep(lambda e, dc=dc: e.tensor_mul(dcf[:, dc + 224:dc + 240], dcf[:, dc + 128:dc + 144], dcf[:, dc + 32:dc + 48]))
                    vstep(lambda e, dc=dc: e.tensor_mul(dcf[:, dc + 224:dc + 240], dcf[:, dc + 224:dc + 240], dcf[:, dc + 240:dc + 256]))
                    vstep(lambda e, dc=dc: e.tensor_single_scalar(dcf[:, dc + 224:dc + 240], dcf[:, dc + 224:dc + 240], 0.125, OP.mult))
                em.wait("vector", cS, SL[s]["vg_S"])
                if s + 1 in SL and "silu_last" in SL[s + 1]:
                    em.wait("vector", cS, SL[s + 1]["silu_last"])
                outf_b.acquire(em, "vector", 0)
                r3v = r3
                bc = lambda ap: ap.broadcast_to((128, 16, 128))
                sq = SCR2[:, :]
                for pi in range(2):
                    tb0, tb1 = pi, 2 + pi
                    dc = 256 * pi
                    o0 = o2abf[:, 4096 * pi:4096 * pi + 2048]
                    o1 = o2abf[:, 4096 * pi + 2048:4096 * (pi + 1)]
                    vb0 = vb[:, 2048 * tb0:2048 * (tb0 + 1)]
                    vb1 = vb[:, 2048 * tb1:2048 * (tb1 + 1)]
                    em.do("vector", lambda e, o0=o0, vb0=vb0, dc=dc: e.tensor_tensor(
                        r3v(o0), r3v(vb0), bc(dcf[:, dc + 192:dc + 208]), OP.mult))
                    em.do("vector", lambda e, sq=sq, vb0=vb0, dc=dc: e.tensor_tensor(
                        r3v(sq), r3v(vb0), bc(dcf[:, dc + 208:dc + 224]), OP.mult))
                    em.do("vector", lambda e, o1=o1, vb1=vb1, dc=dc: e.tensor_tensor(
                        r3v(o1), r3v(vb1), bc(dcf[:, dc + 224:dc + 240]), OP.mult))
                    em.do("vector", lambda e, o1=o1, sq=sq: e.tensor_add(o1, o1, sq))
                SL[s]["v_read"] = em.do("vector", lambda e: e.engine_nop(), inc=[(cV, 1)])
                for pi in range(2):
                    tb0, tb1 = pi, 2 + pi
                    o0 = o2abf[:, 4096 * pi:4096 * pi + 2048]
                    o1 = o2abf[:, 4096 * pi + 2048:4096 * (pi + 1)]
                    vstep(lambda e, o0=o0: e.tensor_reduce(st[:, 48:64], r3(o0), AXX, OP.add))
                    vstep(lambda e, o0=o0, sq=sq: e.scalar_tensor_tensor(sq, o0, 1.0, o0, OP.bypass, OP.mult))
                    vstep(lambda e, sq=sq: e.tensor_reduce(st[:, 64:80], r3(sq), AXX, OP.add))
                    vstep(lambda e, o1=o1: e.tensor_reduce(st[:, 80:96], r3(o1), AXX, OP.add))
                    vstep(lambda e, o1=o1, sq=sq: e.scalar_tensor_tensor(sq, o1, 1.0, o1, OP.bypass, OP.mult))
                    vstep(lambda e, sq=sq: e.tensor_reduce(st[:, 96:112], r3(sq), AXX, OP.add))
                    vstep(lambda e: e.tensor_single_scalar(st[:, 48:64], st[:, 48:64], 1.0 / HDV, OP.mult))
                    vstep(lambda e: e.tensor_single_scalar(st[:, 64:80], st[:, 64:80], 1.0 / HDV, OP.mult))
                    vstep(lambda e: e.tensor_single_scalar(st[:, 80:96], st[:, 80:96], 1.0 / HDV, OP.mult))
                    vstep(lambda e: e.tensor_single_scalar(st[:, 96:112], st[:, 96:112], 1.0 / HDV, OP.mult))
                    vstep(lambda e: e.tensor_mul(st[:, 112:128], st[:, 48:64], st[:, 48:64]))
                    vstep(lambda e: e.tensor_sub(st[:, 64:80], st[:, 64:80], st[:, 112:128]))
                    vstep(lambda e: e.tensor_mul(st[:, 112:128], st[:, 80:96], st[:, 80:96]))
                    vhl = em.do("vector", lambda e: e.tensor_sub(st[:, 96:112], st[:, 96:112], st[:, 112:128]), inc=[(cV, 1)])
                    em.wait("scalar", cV, vhl)
                    em.do("scalar", lambda e: e.activation(st[:, 128:144], st[:, 64:80], AF.Sqrt, bias=epst[:, 0:1]), inc=[(cS, 1)])
                    shl2 = em.do("scalar", lambda e: e.activation(st[:, 144:160], st[:, 96:112], AF.Sqrt, bias=epst[:, 0:1]), inc=[(cS, 1)])
                    em.wait("vector", cS, shl2)
                    vstep(lambda e: e.reciprocal(st[:, 112:128], st[:, 128:144]))
                    vstep(lambda e: e.reciprocal(st[:, 128:144], st[:, 144:160]))
                    for (oo, tb, mu0, rs0) in [(o0, tb0, 48, 112), (o1, tb1, 80, 128)]:
                        usl = um_b.acquire(em, "vector", umi[0])
                        umi[0] += 1
                        uslmap[tb] = usl
                        for h in range(H):
                            hs0 = slice(128 * h, 128 * (h + 1))
                            em.do("vector", lambda e, oo=oo, h=h, hs0=hs0, mu0=mu0, rs0=rs0: e.tensor_scalar(
                                SCR[:, hs0], oo[:, hs0], st[:, mu0 + h:mu0 + h + 1], st[:, rs0 + h:rs0 + h + 1], OP.subtract, OP.mult))
                        vme = em.do("vector", lambda e, usl=usl, tb=tb: e.tensor_mul(
                            umb[:, 2048 * usl:2048 * (usl + 1)], SCR[:, :], gsb[:, 2048 * tb:2048 * (tb + 1)]), inc=[(cV, 1)])
                        vmes[tb] = vme
                SL[s]["g_read"] = vme
                SL[s]["uslmap"] = uslmap
                SL[s]["vmes"] = vmes

            def emit_mid_t(s):
                mevs = [0] * 4
                for tb in [0, 2, 1, 3]:
                    usl = SL[s]["uslmap"][tb]
                    em.wait("tensor", cV, SL[s]["vmes"][tb])
                    for h2 in range(2):
                        bk = 4 + h2
                        bankb[bk].acquire(em, "tensor", 0)
                        pv = pb[bk][:, :].bitcast(BF16)
                        for k in range(8):
                            kk = 8 * h2 + k
                            last = em.do("tensor", lambda e, pv=pv, k=k, kk=kk, usl=usl: e.transpose(
                                pv[:, 128 * k:128 * (k + 1)], umb[:, 2048 * usl + 128 * kk:2048 * usl + 128 * (kk + 1)], idb[:, :]),
                                inc=[(cT, 1)] if k == 7 else None)
                        em.wait("scalar", cT, last)
                        dst = hT[:, :].rearrange("p (a m) -> p a m", a=16)[:, 8 * h2:8 * h2 + 8, 128 * tb:128 * (tb + 1)]
                        mev = em.do("scalar", lambda e, dst=dst, pv=pv: e.copy(
                            dst, pv.rearrange("p (a m) -> p a m", a=8)), inc=[(cS, 1)])
                        bankb[bk].used_by(0, cS, mev)
                    um_b.used_by(usl, cT, last)
                    mevs[tb] = mev
                SL[s]["mevs"] = mevs

            def emit_wout(s):
                oevs = [0] * 4
                def ev_o(tt, bkk, last, gi2):
                    o2blk = o2a[:, :].bitcast(BF16)[:, 2048 * tt + 512 * gi2:2048 * tt + 512 * (gi2 + 1)]
                    em.wait("scalar", cT, last)
                    ee = em.do("scalar", lambda e, o2blk=o2blk, bkk=bkk: e.activation(
                        o2blk, pb[bkk][:, :], AF.Copy, accum_out=st[:, 160 + 4 * tt + gi2:161 + 4 * tt + gi2]), inc=[(cS, 1)])
                    bankb[bkk].used_by(0, cS, ee)
                    if gi2 == 3:
                        oevs[tt] = ee
                hTm = lambda k, tt: hT[:, :].rearrange("p (a m) -> p a m", a=16)[:, k, 128 * tt:128 * (tt + 1)]
                for gi2 in range(4):
                    run_group(6144, 512 * gi2,
                              lambda tt, bkk, last, gi2=gi2: ev_o(tt, bkk, last, gi2),
                              gi2 % 2, ttwaits=SL[s]["mevs"], srcf=hTm)
                hT_b.used_by(0, cT, cT.n)
                SL[s]["oevs"] = oevs

            def emit_fin(s):
                x1 = x1v(s)
                o2bf = o2a[:, :].bitcast(BF16)
                outf_slots = [SCR2[:, :], xn1b[:, :].bitcast(F32)[:, 0:2048]]
                vfo = 0
                for tt in range(4):
                    i = [2 * s, 2 * s + 1, 8 + 2 * s, 9 + 2 * s][tt]
                    bbi, qq = i // 8, i % 8
                    o2t = o2bf[:, 2048 * tt:2048 * (tt + 1)]
                    em.wait("vector", cS, SL[s]["oevs"][tt])
                    em.do("vector", lambda e, o2t=o2t, tt=tt: e.reduce_sum(st[:, 176 + tt:177 + tt], o2t, AXX))
                    vstep(lambda e, o2t=o2t, tt=tt: e.scalar_tensor_tensor(
                        SCR[:, :], o2t, 1.0, o2t, OP.bypass, OP.mult,
                        accum_out=st[:, 180 + tt:181 + tt]))
                    vstep(lambda e, tt=tt: e.tensor_single_scalar(st[:, 184 + tt:185 + tt], st[:, 176 + tt:177 + tt], 1.0 / C, OP.mult))
                    vstep(lambda e, tt=tt: e.tensor_single_scalar(st[:, 188 + tt:189 + tt], st[:, 180 + tt:181 + tt], 1.0 / C, OP.mult))
                    vstep(lambda e, tt=tt: e.tensor_mul(st[:, 200 + tt:201 + tt], st[:, 184 + tt:185 + tt], st[:, 184 + tt:185 + tt]))
                    vv2 = em.do("vector", lambda e, tt=tt: e.tensor_sub(st[:, 188 + tt:189 + tt], st[:, 188 + tt:189 + tt], st[:, 200 + tt:201 + tt]), inc=[(cV, 1)])
                    em.wait("scalar", cV, vv2)
                    sq2 = em.do("scalar", lambda e, tt=tt: e.activation(st[:, 192 + tt:193 + tt], st[:, 188 + tt:189 + tt], AF.Sqrt, bias=epst[:, 0:1]), inc=[(cS, 1)])
                    em.wait("vector", cS, sq2)
                    vstep(lambda e, tt=tt: e.reciprocal(st[:, 196 + tt:197 + tt], st[:, 192 + tt:193 + tt]))
                    fs = outf_b.acquire(em, "vector", tt)
                    fblk = outf_slots[fs]
                    em.do("vector", lambda e, o2t=o2t, tt=tt: e.tensor_scalar(
                        SCR[:, :], o2t, st[:, 184 + tt:185 + tt], st[:, 196 + tt:197 + tt], OP.subtract, OP.mult))
                    vfo = em.do("vector", lambda e, fblk=fblk, tt=tt, x1=x1: e.tensor_add(
                        fblk, SCR[:, :], x1[:, 2048 * tt:2048 * (tt + 1)]), inc=[(cV, 1)])
                    em.wait("sync", cV, vfo)
                    cst = em.dma("sync", yout[bbi, 128 * qq:128 * (qq + 1), :], fblk, dOF[fs])
                    outf_b.used_by(fs, dOF[fs], cst)
                x1_b.used_by(s % 2, cV, vfo)
                if DEBUG and s == 0:
                    em.wait("gpsimd", cV, cV.n)
                    em.wait("gpsimd", cS, cS.n)
                    em.dma("gpsimd", dbg["d_x1"], x1[:, :], dDBG)
                    dlast = em.dma("gpsimd", dbg["d_out2"], o2bf[:, 0:2048], dDBG)
                    em.wait("vector", dDBG, dlast)
                    em.wait("scalar", dDBG, dlast)
                    em.wait("tensor", dDBG, dlast)

            emit_B1(0)
            emit_P2(0)
            emit_P3P4(0)
            emit_logits(0)
            emit_qkvg(0)
            emit_qkvg_vg(0)
            emit_B1(1)
            for s in range(NSLAB):
                if s + 1 < NSLAB:
                    emit_P2(s + 1)
                if s >= 1:
                    emit_qkvg_vg(s)
                if s + 1 < NSLAB:
                    emit_P3P4(s + 1)
                    emit_logits(s + 1)
                emit_mid_v(s)
                if s + 1 < NSLAB:
                    emit_qkvg(s + 1)
                emit_mid_t(s)
                emit_wout(s)
                if s + 2 < NSLAB:
                    emit_B1(s + 2)
                emit_fin(s)
            em.wait("sync", dOF[0], dOF[0].n)
            em.wait("sync", dOF[1], dOF[1].n)
            if DEBUG:
                em.wait("gpsimd", dDBG, dDBG.n)

        with ExitStack() as stA_alloc:
            sa = lambda *a: stA_alloc.enter_context(nc.sbuf_tensor(*a))
            tA = {"b2t": sa("b2t", [128, 32], F32),
                  "xt": sa("xtA", [128, 2 * 2048], F32),
                  "xnb": sa("xnbA", [128, 2 * 2048], BF16),
                  "xlnF": sa("xlnFA", [128, 16 * 2048], BF16),
                  "sqd": sa("sqdA", [128, 2048], F32),
                  "stA": sa("stAA", [128, 128], F32),
                  "wA": sa("wAA", [128, 4 * 2048], BF16),
                  "gv": sa("gvA", [128, 2 * 4096], F32),
                  "hloc": sa("hlocA", [128, 2048], F32),
                  "ones1": sa("ones1A", [128, 1024], F32),
                  "wvstage": sa("wvstageA", [128, 2 * 2048], BF16),
                  "AS": sa("ASA", [128, 32], F32),
                  "BS": sa("BSA", [128, 32], F32),
                  "maskt": sa("masktA", [128, 8], F32),
                  "ccAll": sa("ccAllA", [128, 512], F32),
                  "hcomb": sa("hcombA", [128, 96], F32)}
            base = snap()
            with nc.Block() as blk:
                for eng in ENGS:
                    def mkA(eng):
                        def f(e):
                            restore(base)
                            sched_A(Em(eng, e), tA)
                        return f
                    getattr(blk, eng)(mkA(eng))

        with ExitStack() as stB_alloc:
            sbb = lambda *a: stB_alloc.enter_context(nc.sbuf_tensor(*a))
            tB = {"hT": sbb("hTB", [128, 4 * 2048], BF16),
                  "x1": sbb("x1B", [128, 8 * 2048], BF16),
                  "xbf": sbb("xbfB", [128, 2048], F32),
                  "SCR": sbb("SCRB", [128, 2048], F32),
                  "SCR2": sbb("SCR2B", [128, 2048], F32),
                  "o2a": sbb("o2aB", [128, 2 * 2048], F32),
                  "xn1b": sbb("xn1bB", [128, 2 * 2048], BF16),
                  "vb": sbb("vbB", [128, 4 * 2048], BF16),
                  "gsb": sbb("gsbB", [128, 4 * 2048], BF16),
                  "umb": sbb("umbB", [128, 4 * 2048], BF16),
                  "fF": sbb("fFB", [128, 16 * 512], BF16),
                  "wS": sbb("wSB", [128, 8 * 512], BF16),
                  "qb": sbb("qbB", [128, 4 * 1024], BF16),
                  "kb": sbb("kbB", [128, 4 * 1024], BF16),
                  "lg": sbb("lgB", [128, 256], BF16),
                  "dcf": sbb("dcfB", [128, 512], F32),
                  "gvl": sbb("gvlB", [128, 2048], BF16),
                  "hFt": sbb("hFtB", [128, 1024], BF16),
                  "st": sbb("stBB", [128, 204], F32)}
            baseB = snap()
            with nc.Block() as blk:
                for eng in ENGS:
                    def mkB(eng):
                        def f(e):
                            restore(baseB)
                            sched_B(Em(eng, e), tB)
                        return f
                    getattr(blk, eng)(mkB(eng))

    return nc



def kernel(L0_attn_x, L0_attn_W, L0_attn_b, L0_ffn_conv_w, L0_ffn_Wq, L0_ffn_Wk,
           L0_ffn_Wv, L0_ffn_Wg, L0_ffn_router_w, L0_ffn_Wout):
    x = np.ascontiguousarray(np.asarray(L0_attn_x, dtype=np.float32))
    W = np.asarray(L0_attn_W, dtype=np.float32)
    bb = np.asarray(L0_attn_b, dtype=np.float32)
    conv_w = np.asarray(L0_ffn_conv_w, dtype=np.float32)
    Wq = np.ascontiguousarray(np.asarray(L0_ffn_Wq, dtype=np.float32))
    Wk = np.ascontiguousarray(np.asarray(L0_ffn_Wk, dtype=np.float32))
    Wv = np.ascontiguousarray(np.asarray(L0_ffn_Wv, dtype=np.float32))
    Wg = np.ascontiguousarray(np.asarray(L0_ffn_Wg, dtype=np.float32))
    rw = np.asarray(L0_ffn_router_w, dtype=np.float32)
    Wo = np.ascontiguousarray(np.asarray(L0_ffn_Wout, dtype=np.float32))

    wt = np.ascontiguousarray(W.T)
    b2 = np.ascontiguousarray(bb.reshape(32, 128).T)
    rbdm = np.zeros((C, 32), np.float32)
    for h in range(H):
        rbdm[HDV * h:HDV * (h + 1), h] = rw[h, :, 0]
        rbdm[HDV * h:HDV * (h + 1), 16 + h] = rw[h, :, 1]
    cw0b = np.ascontiguousarray(np.broadcast_to(conv_w[:, 0], (128, C)))
    cw1b = np.ascontiguousarray(np.broadcast_to(conv_w[:, 1], (128, C)))
    inv_freq = (1.0 / (10000.0 ** (np.arange(0, HDK, 2, dtype=np.float32) / HDK))).astype(np.float32)
    cos_row = np.cos(inv_freq).astype(np.float32)
    sin_row = np.sin(inv_freq).astype(np.float32)
    cosqm = np.zeros((128, 1024), np.float32)
    sinsm = np.zeros((128, 1024), np.float32)
    for h in range(H):
        cosqm[:, 64 * h:64 * h + 32] = cos_row
        cosqm[:, 64 * h + 32:64 * h + 64] = cos_row
        sinsm[:, 64 * h:64 * h + 32] = -sin_row
        sinsm[:, 64 * h + 32:64 * h + 64] = sin_row
    ident = np.eye(128, dtype=np.float32)

    nc = build_nc()
    in_maps = []
    for c in range(NC):
        xc = np.ascontiguousarray(x[:, TL * c:TL * (c + 1), :])
        mk = np.zeros((128, 8), np.float32)
        mk[:, :c] = 1.0
        in_maps.append({
            "x": xc, "wt": wt, "b2": b2, "wq": Wq, "wk": Wk, "wv": Wv, "wg": Wg,
            "wo": Wo, "rbd": rbdm, "cw0": cw0b, "cw1": cw1b, "cosq": cosqm,
            "sins": sinsm, "idin": ident, "maskin": mk, "epsin": np.full((128, 1), EPS, np.float32),
            "wvt": np.ascontiguousarray(Wv.T),
        })
    res = run_bass_kernel_spmd(nc, in_maps, core_ids=list(range(NC)))
    out = np.concatenate([res.results[c]["y"] for c in range(NC)], axis=1)
    if DEBUG:
        kernel.debug = res.results
    return out


# version 16
# speedup vs baseline: 1.0664x; 1.0054x over previous
import numpy as np
import concourse.bass as bass
import concourse.mybir as mybir
from concourse.bass_utils import run_bass_kernel_spmd

F32 = mybir.dt.float32
BF16 = mybir.dt.bfloat16
AF = mybir.ActivationFunctionType
OP = mybir.AluOpType
AXX = mybir.AxisListType.X

NC = 8
B = 2
TL = 1024
C = 2048
DP = 4096
H = 16
HDK = 64
HDV = 128
NT = 16
NCH = 16
EPS = 1e-5
NSLAB = 4
SPOS = 256
DEBUG = False

ENGS = ["sync", "tensor", "vector", "scalar", "gpsimd"]


class Sem:
    def __init__(self, handle):
        self.h = handle
        self.n = 0

class Buf:
    def __init__(self, nslots):
        self.waits = [[] for _ in range(nslots)]
        self.k = nslots
    def acquire(self, em, eng, idx):
        idx %= self.k
        for (s, v) in self.waits[idx]:
            em.wait(eng, s, v)
        self.waits[idx] = []
        return idx
    def used_by(self, idx, sem, val):
        self.waits[idx % self.k].append((sem, val))

class Em:
    def __init__(self, name, eng):
        self.name = name
        self.e = eng
    def do(self, name, fn, inc=None):
        if self.name == name:
            inst = fn(self.e)
            if inc is not None:
                for s, k in inc:
                    inst.then_inc(s.h, k)
        if inc is not None:
            for s, k in inc:
                s.n += k
            return inc[0][0].n
        return None
    def wait(self, name, sem, val=None):
        v = sem.n if val is None else val
        if self.name == name:
            self.e.wait_ge(sem.h, v)
        return v
    def dma(self, name, out, in_, sem):
        return self.do(name, lambda e: e.dma_start(out=out, in_=in_), inc=[(sem, 16)])


def build_nc():
    nc = bass.Bass("TRN2", target_bir_lowering=False, detect_race_conditions=False)

    xin = nc.dram_tensor("x", [B, TL, C], F32, kind="ExternalInput").ap()
    wt = nc.dram_tensor("wt", [C, DP], F32, kind="ExternalInput").ap()
    b2 = nc.dram_tensor("b2", [128, 32], F32, kind="ExternalInput").ap()
    wq = nc.dram_tensor("wq", [C, 1024], F32, kind="ExternalInput").ap()
    wk = nc.dram_tensor("wk", [C, 1024], F32, kind="ExternalInput").ap()
    wv = nc.dram_tensor("wv", [C, 2048], F32, kind="ExternalInput").ap()
    wg = nc.dram_tensor("wg", [C, 2048], F32, kind="ExternalInput").ap()
    wo = nc.dram_tensor("wo", [C, 2048], F32, kind="ExternalInput").ap()
    rbd = nc.dram_tensor("rbd", [C, 32], F32, kind="ExternalInput").ap()
    cw0 = nc.dram_tensor("cw0", [128, C], F32, kind="ExternalInput").ap()
    cw1 = nc.dram_tensor("cw1", [128, C], F32, kind="ExternalInput").ap()
    cosq = nc.dram_tensor("cosq", [128, 1024], F32, kind="ExternalInput").ap()
    sins = nc.dram_tensor("sins", [128, 1024], F32, kind="ExternalInput").ap()
    idin = nc.dram_tensor("idin", [128, 128], F32, kind="ExternalInput").ap()
    maskin = nc.dram_tensor("maskin", [128, 8], F32, kind="ExternalInput").ap()
    wvt = nc.dram_tensor("wvt", [2048, C], F32, kind="ExternalInput").ap()
    epsin = nc.dram_tensor("epsin", [128, 1], F32, kind="ExternalInput").ap()
    yout = nc.dram_tensor("y", [B, TL, C], F32, kind="ExternalOutput").ap()

    gsp = nc.dram_tensor("gsp", [NCH, 128, 2 * TL], BF16).ap()
    vsp = nc.dram_tensor("vsp", [NCH, 128, 2 * TL], BF16).ap()
    wsc = nc.dram_tensor("wsc", [NCH, 128, 8192], BF16).ap()
    cc_in = nc.dram_tensor("cc_in", [2, 128, 32], F32)
    cc_out = nc.dram_tensor("cc_out", [2 * NC, 128, 32], F32)

    dbg = {}
    if DEBUG:
        for nm, shp in [("d_carry", [128, 32]), ("d_hT", [128, 8192]),
                        ("d_x1", [128, 8192]), ("d_uF", [128, 8192]),
                        ("d_q", [128, 4096]), ("d_k", [128, 4096]),
                        ("d_lg", [128, 128]), ("d_o0", [128, 2048]),
                        ("d_o1", [128, 2048]), ("d_mF", [128, 8192]),
                        ("d_out2", [128, 2048]), ("d_cf", [128, 256]),
                        ("d_gv", [128, 4096]), ("d_AS", [128, 32]),
                        ("d_BS", [128, 32]), ("d_stA", [128, 128]),
                        ("d_xlnF", [128, 32768]), ("d_xt", [128, 2048])]:
            dbg[nm] = nc.dram_tensor(nm, shp, F32, kind="ExternalOutput").ap()

    wsrc = [(wq, 0, 1024), (wk, 1024, 1024), (wv, 2048, 2048), (wg, 4096, 2048), (wo, 6144, 2048)]

    from contextlib import ExitStack
    with ExitStack() as stack:
        sb = lambda *a: stack.enter_context(nc.sbuf_tensor(*a))
        idb = sb("idb", [128, 128], BF16)
        cw0b = sb("cw0b", [128, C], BF16)
        cw1b = sb("cw1b", [128, C], BF16)
        cosb = sb("cosb", [128, 1024], BF16)
        sinb = sb("sinb", [128, 1024], BF16)
        rbdb = sb("rbdb", [128, 512], BF16)
        wvrb = sb("wvrb", [128, 512], BF16)
        carry = sb("carry", [128, 32], F32)
        epst = sb("epst", [128, 1], F32)
        pb = [stack.enter_context(nc.psum_tensor("pb%d" % i, [128, 512], F32)) for i in range(8)]
        def mksem(nm):
            return Sem(stack.enter_context(nc.semaphore(nm)))
        dCS = mksem("dCS"); dCG = mksem("dCG"); dCA = mksem("dCA")
        dX = [mksem("dX0"), mksem("dX1")]
        dWA = [mksem("dWA0"), mksem("dWA1")]
        dSP = [mksem("dSP0"), mksem("dSP1")]
        dCC = mksem("dCC")
        dGV = [mksem("dGV0"), mksem("dGV1")]
        dWS = [mksem("dWS0"), mksem("dWS1")]
        dOF = [mksem("dOF0"), mksem("dOF1")]
        dXB = [mksem("dXB0"), mksem("dXB1"), mksem("dXB2"), mksem("dXB3")]
        dWV = [mksem("dWV0"), mksem("dWV1")]
        dDB = mksem("dDB")
        dDBG = mksem("dDBG")
        cT = mksem("cT"); cV = mksem("cV"); cS = mksem("cS"); cG = mksem("cG")
        SEMS = [dCS, dCG, dCA] + dX + dWA + dSP + [dCC] + dGV + dWS + dOF + dXB + dWV + [dDB, dDBG, cT, cV, cS, cG]
        bankb = [Buf(1) for _ in range(8)]

        state = {}

        def snap():
            return [s.n for s in SEMS]

        def restore(v):
            for s, x in zip(SEMS, v):
                s.n = x

        def sched_A(em, t):
            for bk in bankb:
                bk.waits = [[]]
            em.dma("sync", t["maskt"][:, :], maskin, dCS)
            em.dma("sync", t["b2t"][:, :], b2, dCS)
            sy_const = em.dma("sync", epst[:, :], epsin, dCS)
            em.dma("gpsimd", idb[:, :], idin, dCG)
            em.dma("gpsimd", cosb[:, :], cosq, dCG)
            em.dma("gpsimd", sinb[:, :], sins, dCG)
            em.dma("gpsimd", cw0b[:, :], cw0, dCG)
            em.dma("gpsimd", cw1b[:, :], cw1, dCG)
            gp_const = em.dma("gpsimd",
                              rbdb[:, :].rearrange("p (a m) -> p a m", a=16),
                              rbd.rearrange("(a p) m -> p a m", p=128), dCG)


            def vstep(fn):
                ev = em.do("vector", fn, inc=[(cV, 1)])
                em.wait("vector", cV, ev)
                return ev

            cast_jobs = []
            for (csrc, col0, w) in wsrc:
                for ck in range(NCH):
                    cast_jobs.append((csrc, col0, w, ck))

            def emit_casts(njobs):
                for _ in range(njobs):
                    if not cast_jobs:
                        return
                    csrc, col0, w, ck = cast_jobs.pop(0)
                    em.dma("gpsimd", wsc[ck, :, col0:col0 + w],
                           csrc[128 * ck:128 * (ck + 1), :], dCA)

            xt = t["xt"]; xnb = t["xnb"]; xlnF = t["xlnF"]; sqd = t["sqd"]
            stA = t["stA"]
            xt_b = Buf(2); xnb_b = Buf(2); wvs_b = Buf(2)
            evA1 = [0] * NT

            em.do("vector", lambda e: e.memset(t["ones1"][:, :], 1.0))
            em.wait("vector", dCS, sy_const)
            em.wait("scalar", dCS, sy_const)
            em.wait("tensor", dCG, gp_const)

            for i in range(NT):
                bbi, qq = i // 8, i % 8
                s = xt_b.acquire(em, "sync", i)
                cx = em.dma("sync", xt[:, 2048 * s:2048 * (s + 1)],
                            xin[bbi, 128 * qq:128 * (qq + 1), :], dX[s])
                xts = xt[:, 2048 * s:2048 * (s + 1)]
                em.wait("vector", dX[s], cx)
                em.do("vector", lambda e, xts=xts, i=i: e.reduce_sum(stA[:, i:i + 1], xts, AXX))
                em.do("vector", lambda e, xts=xts, i=i: e.scalar_tensor_tensor(
                    sqd[:, :], xts, 1.0, xts, OP.bypass, OP.mult, accum_out=stA[:, 16 + i:17 + i]))
                em.do("vector", lambda e, i=i: e.tensor_single_scalar(stA[:, 32 + i:33 + i], stA[:, i:i + 1], 1.0 / C, OP.mult))
                em.do("vector", lambda e, i=i: e.tensor_single_scalar(stA[:, 48 + i:49 + i], stA[:, 16 + i:17 + i], 1.0 / C, OP.mult))
                vmu2 = em.do("vector", lambda e, i=i: e.tensor_mul(stA[:, 64 + i:65 + i], stA[:, 32 + i:33 + i], stA[:, 32 + i:33 + i]), inc=[(cV, 1)])
                em.wait("vector", cV, vmu2)
                v2 = em.do("vector", lambda e, i=i: e.tensor_sub(stA[:, 80 + i:81 + i], stA[:, 48 + i:49 + i], stA[:, 64 + i:65 + i]), inc=[(cV, 1)])
                em.wait("scalar", cV, v2)
                s2 = em.do("scalar", lambda e, i=i: e.activation(stA[:, 96 + i:97 + i], stA[:, 80 + i:81 + i], AF.Sqrt, bias=epst[:, 0:1]), inc=[(cS, 1)])
                em.wait("vector", cS, s2)
                vrc = em.do("vector", lambda e, i=i: e.reciprocal(stA[:, 112 + i:113 + i], stA[:, 96 + i:97 + i]), inc=[(cV, 1)])
                em.wait("vector", cV, vrc)
                sn = xnb_b.acquire(em, "vector", i)
                xnbs = xnb[:, 2048 * sn:2048 * (sn + 1)]
                v3 = em.do("vector", lambda e, xnbs=xnbs, xts=xts, i=i: e.tensor_scalar(
                    xnbs, xts, stA[:, 32 + i:33 + i], stA[:, 112 + i:113 + i], OP.subtract, OP.mult), inc=[(cV, 1)])
                xt_b.used_by(s, cV, v3)
                em.wait("tensor", cV, v3)
                for h2 in range(2):
                    bk = 4 + h2
                    bankb[bk].acquire(em, "tensor", 0)
                    pv = pb[bk][:, :].bitcast(BF16)
                    for k in range(8):
                        kk = 8 * h2 + k
                        last = em.do("tensor", lambda e, kk=kk, k=k, pv=pv, xnbs=xnbs: e.transpose(
                            pv[:, 128 * k:128 * (k + 1)], xnbs[:, 128 * kk:128 * (kk + 1)], idb[:, :]),
                            inc=[(cT, 1)] if k == 7 else None)
                    em.wait("scalar", cT, last)
                    dst = xlnF[:, :].rearrange("p (a m) -> p a m", a=16)[:, 8 * h2:8 * h2 + 8, 128 * i:128 * (i + 1)]
                    ve = em.do("scalar", lambda e, dst=dst, pv=pv: e.copy(
                        dst, pv.rearrange("p (a m) -> p a m", a=8)), inc=[(cS, 1)])
                    bankb[bk].used_by(0, cS, ve)
                xnb_b.used_by(sn, cS, ve)
                evA1[i] = ve
                j = i
                wsl = wvs_b.acquire(em, "gpsimd", j)
                cwv = em.dma("gpsimd",
                             t["wvstage"][:, 2048 * wsl:2048 * (wsl + 1)].rearrange("p (a m) -> p a m", a=16),
                             wvt[:, 128 * j:128 * (j + 1)].rearrange("(a p) m -> p a m", p=128), dWV[wsl])
                em.wait("tensor", dWV[wsl], cwv)
                bankb[7].acquire(em, "tensor", 0)
                for k in range(NCH):
                    mmw = em.do("tensor", lambda e, k=k, wsl=wsl: e.matmul(
                        pb[7][:, 0:32],
                        t["wvstage"][:, 2048 * wsl + 128 * k:2048 * wsl + 128 * (k + 1)],
                        rbdb[:, 32 * k:32 * (k + 1)],
                        start=(k == 0), stop=(k == NCH - 1)),
                        inc=[(cT, 1)] if k == NCH - 1 else None)
                wvs_b.used_by(wsl, cT, mmw)
                em.wait("scalar", cT, mmw)
                wvre = em.do("scalar", lambda e, j=j: e.copy(
                    wvrb[:, 32 * j:32 * (j + 1)], pb[7][:, 0:32]), inc=[(cS, 1)])
                bankb[7].used_by(0, cS, wvre)
                emit_casts(2)

            if DEBUG:
                em.wait("sync", cV, cV.n)
                em.wait("sync", cS, cS.n)
                em.dma("sync", dbg["d_stA"], stA[:, :], dDB)
                dxl = em.dma("sync", dbg["d_xt"], xt[:, 2048:4096], dDB)
                em.wait("gpsimd", cV, cV.n)
                dxf = em.dma("gpsimd", dbg["d_xlnF"], xlnF[:, :], dDBG)
                em.wait("vector", dDBG, dxf)
                em.wait("vector", dDB, dxl)
                em.wait("tensor", dDBG, dxf)

            wA = t["wA"]; gv = t["gv"]; hloc = t["hloc"]
            AS = t["AS"]; BS = t["BS"]
            wA_b = Buf(2)
            gv_b = Buf(2)
            se = ve2 = 0
            pend = []

            def emit_spill():
                jd, jpd, gtd, vtd, sed = pend.pop(0)
                em.wait("gpsimd", cS, sed)
                em.dma("gpsimd", gsp[jd, :, :], gtd, dSP[jpd])
                sp2d = em.dma("gpsimd", vsp[jd, :, :], vtd, dSP[jpd])
                gv_b.used_by(jpd, dSP[jpd], sp2d)
                emit_casts(3)

            for j in range(NCH):
                sl = wA_b.acquire(em, "gpsimd", j)
                for idx in range(2):
                    d = j if idx == 0 else 16 + j
                    cwl = em.dma("gpsimd",
                                 wA[:, 4096 * sl + 2048 * idx:4096 * sl + 2048 * (idx + 1)].rearrange("p (a m) -> p a m", a=16),
                                 wt[:, 128 * d:128 * (d + 1)].rearrange("(a p) m -> p a m", p=128), dWA[sl])
                if pend:
                    emit_spill()
                jp = gv_b.acquire(em, "scalar", j)
                for idx in range(2):
                    d = j if idx == 0 else 16 + j
                    em.wait("tensor", dWA[sl], cwl)
                    gvt = gv[:, 4096 * jp + 2048 * idx: 4096 * jp + 2048 * (idx + 1)]
                    for n in range(4):
                        em.wait("tensor", cS, evA1[4 * n + 3])
                        bk = n
                        bankb[bk].acquire(em, "tensor", 0)
                        for k in range(NCH):
                            last = em.do("tensor", lambda e, k=k, bk=bk, sl=sl, n=n, idx=idx: e.matmul(
                                pb[bk][:, :],
                                wA[:, 4096 * sl + 2048 * idx + 128 * k:4096 * sl + 2048 * idx + 128 * (k + 1)],
                                xlnF[:, 2048 * k + 512 * n:2048 * k + 512 * (n + 1)],
                                start=(k == 0), stop=(k == NCH - 1)),
                                inc=[(cT, 1)] if k == NCH - 1 else None)
                        em.wait("scalar", cT, last)
                        fn = AF.Sigmoid if idx == 0 else AF.Tanh
                        se = em.do("scalar", lambda e, gvt=gvt, n=n, bk=bk, fn=fn, d=d: e.activation(
                            gvt[:, 512 * n:512 * (n + 1)], pb[bk][:, :], fn,
                            bias=t["b2t"][:, d:d + 1]), inc=[(cS, 1)])
                        bankb[bk].used_by(0, cS, se)
                    wA_b.used_by(sl, cT, last)
                gt = gv[:, 4096 * jp:4096 * jp + 2048]
                vt = gv[:, 4096 * jp + 2048:4096 * jp + 4096]
                pend.append((j, jp, gt, vt, se))
                em.wait("vector", cS, se)
                em.do("vector", lambda e, gt=gt, vt=vt: e.tensor_tensor_scan(
                    hloc[:, 0:1024], gt[:, 0:1024], vt[:, 0:1024], 0.0, OP.mult, OP.add))
                vstep(lambda e, gt=gt, vt=vt: e.tensor_tensor_scan(
                    hloc[:, 1024:2048], gt[:, 1024:2048], vt[:, 1024:2048], 0.0, OP.mult, OP.add))
                em.do("vector", lambda e, j=j: e.tensor_copy(BS[:, j:j + 1], hloc[:, 1023:1024]))
                em.do("vector", lambda e, j=j: e.tensor_copy(BS[:, 16 + j:17 + j], hloc[:, 2047:2048]))
                vstep(lambda e, gt=gt: e.tensor_tensor_scan(
                    hloc[:, 0:1024], gt[:, 0:1024], t["ones1"][:, :], 1.0, OP.mult, OP.mult))
                em.do("vector", lambda e, j=j: e.tensor_copy(AS[:, j:j + 1], hloc[:, 1023:1024]))
                vstep(lambda e, gt=gt: e.tensor_tensor_scan(
                    hloc[:, 0:1024], gt[:, 1024:2048], t["ones1"][:, :], 1.0, OP.mult, OP.mult))
                ve2 = em.do("vector", lambda e, j=j: e.tensor_copy(AS[:, 16 + j:17 + j], hloc[:, 1023:1024]), inc=[(cV, 1)])
                gv_b.used_by(jp, cV, ve2)

            while pend:
                emit_spill()
            emit_casts(100)
            em.wait("sync", cV, ve2)
            em.dma("sync", cc_in.ap()[0], AS[:, :], dCC)
            ci2 = em.dma("sync", cc_in.ap()[1], BS[:, :], dCC)
            em.wait("gpsimd", dCC, ci2)
            state["a_preex"] = snap()
            cce = em.do("gpsimd", lambda e: e.collective_compute(
                "AllGather", OP.bypass, replica_groups=[list(range(NC))],
                ins=[cc_in.ap().opt()], outs=[cc_out.ap().opt()]), inc=[(cG, 1)])
            em.wait("sync", cG, cce)
            cro = em.dma("sync", t["ccAll"][:, :].rearrange("p (a m) -> p a m", a=16),
                         cc_out.ap().rearrange("a p m -> p a m"), dCC)
            em.wait("vector", dCC, cro)
            vstep(lambda e: e.memset(t["hcomb"][:, 0:32], 0.0))
            cur = 0
            tmpc = t["hcomb"][:, 64:96]
            for jj in range(NC - 1):
                A_j = t["ccAll"][:, 64 * jj:64 * jj + 32]
                B_j = t["ccAll"][:, 64 * jj + 32:64 * jj + 64]
                Hc = t["hcomb"][:, 32 * cur:32 * cur + 32]
                Hn = t["hcomb"][:, 32 * (1 - cur):32 * (1 - cur) + 32]
                vstep(lambda e, A_j=A_j, Hc=Hc: e.tensor_mul(tmpc, A_j, Hc))
                vstep(lambda e, B_j=B_j: e.tensor_add(tmpc, tmpc, B_j))
                vstep(lambda e, Hc=Hc: e.tensor_sub(tmpc, tmpc, Hc))
                vstep(lambda e, jj=jj, Hc=Hc, Hn=Hn: e.scalar_tensor_tensor(
                    Hn, tmpc, t["maskt"][:, jj:jj + 1], Hc, OP.mult, OP.add))
                cur = 1 - cur
            vfin = em.do("vector", lambda e, cur=cur: e.tensor_copy(
                carry[:, :], t["hcomb"][:, 32 * cur:32 * cur + 32]), inc=[(cV, 1)])
            if DEBUG:
                em.wait("sync", cV, vfin)
                em.dma("sync", dbg["d_carry"], carry[:, :], dDB)
                em.dma("sync", dbg["d_gv"], gv[:, 4096:8192], dDB)
                em.dma("sync", dbg["d_AS"], AS[:, :], dDB)
                em.dma("sync", dbg["d_BS"], BS[:, :], dDB)
            state["a_end"] = snap()

        def sched_B(em, t):
            for bk in bankb:
                bk.waits = [[]]
            ae = state["a_end"]
            ape = state["a_preex"]
            for w in ENGS:
                vals = ape if w == "gpsimd" else ae
                for s_, v_ in zip(SEMS, vals):
                    em.wait(w, s_, v_)

            def vstep(fn):
                ev = em.do("vector", fn, inc=[(cV, 1)])
                em.wait("vector", cV, ev)
                return ev

            hT = t["hT"]; x1a = t["x1"]; xbf = t["xbf"]
            SCR = t["SCR"]; SCR2 = t["SCR2"]
            o2a = t["o2a"]; xn1b = t["xn1b"]; vb = t["vb"]; gsb = t["gsb"]
            umb = t["umb"]; fF = t["fF"]; wS = t["wS"]
            qb = t["qb"]; kb = t["kb"]
            qr = t["SCR"][:, :].bitcast(BF16)[:, 2048:3072]
            qt = t["SCR"][:, :].bitcast(BF16)[:, 3072:4096]
            lg = t["lg"]; dcf = t["dcf"]; gvl = t["gvl"]; hFt = t["hFt"]
            st = t["st"]

            xbf_b = Buf(1)
            gvl_b = Buf(2)
            hFt_b = Buf(2)
            um_b = Buf(4)
            wS_b = Buf(2)
            outf_b = Buf(2)
            x1_b = Buf(2)
            hT_b = Buf(1)
            quad = [[0, 1, 2, 3], [4, 5, 6, 7]]
            wgi = [0]
            xli = [0]
            SL = {}

            def x1v(s):
                return x1a[:, 8192 * (s % 2):8192 * (s % 2 + 1)]

            def fF3(k, tt):
                return fF[:, :].rearrange("p (a m) -> p a m", a=16)[:, k, 128 * tt:128 * (tt + 1)]

            def run_group(wbase, col0, evac_fn, qd, ttwaits=None, srcf=None):
                if srcf is None:
                    srcf = fF3
                def emit_wdma(bt):
                    slw = wS_b.acquire(em, "gpsimd", wgi[0])
                    wgi[0] += 1
                    cwl = em.dma("gpsimd",
                                 wS[:, 2048 * slw:2048 * (slw + 1)].rearrange("p (a m) -> p a m", a=4),
                                 wsc[4 * bt:4 * (bt + 1), :, wbase + col0:wbase + col0 + 512].rearrange("a p m -> p a m"),
                                 dWS[slw])
                    return (slw, cwl)
                ch = [emit_wdma(0), emit_wdma(1)]
                for bt in range(4):
                    slw, cwl = ch[bt]
                    em.wait("tensor", dWS[slw], cwl)
                    for kk in range(4):
                        k = 4 * bt + kk
                        for tt in range(4):
                            if k == 0:
                                if ttwaits is not None:
                                    em.wait("tensor", cS, ttwaits[tt])
                                bankb[quad[qd][tt]].acquire(em, "tensor", 0)
                            last = em.do("tensor", lambda e, k=k, kk=kk, tt=tt, qd=qd, slw=slw, srcf=srcf: e.matmul(
                                pb[quad[qd][tt]][:, :], srcf(k, tt),
                                wS[:, 2048 * slw + 512 * kk:2048 * slw + 512 * (kk + 1)],
                                start=(k == 0), stop=(k == NCH - 1)),
                                inc=[(cT, 1)] if tt == 3 else None)
                    wS_b.used_by(slw, cT, last)
                    if bt + 2 < 4:
                        ch.append(emit_wdma(bt + 2))
                for tt in range(4):
                    evac_fn(tt, quad[qd][tt], last)

            def emit_B1(s):
                hT_b.acquire(em, "scalar", 0)
                hev = 0
                for j in range(NCH):
                    sl = gvl_b.acquire(em, "sync", j)
                    em.dma("sync", gvl[:, 1024 * sl:1024 * sl + 512],
                           gsp[j].rearrange("p (b t) -> p b t", b=2)[:, :, SPOS * s:SPOS * (s + 1)], dGV[sl])
                    g2 = em.dma("sync", gvl[:, 1024 * sl + 512:1024 * (sl + 1)],
                                vsp[j].rearrange("p (b t) -> p b t", b=2)[:, :, SPOS * s:SPOS * (s + 1)], dGV[sl])
                    em.wait("vector", dGV[sl], g2)
                    hsl = hFt_b.acquire(em, "vector", j)
                    hfv = hFt[:, 512 * hsl:512 * (hsl + 1)]
                    gg = gvl[:, 1024 * sl:1024 * sl + 512]
                    vv = gvl[:, 1024 * sl + 512:1024 * (sl + 1)]
                    vstep(lambda e, gg=gg, vv=vv, hfv=hfv, j=j: e.tensor_tensor_scan(
                        hfv[:, 0:256], gg[:, 0:256], vv[:, 0:256], carry[:, j:j + 1], OP.mult, OP.add))
                    em.do("vector", lambda e, hfv=hfv, j=j: e.tensor_copy(carry[:, j:j + 1], hfv[:, 255:256]))
                    vstep(lambda e, gg=gg, vv=vv, hfv=hfv, j=j: e.tensor_tensor_scan(
                        hfv[:, 256:512], gg[:, 256:512], vv[:, 256:512], carry[:, 16 + j:17 + j], OP.mult, OP.add))
                    vsc = em.do("vector", lambda e, hfv=hfv, j=j: e.tensor_copy(
                        carry[:, 16 + j:17 + j], hfv[:, 511:512]), inc=[(cV, 1)])
                    gvl_b.used_by(sl, cV, vsc)
                    em.wait("tensor", cV, vsc)
                    pbk = 6 + j % 2
                    bankb[pbk].acquire(em, "tensor", 0)
                    pvb = pb[pbk][:, :].bitcast(BF16)
                    for tt in range(4):
                        last = em.do("tensor", lambda e, tt=tt, pvb=pvb, hfv=hfv: e.transpose(
                            pvb[:, 128 * tt:128 * (tt + 1)], hfv[:, 128 * tt:128 * (tt + 1)], idb[:, :]),
                            inc=[(cT, 1)] if tt == 3 else None)
                    hFt_b.used_by(hsl, cT, last)
                    em.wait("scalar", cT, last)
                    dst = hT[:, :].rearrange("p (a m) -> p a m", a=4)[:, :, 128 * j:128 * (j + 1)]
                    hev = em.do("scalar", lambda e, dst=dst, pvb=pvb: e.copy(
                        dst, pvb[:, 0:512].rearrange("p (a m) -> p a m", a=4)), inc=[(cS, 1)])
                    bankb[pbk].used_by(0, cS, hev)
                SL[s] = {"hev": hev}

            def emit_P2(s):
                x1 = x1v(s)
                hev = SL[s]["hev"]
                x1_b.acquire(em, "vector", s)
                for tt in range(4):
                    em.wait("vector", cS, hev)
                    hblk = hT[:, 2048 * tt:2048 * (tt + 1)]
                    em.do("vector", lambda e, hblk=hblk, tt=tt: e.reduce_sum(st[:, tt:1 + tt], hblk, AXX))
                    em.do("vector", lambda e, hblk=hblk, tt=tt: e.scalar_tensor_tensor(
                        SCR[:, :], hblk, 1.0, hblk, OP.bypass, OP.mult, accum_out=st[:, 4 + tt:5 + tt]))
                vstep(lambda e: e.engine_nop())
                vstep(lambda e: e.tensor_single_scalar(st[:, 8:12], st[:, 0:4], 1.0 / C, OP.mult))
                vstep(lambda e: e.tensor_single_scalar(st[:, 12:16], st[:, 4:8], 1.0 / C, OP.mult))
                vstep(lambda e: e.tensor_mul(st[:, 16:20], st[:, 8:12], st[:, 8:12]))
                vst = em.do("vector", lambda e: e.tensor_sub(st[:, 12:16], st[:, 12:16], st[:, 16:20]), inc=[(cV, 1)])
                em.wait("scalar", cV, vst)
                sst = em.do("scalar", lambda e: e.activation(st[:, 16:20], st[:, 12:16], AF.Sqrt, bias=epst[:, 0:1]), inc=[(cS, 1)])
                em.wait("vector", cS, sst)
                vstep(lambda e: e.reciprocal(st[:, 20:24], st[:, 16:20]))
                for tt in range(4):
                    i = [2 * s, 2 * s + 1, 8 + 2 * s, 9 + 2 * s][tt]
                    bbi, qq = i // 8, i % 8
                    sx = xbf_b.acquire(em, "sync", xli[0])
                    xli[0] += 1
                    cx = em.dma("sync", xbf[:, 2048 * sx:2048 * (sx + 1)],
                                xin[bbi, 128 * qq:128 * (qq + 1), :], dXB[sx])
                    em.wait("vector", dXB[sx], cx)
                    hblk = hT[:, 2048 * tt:2048 * (tt + 1)]
                    x1blk = x1[:, 2048 * tt:2048 * (tt + 1)]
                    em.do("vector", lambda e, hblk=hblk, tt=tt: e.tensor_scalar(
                        SCR[:, :], hblk, st[:, 8 + tt:9 + tt], st[:, 20 + tt:21 + tt], OP.subtract, OP.mult))
                    vx1 = em.do("vector", lambda e, x1blk=x1blk, sx=sx, tt=tt: e.scalar_tensor_tensor(
                        x1blk, SCR[:, :], 1.0, xbf[:, 2048 * sx:2048 * (sx + 1)], OP.bypass, OP.add,
                        accum_out=st[:, 24 + tt:25 + tt]), inc=[(cV, 1)])
                    xbf_b.used_by(sx, cV, vx1)
                    em.do("vector", lambda e, x1blk=x1blk, tt=tt: e.scalar_tensor_tensor(
                        SCR[:, :], x1blk, 1.0, x1blk, OP.bypass, OP.mult, accum_out=st[:, 28 + tt:29 + tt]))
                hT_b.used_by(0, cV, cV.n)
                if DEBUG and s == 0:
                    em.wait("gpsimd", cV, cV.n)
                    dh = em.dma("gpsimd", dbg["d_hT"], hT[:, :], dDBG)
                    em.wait("vector", dDBG, dh)
                vstep(lambda e: e.engine_nop())
                vstep(lambda e: e.tensor_single_scalar(st[:, 32:36], st[:, 24:28], 1.0 / C, OP.mult))
                vstep(lambda e: e.tensor_single_scalar(st[:, 36:40], st[:, 28:32], 1.0 / C, OP.mult))
                vstep(lambda e: e.tensor_mul(st[:, 40:44], st[:, 32:36], st[:, 32:36]))
                vst2 = em.do("vector", lambda e: e.tensor_sub(st[:, 36:40], st[:, 36:40], st[:, 40:44]), inc=[(cV, 1)])
                em.wait("scalar", cV, vst2)
                sst2 = em.do("scalar", lambda e: e.activation(st[:, 40:44], st[:, 36:40], AF.Sqrt, bias=epst[:, 0:1]), inc=[(cS, 1)])
                em.wait("vector", cS, sst2)
                vstep(lambda e: e.reciprocal(st[:, 44:48], st[:, 40:44]))

            def emit_P3P4(s):
                x1 = x1v(s)
                outf_b.acquire(em, "vector", 0)
                outf_b.acquire(em, "vector", 1)
                uF_evs = [0] * 4
                prev_silu = None
                for tt in range(4):
                    x1blk = x1[:, 2048 * tt:2048 * (tt + 1)]
                    if prev_silu is not None:
                        em.wait("vector", cS, prev_silu)
                    if tt < 2:
                        em.do("vector", lambda e, x1blk=x1blk, tt=tt: e.tensor_scalar(
                            xn1b[:, 2048 * tt:2048 * (tt + 1)], x1blk, st[:, 32 + tt:33 + tt], st[:, 44 + tt:45 + tt], OP.subtract, OP.mult))
                        vcv = em.do("vector", lambda e, tt=tt: e.tensor_mul(
                            SCR2[:, :], xn1b[:, 2048 * tt:2048 * (tt + 1)], cw1b[:, :]), inc=[(cV, 1)])
                    else:
                        em.do("vector", lambda e, x1blk=x1blk, tt=tt: e.tensor_scalar(
                            SCR[:, :], x1blk, st[:, 32 + tt:33 + tt], st[:, 44 + tt:45 + tt], OP.subtract, OP.mult))
                        em.do("vector", lambda e: e.tensor_mul(SCR2[:, :], SCR[:, :], cw1b[:, :]))
                        em.do("vector", lambda e, tt=tt: e.tensor_mul(
                            SCR[:, :], xn1b[:, 2048 * (tt - 2):2048 * (tt - 1)], cw0b[:, :]))
                        vcv = em.do("vector", lambda e: e.tensor_add(SCR2[:, :], SCR2[:, :], SCR[:, :]), inc=[(cV, 1)])
                    usl = um_b.acquire(em, "scalar", umi[0])
                    umi[0] += 1
                    em.wait("scalar", cV, vcv)
                    sustp = em.do("scalar", lambda e, usl=usl: e.activation(
                        umb[:, 2048 * usl:2048 * (usl + 1)], SCR2[:, :], AF.Silu), inc=[(cS, 1)])
                    prev_silu = sustp
                    em.wait("tensor", cS, sustp)
                    for h2 in range(2):
                        bk = 4 + h2
                        bankb[bk].acquire(em, "tensor", 0)
                        pv = pb[bk][:, :].bitcast(BF16)
                        for k in range(8):
                            kk = 8 * h2 + k
                            last = em.do("tensor", lambda e, pv=pv, k=k, kk=kk, usl=usl: e.transpose(
                                pv[:, 128 * k:128 * (k + 1)], umb[:, 2048 * usl + 128 * kk:2048 * usl + 128 * (kk + 1)], idb[:, :]),
                                inc=[(cT, 1)] if k == 7 else None)
                        em.wait("scalar", cT, last)
                        dst = fF[:, :].rearrange("p (a m) -> p a m", a=16)[:, 8 * h2:8 * h2 + 8, 128 * tt:128 * (tt + 1)]
                        uev = em.do("scalar", lambda e, dst=dst, pv=pv: e.copy(
                            dst, pv.rearrange("p (a m) -> p a m", a=8)), inc=[(cS, 1)])
                        bankb[bk].used_by(0, cS, uev)
                    um_b.used_by(usl, cT, last)
                    uF_evs[tt] = uev
                SL[s]["uF_evs"] = uF_evs
                SL[s]["silu_last"] = prev_silu

            def emit_logits(s):
                uF_evs = SL[s]["uF_evs"]
                lgp = lg[:, 128 * (s % 2):128 * (s % 2 + 1)]
                for tt in range(4):
                    em.wait("tensor", cS, uF_evs[tt])
                    bankb[2].acquire(em, "tensor", 0)
                    for k in range(NCH):
                        last = em.do("tensor", lambda e, k=k, tt=tt: e.matmul(
                            pb[2][:, 0:32], fF3(k, tt), wvrb[:, 32 * k:32 * (k + 1)],
                            start=(k == 0), stop=(k == NCH - 1)),
                            inc=[(cT, 1)] if k == NCH - 1 else None)
                    em.wait("scalar", cT, last)
                    lev = em.do("scalar", lambda e, tt=tt, lgp=lgp: e.copy(
                        lgp[:, 32 * tt:32 * (tt + 1)], pb[2][:, 0:32]), inc=[(cS, 1)])
                    bankb[2].used_by(0, cS, lev)
                SL[s]["lg_ev"] = lev

            def emit_qkvg(s):
                prev = SL.get(s - 1, {})
                def ev_q(tt, bkk, last, gi2):
                    dste = qb[:, 1024 * tt + 512 * gi2:1024 * tt + 512 * (gi2 + 1)]
                    em.wait("scalar", cT, last)
                    if "qk_read" in prev:
                        em.wait("scalar", cV, prev["qk_read"])
                    ee = em.do("scalar", lambda e, dste=dste, bkk=bkk: e.activation(
                        dste, pb[bkk][:, :], AF.Silu), inc=[(cS, 1)])
                    bankb[bkk].used_by(0, cS, ee)
                def ev_k(tt, bkk, last, gi2):
                    dste = kb[:, 1024 * tt + 512 * gi2:1024 * tt + 512 * (gi2 + 1)]
                    em.wait("scalar", cT, last)
                    if "qk_read" in prev:
                        em.wait("scalar", cV, prev["qk_read"])
                    ee = em.do("scalar", lambda e, dste=dste, bkk=bkk: e.copy(
                        dste, pb[bkk][:, :]), inc=[(cS, 1)])
                    bankb[bkk].used_by(0, cS, ee)
                gidx = 0
                for (wap, ngr, evf) in [(0, 2, ev_q), (1024, 2, ev_k)]:
                    for gi2 in range(ngr):
                        run_group(wap, 512 * gi2,
                                  lambda tt, bkk, last, gi2=gi2, evf=evf: evf(tt, bkk, last, gi2),
                                  gidx % 2)
                        gidx += 1
                SL[s]["qk_S"] = cS.n

            def emit_qkvg_vg(s):
                prev = SL.get(s - 1, {})
                def ev_v(tt, bkk, last, gi2):
                    dste = vb[:, 2048 * tt + 512 * gi2:2048 * tt + 512 * (gi2 + 1)]
                    em.wait("scalar", cT, last)
                    if "v_read" in prev:
                        em.wait("scalar", cV, prev["v_read"])
                    ee = em.do("scalar", lambda e, dste=dste, bkk=bkk: e.copy(
                        dste, pb[bkk][:, :]), inc=[(cS, 1)])
                    bankb[bkk].used_by(0, cS, ee)
                def ev_g(tt, bkk, last, gi2):
                    dste = gsb[:, 2048 * tt + 512 * gi2:2048 * tt + 512 * (gi2 + 1)]
                    em.wait("scalar", cT, last)
                    if "g_read" in prev:
                        em.wait("scalar", cV, prev["g_read"])
                    ee = em.do("scalar", lambda e, dste=dste, bkk=bkk: e.activation(
                        dste, pb[bkk][:, :], AF.Silu), inc=[(cS, 1)])
                    bankb[bkk].used_by(0, cS, ee)
                gidx = 4
                for (wap, ngr, evf) in [(2048, 4, ev_v), (4096, 4, ev_g)]:
                    for gi2 in range(ngr):
                        run_group(wap, 512 * gi2,
                                  lambda tt, bkk, last, gi2=gi2, evf=evf: evf(tt, bkk, last, gi2),
                                  gidx % 2)
                        gidx += 1
                SL[s]["vg_S"] = cS.n
                if DEBUG and s == 0:
                    em.wait("gpsimd", cS, SL[s]["qk_S"])
                    em.dma("gpsimd", dbg["d_uF"], fF[:, :], dDBG)
                    em.dma("gpsimd", dbg["d_q"], qb[:, :], dDBG)
                    duk = em.dma("gpsimd", dbg["d_k"], kb[:, :], dDBG)
                    dlg = em.dma("gpsimd", dbg["d_lg"], lg[:, 0:128], dDBG)
                    em.wait("tensor", dDBG, dlg)

            umi = [0]

            def emit_mid_v(s):
                mevs = [0] * 4
                uslmap = {}
                vmes = {}
                o2abf = o2a[:, :].bitcast(BF16)
                lgp = lg[:, 128 * (s % 2):128 * (s % 2 + 1)]
                r32 = lambda ap: ap.rearrange("p (h two d) -> p h two d", two=2, d=32)
                r3 = lambda ap: ap.rearrange("p (h d) -> p h d", h=16)
                em.wait("vector", cS, SL[s]["qk_S"])
                for pi in range(2):
                    tb0, tb1 = pi, 2 + pi
                    dc = 256 * pi
                    qsrc = qb[:, 1024 * tb1:1024 * (tb1 + 1)]
                    em.do("vector", lambda e, qsrc=qsrc: e.tensor_mul(qr, qsrc, cosb[:, :]))
                    em.do("vector", lambda e, qsrc=qsrc: e.tensor_mul(
                        r32(qt)[:, :, 0, :], r32(qsrc)[:, :, 1, :], r32(sinb[:, :])[:, :, 0, :]))
                    vstep(lambda e, qsrc=qsrc: e.tensor_mul(
                        r32(qt)[:, :, 1, :], r32(qsrc)[:, :, 0, :], r32(sinb[:, :])[:, :, 1, :]))
                    em.do("vector", lambda e: e.tensor_add(qr, qr, qt))
                    DOT = SCR
                    em.do("vector", lambda e, tb0=tb0: e.tensor_mul(
                        DOT[:, 0:1024], qb[:, 1024 * tb0:1024 * (tb0 + 1)], kb[:, 1024 * tb0:1024 * (tb0 + 1)]))
                    em.do("vector", lambda e, dc=dc: e.tensor_reduce(dcf[:, dc:dc + 16], r3(DOT[:, 0:1024]), AXX, OP.add))
                    em.do("vector", lambda e, tb0=tb0: e.tensor_mul(
                        DOT[:, 0:1024], qr, kb[:, 1024 * tb0:1024 * (tb0 + 1)]))
                    em.do("vector", lambda e, dc=dc: e.tensor_reduce(dcf[:, dc + 16:dc + 32], r3(DOT[:, 0:1024]), AXX, OP.add))
                    em.do("vector", lambda e, tb1=tb1: e.tensor_mul(
                        DOT[:, 0:1024], qb[:, 1024 * tb1:1024 * (tb1 + 1)], kb[:, 1024 * tb1:1024 * (tb1 + 1)]))
                    em.do("vector", lambda e, dc=dc: e.tensor_reduce(dcf[:, dc + 32:dc + 48], r3(DOT[:, 0:1024]), AXX, OP.add))
                SL[s]["qk_read"] = em.do("vector", lambda e: e.engine_nop(), inc=[(cV, 1)])
                for pi in range(2):
                    tb0, tb1 = pi, 2 + pi
                    dc = 256 * pi
                    L0 = lgp[:, 32 * tb0:32 * (tb0 + 1)]
                    L1 = lgp[:, 32 * tb1:32 * (tb1 + 1)]
                    vstep(lambda e, L0=L0, dc=dc: e.tensor_sub(dcf[:, dc + 48:dc + 64], L0[:, 0:16], L0[:, 16:32]))
                    vstep(lambda e, L1=L1, dc=dc: e.tensor_sub(dcf[:, dc + 64:dc + 80], L1[:, 0:16], L1[:, 16:32]))
                    vstep(lambda e, dc=dc: e.tensor_single_scalar(dcf[:, dc + 80:dc + 96], dcf[:, dc + 48:dc + 64], -1.0, OP.mult))
                    vstep(lambda e, dc=dc: e.tensor_max(dcf[:, dc + 80:dc + 96], dcf[:, dc + 80:dc + 96], dcf[:, dc + 48:dc + 64]))
                    vstep(lambda e, dc=dc: e.tensor_single_scalar(dcf[:, dc + 96:dc + 112], dcf[:, dc + 64:dc + 80], -1.0, OP.mult))
                    vmx = em.do("vector", lambda e, dc=dc: e.tensor_max(dcf[:, dc + 96:dc + 112], dcf[:, dc + 96:dc + 112], dcf[:, dc + 64:dc + 80]), inc=[(cV, 1)])
                    em.wait("scalar", cV, vmx)
                    ssg = em.do("scalar", lambda e, dc=dc: e.activation(dcf[:, dc + 112:dc + 144], dcf[:, dc + 80:dc + 112], AF.Tanh, scale=0.5), inc=[(cS, 1)])
                    em.wait("vector", cS, ssg)
                    vstep(lambda e, dc=dc: e.tensor_scalar(dcf[:, dc + 112:dc + 144], dcf[:, dc + 112:dc + 144], 0.5, 0.5, OP.mult, OP.add))
                    vstep(lambda e, dc=dc: e.tensor_single_scalar(dcf[:, dc + 144:dc + 160], dcf[:, dc + 48:dc + 64], 0.0, OP.is_ge))
                    vstep(lambda e, dc=dc: e.tensor_single_scalar(dcf[:, dc + 160:dc + 176], dcf[:, dc + 64:dc + 80], 0.0, OP.is_ge))
                    vstep(lambda e, dc=dc: e.tensor_mul(dcf[:, dc + 176:dc + 192], dcf[:, dc + 144:dc + 160], dcf[:, dc + 160:dc + 176]))
                    vstep(lambda e, dc=dc: e.tensor_add(dcf[:, dc + 144:dc + 160], dcf[:, dc + 144:dc + 160], dcf[:, dc + 160:dc + 176]))
                    vstep(lambda e, dc=dc: e.tensor_single_scalar(dcf[:, dc + 176:dc + 192], dcf[:, dc + 176:dc + 192], 2.0, OP.mult))
                    vstep(lambda e, dc=dc: e.tensor_sub(dcf[:, dc + 176:dc + 192], dcf[:, dc + 176:dc + 192], dcf[:, dc + 144:dc + 160]))
                    vstep(lambda e, dc=dc: e.tensor_single_scalar(dcf[:, dc + 176:dc + 192], dcf[:, dc + 176:dc + 192], 1.0, OP.add))
                    vstep(lambda e, dc=dc: e.tensor_mul(dcf[:, dc + 192:dc + 208], dcf[:, dc + 112:dc + 128], dcf[:, dc:dc + 16]))
                    vstep(lambda e, dc=dc: e.tensor_single_scalar(dcf[:, dc + 192:dc + 208], dcf[:, dc + 192:dc + 208], 0.125, OP.mult))
                    vstep(lambda e, dc=dc: e.tensor_mul(dcf[:, dc + 208:dc + 224], dcf[:, dc + 128:dc + 144], dcf[:, dc + 16:dc + 32]))
                    vstep(lambda e, dc=dc: e.tensor_mul(dcf[:, dc + 208:dc + 224], dcf[:, dc + 208:dc + 224], dcf[:, dc + 176:dc + 192]))
                    vstep(lambda e, dc=dc: e.tensor_single_scalar(dcf[:, dc + 208:dc + 224], dcf[:, dc + 208:dc + 224], 0.0625, OP.mult))
                    vstep(lambda e, dc=dc: e.tensor_scalar(dcf[:, dc + 240:dc + 256], dcf[:, dc + 176:dc + 192], -0.5, 1.0, OP.mult, OP.add))
                    vstep(lambda e, dc=dc: e.tensor_mul(dcf[:, dc + 224:dc + 240], dcf[:, dc + 128:dc + 144], dcf[:, dc + 32:dc + 48]))
                    vstep(lambda e, dc=dc: e.tensor_mul(dcf[:, dc + 224:dc + 240], dcf[:, dc + 224:dc + 240], dcf[:, dc + 240:dc + 256]))
                    vstep(lambda e, dc=dc: e.tensor_single_scalar(dcf[:, dc + 224:dc + 240], dcf[:, dc + 224:dc + 240], 0.125, OP.mult))
                em.wait("vector", cS, SL[s]["vg_S"])
                if s + 1 in SL and "silu_last" in SL[s + 1]:
                    em.wait("vector", cS, SL[s + 1]["silu_last"])
                outf_b.acquire(em, "vector", 0)
                r3v = r3
                bc = lambda ap: ap.broadcast_to((128, 16, 128))
                sq = SCR2[:, :]
                for pi in range(2):
                    tb0, tb1 = pi, 2 + pi
                    dc = 256 * pi
                    o0 = o2abf[:, 4096 * pi:4096 * pi + 2048]
                    o1 = o2abf[:, 4096 * pi + 2048:4096 * (pi + 1)]
                    vb0 = vb[:, 2048 * tb0:2048 * (tb0 + 1)]
                    vb1 = vb[:, 2048 * tb1:2048 * (tb1 + 1)]
                    em.do("vector", lambda e, o0=o0, vb0=vb0, dc=dc: e.tensor_tensor(
                        r3v(o0), r3v(vb0), bc(dcf[:, dc + 192:dc + 208]), OP.mult))
                    em.do("vector", lambda e, sq=sq, vb0=vb0, dc=dc: e.tensor_tensor(
                        r3v(sq), r3v(vb0), bc(dcf[:, dc + 208:dc + 224]), OP.mult))
                    em.do("vector", lambda e, o1=o1, vb1=vb1, dc=dc: e.tensor_tensor(
                        r3v(o1), r3v(vb1), bc(dcf[:, dc + 224:dc + 240]), OP.mult))
                    em.do("vector", lambda e, o1=o1, sq=sq: e.tensor_add(o1, o1, sq))
                SL[s]["v_read"] = em.do("vector", lambda e: e.engine_nop(), inc=[(cV, 1)])
                for pi in range(2):
                    tb0, tb1 = pi, 2 + pi
                    o0 = o2abf[:, 4096 * pi:4096 * pi + 2048]
                    o1 = o2abf[:, 4096 * pi + 2048:4096 * (pi + 1)]
                    vstep(lambda e, o0=o0: e.tensor_reduce(st[:, 48:64], r3(o0), AXX, OP.add))
                    vstep(lambda e, o0=o0, sq=sq: e.scalar_tensor_tensor(sq, o0, 1.0, o0, OP.bypass, OP.mult))
                    vstep(lambda e, sq=sq: e.tensor_reduce(st[:, 64:80], r3(sq), AXX, OP.add))
                    vstep(lambda e, o1=o1: e.tensor_reduce(st[:, 80:96], r3(o1), AXX, OP.add))
                    vstep(lambda e, o1=o1, sq=sq: e.scalar_tensor_tensor(sq, o1, 1.0, o1, OP.bypass, OP.mult))
                    vstep(lambda e, sq=sq: e.tensor_reduce(st[:, 96:112], r3(sq), AXX, OP.add))
                    vstep(lambda e: e.tensor_single_scalar(st[:, 48:64], st[:, 48:64], 1.0 / HDV, OP.mult))
                    vstep(lambda e: e.tensor_single_scalar(st[:, 64:80], st[:, 64:80], 1.0 / HDV, OP.mult))
                    vstep(lambda e: e.tensor_single_scalar(st[:, 80:96], st[:, 80:96], 1.0 / HDV, OP.mult))
                    vstep(lambda e: e.tensor_single_scalar(st[:, 96:112], st[:, 96:112], 1.0 / HDV, OP.mult))
                    vstep(lambda e: e.tensor_mul(st[:, 112:128], st[:, 48:64], st[:, 48:64]))
                    vstep(lambda e: e.tensor_sub(st[:, 64:80], st[:, 64:80], st[:, 112:128]))
                    vstep(lambda e: e.tensor_mul(st[:, 112:128], st[:, 80:96], st[:, 80:96]))
                    vhl = em.do("vector", lambda e: e.tensor_sub(st[:, 96:112], st[:, 96:112], st[:, 112:128]), inc=[(cV, 1)])
                    em.wait("scalar", cV, vhl)
                    em.do("scalar", lambda e: e.activation(st[:, 128:144], st[:, 64:80], AF.Sqrt, bias=epst[:, 0:1]), inc=[(cS, 1)])
                    shl2 = em.do("scalar", lambda e: e.activation(st[:, 144:160], st[:, 96:112], AF.Sqrt, bias=epst[:, 0:1]), inc=[(cS, 1)])
                    em.wait("vector", cS, shl2)
                    vstep(lambda e: e.reciprocal(st[:, 112:128], st[:, 128:144]))
                    vstep(lambda e: e.reciprocal(st[:, 128:144], st[:, 144:160]))
                    for (oo, tb, mu0, rs0) in [(o0, tb0, 48, 112), (o1, tb1, 80, 128)]:
                        usl = um_b.acquire(em, "vector", umi[0])
                        umi[0] += 1
                        uslmap[tb] = usl
                        for h in range(H):
                            hs0 = slice(128 * h, 128 * (h + 1))
                            em.do("vector", lambda e, oo=oo, h=h, hs0=hs0, mu0=mu0, rs0=rs0: e.tensor_scalar(
                                SCR[:, hs0], oo[:, hs0], st[:, mu0 + h:mu0 + h + 1], st[:, rs0 + h:rs0 + h + 1], OP.subtract, OP.mult))
                        vme = em.do("vector", lambda e, usl=usl, tb=tb: e.tensor_mul(
                            umb[:, 2048 * usl:2048 * (usl + 1)], SCR[:, :], gsb[:, 2048 * tb:2048 * (tb + 1)]), inc=[(cV, 1)])
                        vmes[tb] = vme
                SL[s]["g_read"] = vme
                SL[s]["uslmap"] = uslmap
                SL[s]["vmes"] = vmes

            def emit_mid_t(s):
                mevs = [0] * 4
                for tb in [0, 2, 1, 3]:
                    usl = SL[s]["uslmap"][tb]
                    em.wait("tensor", cV, SL[s]["vmes"][tb])
                    for h2 in range(2):
                        bk = 4 + h2
                        bankb[bk].acquire(em, "tensor", 0)
                        pv = pb[bk][:, :].bitcast(BF16)
                        for k in range(8):
                            kk = 8 * h2 + k
                            last = em.do("tensor", lambda e, pv=pv, k=k, kk=kk, usl=usl: e.transpose(
                                pv[:, 128 * k:128 * (k + 1)], umb[:, 2048 * usl + 128 * kk:2048 * usl + 128 * (kk + 1)], idb[:, :]),
                                inc=[(cT, 1)] if k == 7 else None)
                        em.wait("scalar", cT, last)
                        dst = hT[:, :].rearrange("p (a m) -> p a m", a=16)[:, 8 * h2:8 * h2 + 8, 128 * tb:128 * (tb + 1)]
                        mev = em.do("scalar", lambda e, dst=dst, pv=pv: e.copy(
                            dst, pv.rearrange("p (a m) -> p a m", a=8)), inc=[(cS, 1)])
                        bankb[bk].used_by(0, cS, mev)
                    um_b.used_by(usl, cT, last)
                    mevs[tb] = mev
                SL[s]["mevs"] = mevs

            def emit_wout(s):
                oevs = [0] * 4
                def ev_o(tt, bkk, last, gi2):
                    o2blk = o2a[:, :].bitcast(BF16)[:, 2048 * tt + 512 * gi2:2048 * tt + 512 * (gi2 + 1)]
                    em.wait("scalar", cT, last)
                    ee = em.do("scalar", lambda e, o2blk=o2blk, bkk=bkk: e.activation(
                        o2blk, pb[bkk][:, :], AF.Copy, accum_out=st[:, 160 + 4 * tt + gi2:161 + 4 * tt + gi2]), inc=[(cS, 1)])
                    bankb[bkk].used_by(0, cS, ee)
                    if gi2 == 3:
                        oevs[tt] = ee
                hTm = lambda k, tt: hT[:, :].rearrange("p (a m) -> p a m", a=16)[:, k, 128 * tt:128 * (tt + 1)]
                for gi2 in range(4):
                    run_group(6144, 512 * gi2,
                              lambda tt, bkk, last, gi2=gi2: ev_o(tt, bkk, last, gi2),
                              gi2 % 2, ttwaits=SL[s]["mevs"], srcf=hTm)
                hT_b.used_by(0, cT, cT.n)
                SL[s]["oevs"] = oevs

            def emit_fin(s):
                x1 = x1v(s)
                o2bf = o2a[:, :].bitcast(BF16)
                outf_slots = [SCR2[:, :], xn1b[:, :].bitcast(F32)[:, 0:2048]]
                vfo = 0
                for tt in range(4):
                    i = [2 * s, 2 * s + 1, 8 + 2 * s, 9 + 2 * s][tt]
                    bbi, qq = i // 8, i % 8
                    o2t = o2bf[:, 2048 * tt:2048 * (tt + 1)]
                    em.wait("vector", cS, SL[s]["oevs"][tt])
                    em.do("vector", lambda e, o2t=o2t, tt=tt: e.reduce_sum(st[:, 176 + tt:177 + tt], o2t, AXX))
                    vstep(lambda e, o2t=o2t, tt=tt: e.scalar_tensor_tensor(
                        SCR[:, :], o2t, 1.0, o2t, OP.bypass, OP.mult,
                        accum_out=st[:, 180 + tt:181 + tt]))
                    vstep(lambda e, tt=tt: e.tensor_single_scalar(st[:, 184 + tt:185 + tt], st[:, 176 + tt:177 + tt], 1.0 / C, OP.mult))
                    vstep(lambda e, tt=tt: e.tensor_single_scalar(st[:, 188 + tt:189 + tt], st[:, 180 + tt:181 + tt], 1.0 / C, OP.mult))
                    vstep(lambda e, tt=tt: e.tensor_mul(st[:, 200 + tt:201 + tt], st[:, 184 + tt:185 + tt], st[:, 184 + tt:185 + tt]))
                    vv2 = em.do("vector", lambda e, tt=tt: e.tensor_sub(st[:, 188 + tt:189 + tt], st[:, 188 + tt:189 + tt], st[:, 200 + tt:201 + tt]), inc=[(cV, 1)])
                    em.wait("scalar", cV, vv2)
                    sq2 = em.do("scalar", lambda e, tt=tt: e.activation(st[:, 192 + tt:193 + tt], st[:, 188 + tt:189 + tt], AF.Sqrt, bias=epst[:, 0:1]), inc=[(cS, 1)])
                    em.wait("vector", cS, sq2)
                    vstep(lambda e, tt=tt: e.reciprocal(st[:, 196 + tt:197 + tt], st[:, 192 + tt:193 + tt]))
                    fs = outf_b.acquire(em, "vector", tt)
                    fblk = outf_slots[fs]
                    em.do("vector", lambda e, o2t=o2t, tt=tt: e.tensor_scalar(
                        SCR[:, :], o2t, st[:, 184 + tt:185 + tt], st[:, 196 + tt:197 + tt], OP.subtract, OP.mult))
                    vfo = em.do("vector", lambda e, fblk=fblk, tt=tt, x1=x1: e.tensor_add(
                        fblk, SCR[:, :], x1[:, 2048 * tt:2048 * (tt + 1)]), inc=[(cV, 1)])
                    em.wait("sync", cV, vfo)
                    cst = em.dma("sync", yout[bbi, 128 * qq:128 * (qq + 1), :], fblk, dOF[fs])
                    outf_b.used_by(fs, dOF[fs], cst)
                x1_b.used_by(s % 2, cV, vfo)
                if DEBUG and s == 0:
                    em.wait("gpsimd", cV, cV.n)
                    em.wait("gpsimd", cS, cS.n)
                    em.dma("gpsimd", dbg["d_x1"], x1[:, :], dDBG)
                    dlast = em.dma("gpsimd", dbg["d_out2"], o2bf[:, 0:2048], dDBG)
                    em.wait("vector", dDBG, dlast)
                    em.wait("scalar", dDBG, dlast)
                    em.wait("tensor", dDBG, dlast)

            emit_B1(0)
            emit_P2(0)
            emit_P3P4(0)
            emit_logits(0)
            emit_qkvg(0)
            emit_qkvg_vg(0)
            emit_B1(1)
            emit_P2(1)
            emit_P3P4(1)
            emit_logits(1)
            for s in range(NSLAB):
                emit_mid_v(s)
                if s + 1 < NSLAB:
                    emit_qkvg(s + 1)
                emit_mid_t(s)
                emit_wout(s)
                if s + 2 < NSLAB:
                    emit_B1(s + 2)
                emit_fin(s)
                if s + 2 < NSLAB:
                    emit_P2(s + 2)
                if s + 1 < NSLAB:
                    emit_qkvg_vg(s + 1)
                if s + 2 < NSLAB:
                    emit_P3P4(s + 2)
                    emit_logits(s + 2)
            em.wait("sync", dOF[0], dOF[0].n)
            em.wait("sync", dOF[1], dOF[1].n)
            if DEBUG:
                em.wait("gpsimd", dDBG, dDBG.n)

        with ExitStack() as stA_alloc:
            sa = lambda *a: stA_alloc.enter_context(nc.sbuf_tensor(*a))
            tA = {"b2t": sa("b2t", [128, 32], F32),
                  "xt": sa("xtA", [128, 2 * 2048], F32),
                  "xnb": sa("xnbA", [128, 2 * 2048], BF16),
                  "xlnF": sa("xlnFA", [128, 16 * 2048], BF16),
                  "sqd": sa("sqdA", [128, 2048], F32),
                  "stA": sa("stAA", [128, 128], F32),
                  "wA": sa("wAA", [128, 4 * 2048], BF16),
                  "gv": sa("gvA", [128, 2 * 4096], F32),
                  "hloc": sa("hlocA", [128, 2048], F32),
                  "ones1": sa("ones1A", [128, 1024], F32),
                  "wvstage": sa("wvstageA", [128, 2 * 2048], BF16),
                  "AS": sa("ASA", [128, 32], F32),
                  "BS": sa("BSA", [128, 32], F32),
                  "maskt": sa("masktA", [128, 8], F32),
                  "ccAll": sa("ccAllA", [128, 512], F32),
                  "hcomb": sa("hcombA", [128, 96], F32)}
            base = snap()
            with nc.Block() as blk:
                for eng in ENGS:
                    def mkA(eng):
                        def f(e):
                            restore(base)
                            sched_A(Em(eng, e), tA)
                        return f
                    getattr(blk, eng)(mkA(eng))

        with ExitStack() as stB_alloc:
            sbb = lambda *a: stB_alloc.enter_context(nc.sbuf_tensor(*a))
            tB = {"hT": sbb("hTB", [128, 4 * 2048], BF16),
                  "x1": sbb("x1B", [128, 8 * 2048], BF16),
                  "xbf": sbb("xbfB", [128, 2048], F32),
                  "SCR": sbb("SCRB", [128, 2048], F32),
                  "SCR2": sbb("SCR2B", [128, 2048], F32),
                  "o2a": sbb("o2aB", [128, 2 * 2048], F32),
                  "xn1b": sbb("xn1bB", [128, 2 * 2048], BF16),
                  "vb": sbb("vbB", [128, 4 * 2048], BF16),
                  "gsb": sbb("gsbB", [128, 4 * 2048], BF16),
                  "umb": sbb("umbB", [128, 4 * 2048], BF16),
                  "fF": sbb("fFB", [128, 16 * 512], BF16),
                  "wS": sbb("wSB", [128, 8 * 512], BF16),
                  "qb": sbb("qbB", [128, 4 * 1024], BF16),
                  "kb": sbb("kbB", [128, 4 * 1024], BF16),
                  "lg": sbb("lgB", [128, 256], BF16),
                  "dcf": sbb("dcfB", [128, 512], F32),
                  "gvl": sbb("gvlB", [128, 2048], BF16),
                  "hFt": sbb("hFtB", [128, 1024], BF16),
                  "st": sbb("stBB", [128, 204], F32)}
            baseB = snap()
            with nc.Block() as blk:
                for eng in ENGS:
                    def mkB(eng):
                        def f(e):
                            restore(baseB)
                            sched_B(Em(eng, e), tB)
                        return f
                    getattr(blk, eng)(mkB(eng))

    return nc



def kernel(L0_attn_x, L0_attn_W, L0_attn_b, L0_ffn_conv_w, L0_ffn_Wq, L0_ffn_Wk,
           L0_ffn_Wv, L0_ffn_Wg, L0_ffn_router_w, L0_ffn_Wout):
    x = np.ascontiguousarray(np.asarray(L0_attn_x, dtype=np.float32))
    W = np.asarray(L0_attn_W, dtype=np.float32)
    bb = np.asarray(L0_attn_b, dtype=np.float32)
    conv_w = np.asarray(L0_ffn_conv_w, dtype=np.float32)
    Wq = np.ascontiguousarray(np.asarray(L0_ffn_Wq, dtype=np.float32))
    Wk = np.ascontiguousarray(np.asarray(L0_ffn_Wk, dtype=np.float32))
    Wv = np.ascontiguousarray(np.asarray(L0_ffn_Wv, dtype=np.float32))
    Wg = np.ascontiguousarray(np.asarray(L0_ffn_Wg, dtype=np.float32))
    rw = np.asarray(L0_ffn_router_w, dtype=np.float32)
    Wo = np.ascontiguousarray(np.asarray(L0_ffn_Wout, dtype=np.float32))

    wt = np.ascontiguousarray(W.T)
    b2 = np.ascontiguousarray(bb.reshape(32, 128).T)
    rbdm = np.zeros((C, 32), np.float32)
    for h in range(H):
        rbdm[HDV * h:HDV * (h + 1), h] = rw[h, :, 0]
        rbdm[HDV * h:HDV * (h + 1), 16 + h] = rw[h, :, 1]
    cw0b = np.ascontiguousarray(np.broadcast_to(conv_w[:, 0], (128, C)))
    cw1b = np.ascontiguousarray(np.broadcast_to(conv_w[:, 1], (128, C)))
    inv_freq = (1.0 / (10000.0 ** (np.arange(0, HDK, 2, dtype=np.float32) / HDK))).astype(np.float32)
    cos_row = np.cos(inv_freq).astype(np.float32)
    sin_row = np.sin(inv_freq).astype(np.float32)
    cosqm = np.zeros((128, 1024), np.float32)
    sinsm = np.zeros((128, 1024), np.float32)
    for h in range(H):
        cosqm[:, 64 * h:64 * h + 32] = cos_row
        cosqm[:, 64 * h + 32:64 * h + 64] = cos_row
        sinsm[:, 64 * h:64 * h + 32] = -sin_row
        sinsm[:, 64 * h + 32:64 * h + 64] = sin_row
    ident = np.eye(128, dtype=np.float32)

    nc = build_nc()
    in_maps = []
    for c in range(NC):
        xc = np.ascontiguousarray(x[:, TL * c:TL * (c + 1), :])
        mk = np.zeros((128, 8), np.float32)
        mk[:, :c] = 1.0
        in_maps.append({
            "x": xc, "wt": wt, "b2": b2, "wq": Wq, "wk": Wk, "wv": Wv, "wg": Wg,
            "wo": Wo, "rbd": rbdm, "cw0": cw0b, "cw1": cw1b, "cosq": cosqm,
            "sins": sinsm, "idin": ident, "maskin": mk, "epsin": np.full((128, 1), EPS, np.float32),
            "wvt": np.ascontiguousarray(Wv.T),
        })
    res = run_bass_kernel_spmd(nc, in_maps, core_ids=list(range(NC)))
    out = np.concatenate([res.results[c]["y"] for c in range(NC)], axis=1)
    if DEBUG:
        kernel.debug = res.results
    return out
